# version 31
# baseline (speedup 1.0000x reference)
import bisect
import numpy as np
import ml_dtypes

_SHARED_SPACE = "Shared"
import concourse.bass as bass
import concourse.mybir as mybir
from concourse.bass_utils import run_bass_kernel_spmd

F32 = mybir.dt.float32
BF16 = mybir.dt.bfloat16
F16 = mybir.dt.float16
F8E4 = mybir.dt.float8e4
I32 = mybir.dt.int32
AF = mybir.ActivationFunctionType
ALU = mybir.AluOpType
AX = mybir.AxisListType

BSZ, SEQ, D = 4, 4096, 1024
TOK = 2048
NLT = 16
NDG = 8
CH = 256
NCH = 8
LN_EPS = 1e-5
RMS_EPS = 1e-6
HFF = 4096
IMLP = 2816
NHEAD = 4
DK = 128
DV = 256
BF = ml_dtypes.bfloat16

STAGES = ("A", "B", "C", "E")

XA = 0
XBUF = [8192, 12288, 16384]
YH = 20480
KH = 24576


def host_constants():
    c = {}
    t = np.arange(256)
    f = np.arange(257)
    ang = 2.0 * np.pi * np.outer(t, f) / 512.0
    Dre = np.cos(ang)
    Dim = -np.sin(ang)
    DfT = np.zeros((128, 8 * 128), np.float32)
    for tt in range(2):
        for comp in range(2):
            M = Dre if comp == 0 else Dim
            for fh in range(2):
                k = tt * 4 + comp * 2 + fh
                DfT[:, k * 128:(k + 1) * 128] = M[tt * 128:(tt + 1) * 128,
                                                  fh * 128:(fh + 1) * 128]
    c["DfT"] = DfT
    Dn = np.zeros((128, 2), np.float32)
    for tt in range(2):
        Dn[:, tt] = Dre[tt * 128:(tt + 1) * 128, 256]
    c["Dnyq"] = Dn
    Ere = np.zeros((257, 256))
    Eim = np.zeros((257, 256))
    for ff in range(257):
        e = np.zeros(257, complex)
        e[ff] = 1.0
        Ere[ff] = np.fft.irfft(e, 512)[256:512]
        e[ff] = 1j
        Eim[ff] = np.fft.irfft(e, 512)[256:512]
    Einv = np.zeros((128, 4 * 256), np.float32)
    for comp in range(2):
        M = Ere if comp == 0 else Eim
        for fh in range(2):
            k = comp * 2 + fh
            Einv[:, k * 256:(k + 1) * 256] = M[fh * 128:(fh + 1) * 128, :]
    c["Einv"] = Einv
    c["Enyq"] = np.ascontiguousarray(Ere[256:257, :]).astype(np.float32)
    c["sgnf"] = np.where(np.arange(128) % 2 == 0, 1.0, -1.0).astype(
        ml_dtypes.bfloat16).reshape(128, 1)
    tl = np.arange(256)
    UT = (tl[:, None] <= tl[None, :]).astype(np.float32)
    c["utri_a"] = np.ascontiguousarray(UT[0:128, :])
    c["utri_b"] = np.ascontiguousarray(UT[128:256, 128:256])
    s64 = np.arange(64)
    tril64 = (s64[:, None] <= s64[None, :]).astype(np.float32)
    mp = np.zeros((128, 128), np.float32)
    mp[0:64, 0:64] = tril64
    mp[64:128, 64:128] = tril64
    c["mask_pair"] = mp
    T2 = np.zeros((128, 128), np.float32)
    T2[0:64, 0:64] = tril64 / 16.0
    T2[64:128, 64:128] = tril64 / 16.0
    c["trilT16"] = T2
    return c


HOST_CONST = host_constants()


def host_prepare(core, full, nonce):
    b, s = core // 2, core % 2
    m = {k: v for k, v in HOST_CONST.items()}
    x = full["L0_attn_a_x"]
    m["xT"] = np.ascontiguousarray(x[b, s * TOK:(s + 1) * TOK, :].T)
    m["WinT"] = np.ascontiguousarray(full["L0_attn_a_W_in"].T)
    m["b_in_bc"] = np.ascontiguousarray(np.broadcast_to(full["L0_attn_a_b_in"], (128, D)))
    m["gamma_bc"] = np.ascontiguousarray(np.broadcast_to(full["L0_attn_a_gamma"], (128, D)))
    m["beta_bc"] = np.ascontiguousarray(np.broadcast_to(full["L0_attn_a_beta"], (128, D)))
    m["A_re"] = np.ascontiguousarray(full["L0_attn_a_A_re"].reshape(64, 1))
    m["A_im"] = np.ascontiguousarray(full["L0_attn_a_A_im"].reshape(64, 1))
    m["B_re"] = np.ascontiguousarray(full["L0_attn_a_B_re"].reshape(64, 1))
    m["B_im"] = np.ascontiguousarray(full["L0_attn_a_B_im"].reshape(64, 1))
    m["CT_re"] = np.ascontiguousarray(full["L0_attn_a_C_re"].T)
    m["CT_im"] = np.ascontiguousarray(full["L0_attn_a_C_im"].T)
    m["D_bc"] = np.ascontiguousarray(np.broadcast_to(full["L0_attn_a_D_re"], (128, D)))
    m["D_col"] = np.ascontiguousarray(full["L0_attn_a_D_re"].reshape(NDG, 128).T)
    m["w1T"] = np.ascontiguousarray(full["L0_ffn_w1"].T)
    m["b1c"] = np.ascontiguousarray(full["L0_ffn_b1"].reshape(HFF // 128, 128).T)
    m["w2T"] = np.ascontiguousarray(full["L0_ffn_w2"].T)
    m["b2c"] = np.ascontiguousarray(full["L0_ffn_b2"].reshape(NDG, 128).T)
    m["attn_nw"] = np.ascontiguousarray(full["L0_attn_b_attn_norm_w"].reshape(NDG, 128).T)
    m["mlp_nw"] = np.ascontiguousarray(full["L0_attn_b_mlp_norm_w"].reshape(NDG, 128).T)
    m["gnw"] = np.ascontiguousarray(full["L0_attn_b_gnorm_w"].reshape(2, 128).T)
    for ks, kd in [("Wq", "Wq"), ("Wk", "Wk"), ("Wv", "Wv"), ("Wg", "Wg"),
                   ("Wgk1", "Wgk1"), ("Wgk2", "Wgk2"), ("Wo", "Wo"),
                   ("W_gate", "Wgate"), ("W_up", "Wup"), ("W_down", "Wdown")]:
        m[kd] = full["L0_attn_b_" + ks]
    m["bgk2c"] = np.ascontiguousarray(full["L0_attn_b_bgk2"].reshape(4, 128).T)
    m["bgk2_bc"] = np.ascontiguousarray(np.broadcast_to(full["L0_attn_b_bgk2"], (128, 512)))
    m["nonce"] = np.full((1, 16), nonce, np.int32)
    m["pmaskc"] = np.full((128, 1), float(s), np.float32)
    return m


class _SerialEng:

    _OPS = ("tensor_scalar", "tensor_tensor", "memset", "tensor_copy",
            "tensor_reduce", "scalar_tensor_tensor", "tensor_single_scalar",
            "activation", "tensor_select", "iota", "matmul")

    def __init__(self, k, eng, en):
        self._k, self._e, self._en = k, eng, en

    def __getattr__(self, n):
        a = getattr(self._e, n)
        if n in _SerialEng._OPS:
            k, en = self._k, self._en

            def wrap(*args, **kw):
                k.w(self._e, en, (en, k.cnt[en]))
                return a(*args, **kw)
            return wrap
        return a


class _QProxy:

    def __init__(self, k, g, qn):
        self._k, self._g, self._qn = k, g, qn

    def __getattr__(self, n):
        return getattr(self._g, n)

    def dma_start(self, *a, **kw):
        k, qn = self._k, self._qn
        if k.dma_wm is not None:
            wm = k.dma_wm[qn]
            pre = k.cnt[qn]
            i = bisect.bisect_right(wm, pre) - 1
            if i >= 0 and wm[i] > k.enf[qn]:
                self._g.wait_ge(k.sem[qn], wm[i])
                k.enf[qn] = wm[i]
        return self._g.dma_start(*a, **kw)


class K:
    def __init__(self):
        self.nc = bass.Bass("TRN2", target_bir_lowering=False)
        nc = self.nc
        self.sem = {}
        self.cnt = {}
        self.waited = {}
        for n in ["TE", "DVE", "ACT", "POOL", "SP", "DSY", "DPO", "DAC"]:
            self.sem[n] = nc.alloc_semaphore("sem_" + n)
            self.cnt[n] = 0
            self.waited[n] = {}
        self.psem = nc.alloc_semaphore("psem")
        self.psem2 = nc.alloc_semaphore("psem2")
        self.rec_wm = {"DSY": set(), "DPO": set(), "DAC": set()}
        self.dma_wm = None
        self.enf = {"DSY": 0, "DPO": 0, "DAC": 0}
        self.T = {}
        self.scr_free = None
        self.pslast = {"A": None, "B": None, "C": None}
        self._io()
        self._sbuf()
        self._psum()

    def mk(self, en, instr):
        self.cnt[en] += 1
        instr.then_inc(self.sem[en], 1)
        return (en, self.cnt[en])

    def mkd(self, qn, instr):
        self.cnt[qn] += 16
        instr.then_inc(self.sem[qn], 16)
        return (qn, self.cnt[qn])

    def dma_gw(self, qn, g, dst2d, src2d, ng):
        return self.mkd(qn, g.dma_start(
            dst2d.rearrange("p (g w) -> p g w", g=ng),
            src2d.rearrange("(g p) w -> p g w", p=128)))

    def act_table(self, s, set_id):
        s.add_instruction(mybir.InstLoadActFuncSet(
            name=s.bass.get_next_instruction_name(),
            act_func_set_id=set_id, ins=[], outs=[]))

    def w(self, eng, consumer, *toks):
        best = {}
        for t in toks:
            if t is None:
                continue
            p, v = t
            if best.get(p, 0) < v:
                best[p] = v
        for p, v in sorted(best.items()):
            if self.waited[consumer].get(p, 0) >= v:
                continue
            eng.wait_ge(self.sem[p], v)
            self.waited[consumer][p] = v
            if p in self.rec_wm:
                self.rec_wm[p].add(v)

    def _io(self):
        nc = self.nc

        def din(name, shape, dtype=F32):
            return nc.dram_tensor(name, shape, dtype, kind="ExternalInput")
        self.t = t = {}
        t["xT"] = din("xT", [D, TOK])
        t["WinT"] = din("WinT", [D, D])
        t["b_in_bc"] = din("b_in_bc", [128, D])
        t["gamma_bc"] = din("gamma_bc", [128, D])
        t["beta_bc"] = din("beta_bc", [128, D])
        t["A_re"] = din("A_re", [64, 1])
        t["A_im"] = din("A_im", [64, 1])
        t["B_re"] = din("B_re", [64, 1])
        t["B_im"] = din("B_im", [64, 1])
        t["CT_re"] = din("CT_re", [64, D])
        t["CT_im"] = din("CT_im", [64, D])
        t["D_bc"] = din("D_bc", [128, D])
        t["D_col"] = din("D_col", [128, NDG])
        t["DfT"] = din("DfT", [128, 8 * 128])
        t["Dnyq"] = din("Dnyq", [128, 2])
        t["Einv"] = din("Einv", [128, 4 * 256])
        t["Enyq"] = din("Enyq", [1, 256])
        t["sgnf"] = din("sgnf", [128, 1], BF16)
        t["utri_a"] = din("utri_a", [128, 256])
        t["utri_b"] = din("utri_b", [128, 128])
        t["mask_pair"] = din("mask_pair", [128, 128])
        t["trilT16"] = din("trilT16", [128, 128])
        t["w1T"] = din("w1T", [D, HFF])
        t["b1c"] = din("b1c", [128, HFF // 128])
        t["w2T"] = din("w2T", [HFF, D])
        t["b2c"] = din("b2c", [128, NDG])
        t["attn_nw"] = din("attn_nw", [128, NDG])
        t["mlp_nw"] = din("mlp_nw", [128, NDG])
        t["gnw"] = din("gnw", [128, 2])
        t["Wq"] = din("Wq", [D, 512])
        t["Wk"] = din("Wk", [D, 512])
        t["Wv"] = din("Wv", [D, D])
        t["Wg"] = din("Wg", [D, D])
        t["Wgk1"] = din("Wgk1", [D, 16])
        t["Wgk2"] = din("Wgk2", [16, 512])
        t["bgk2c"] = din("bgk2c", [128, 4])
        t["bgk2_bc"] = din("bgk2_bc", [128, 512])
        t["Wo"] = din("Wo", [D, D])
        t["Wgate"] = din("Wgate", [D, IMLP])
        t["Wup"] = din("Wup", [D, IMLP])
        t["Wdown"] = din("Wdown", [IMLP, D])
        t["nonce"] = din("nonce", [1, 16], I32)
        t["pmaskc"] = din("pmaskc", [128, 1])
        self.t_out = nc.dram_tensor("y_out", [D, TOK], F32, kind="ExternalOutput")
        import os
        self.DBG = bool(int(os.environ.get("KDBG", "0")))
        if self.DBG:
            self.d_ar1 = nc.dram_tensor("d_ar1", [128, 16384], BF16, kind="ExternalOutput")
            self.d_ar2 = nc.dram_tensor("d_ar2", [128, 28672], BF16, kind="ExternalOutput")
            self.d_cols = nc.dram_tensor("d_cols", [128, 256], F32, kind="ExternalOutput")
            self.d_S = nc.dram_tensor("d_S", [128, 1024], F32, kind="ExternalOutput")
            self.d_M = nc.dram_tensor("d_M", [128, 2048], F32, kind="ExternalOutput")
            self.d_M2 = nc.dram_tensor("d_M2", [128, 2048], BF16, kind="ExternalOutput")
            self.d_nyq = nc.dram_tensor("d_nyq", [1, 2048], BF16, kind="ExternalOutput")
        self.sh_tail = nc.dram_tensor("sh_tail", [2, 128, 2048], BF16,
                                      kind="Internal", addr_space=_SHARED_SPACE)
        self.sh_T = nc.dram_tensor("sh_T", [2, 128, NDG], F32,
                                   kind="Internal", addr_space=_SHARED_SPACE)
        self.sh_S = nc.dram_tensor("sh_S", [2, 128, NHEAD * DV], F32,
                                   kind="Internal", addr_space=_SHARED_SPACE)
        self.sh_flagA = nc.dram_tensor("sh_flagA", [2, 16], I32,
                                       kind="Internal", addr_space=_SHARED_SPACE)
        self.sh_flagS = nc.dram_tensor("sh_flagS", [2, 16], I32,
                                       kind="Internal", addr_space=_SHARED_SPACE)
        import os as _os
        _dbg = bool(int(_os.environ.get("KDBG", "0")))
        _kind = "ExternalOutput" if _dbg else "Internal"
        self.d_oS = nc.dram_tensor("d_oS", [4, 128, 4096], BF16, kind=_kind)
        self.d_QE = nc.dram_tensor("d_QE", [4, 128, 2048], BF16, kind=_kind)
        self.d_nb = nc.dram_tensor("d_nb", [4, 128, 4096], BF16, kind=_kind)

    def _sbuf(self):
        A = self.nc.alloc_sbuf_tensor
        self.x_res = A("x_res", [128, NDG * TOK], F32)
        self.AR1 = A("AR1", [128, 16384], BF16)
        self.AR2 = A("AR2", [128, 28672], BF16)
        self.AR1f = self.AR1.bitcast(F32)
        self.AR2f = self.AR2.bitcast(F32)
        self.AR1q = self.AR1.bitcast(F8E4)
        self.c_pb = self.AR2[0:64, 8192:8704]
        self.AR2q = self.AR2.bitcast(F8E4)
        self.AR2h = self.AR2.bitcast(F16)
        self.onesc = A("onesc", [128, 1], BF16)
        self.onescf = A("onescf", [128, 1], F32)
        self.onesr = A("onesr", [1, 128], BF16)
        self.RS1 = A("RS1", [1, 1024], F32)
        self.bc_a = A("bc_a", [128, TOK], BF16)
        self.bc_b = A("bc_b", [128, TOK], BF16)
        self.row_bf = self.bc_b[0:1, :]
        self.sq_s = A("sq_s", [128, TOK], F32)
        self.xcs = A("xcs", [128, TOK], BF16)
        self.xad_s = self.xcs
        self.sc_a = self.sq_s[:, 1024:2048]
        self.pw_t = self.sq_s.bitcast(BF16)[:, 0:1024]
        self.cols = A("cols", [128, 64], F32)
        self.colsT = A("colsT", [128, 8], F32)
        self.colsC = A("colsC", [128, 64], F32)
        self.colsD = A("colsD", [128, 64], F32)
        self.colmu = A("colmu", [128, 1], F32)
        self.colv = A("colv", [128, 1], F32)
        self.tpeer = A("tpeer", [128, NDG], F32)
        self.pcol = A("pcol", [128, NDG], F32)
        self.c_bin = A("c_bin", [128, D], BF16)
        self.c_gam = A("c_gam", [128, D], BF16)
        self.c_bet = A("c_bet", [128, D], BF16)
        self.c_Dbc = A("c_Dbc", [128, D], BF16)
        self.c_Dcol = A("c_Dcol", [128, NDG], F32)
        self.c_DfT = A("c_DfT", [128, 8 * 128], BF16)
        self.c_Dnyq = A("c_Dnyq", [128, 2], BF16)
        self.c_Einv = A("c_Einv", [128, 4 * 256], BF16)
        self.c_Enyq = A("c_Enyq", [1, 256], BF16)
        self.c_sgnf = A("c_sgnf", [128, 1], BF16)
        self.c_utri_a = A("c_utri_a", [128, 256], BF16)
        self.c_utri_b = A("c_utri_b", [128, 128], BF16)
        self.c_mask = A("c_mask", [128, 128], BF16)
        self.c_tril16 = A("c_tril16", [128, 128], F32)
        self.c_Are = A("c_Are", [64, 1], F32)
        self.c_Aim = A("c_Aim", [64, 1], F32)
        self.c_Bre = A("c_Bre", [64, 1], F32)
        self.c_Bim = A("c_Bim", [64, 1], F32)
        self.c_CTre = self.AR1[0:64, 0:1024]
        self.c_CTim = self.AR1[0:64, 1024:2048]
        self.pows = self.sq_s[0:64, 0:1024]
        self.apow = self.cols[0:64, 56:64]
        self.c_b1c = A("c_b1c", [128, 32], F32)
        self.c_b2c = A("c_b2c", [128, NDG], F32)
        self.c_bq = A("c_bq", [128, NDG], F32)
        self.c_bq16 = A("c_bq16", [128, NDG], F16)
        self.c_anw = A("c_anw", [128, NDG], F32)
        self.c_mnw = A("c_mnw", [128, NDG], F32)
        self.c_gnw = A("c_gnw", [128, 2], F32)
        self.c_bgk2c = A("c_bgk2c", [128, 4], F32)
        self.c_bgk2bc = A("c_bgk2bc", [128, 512], BF16)
        self.c_pmask = A("c_pmask", [128, 1], F32)
        self.flag_src = A("flag_src", [1, 16], I32)
        self.flag_rd = A("flag_rd", [1, 16], I32)
        self.nyqrow = A("nyqrow", [1, 2048], BF16)
        self.y_nyq = self.nyqrow[0:1, 0:1024]
        self.k_nyq = self.nyqrow[0:1, 1024:2048]
        self.z64 = A("z64", [128, 64], F32)
        import os as _os
        self.dbgM = self.bc_a.bitcast(F32) if bool(int(_os.environ.get(
            "KDBG", "0"))) else None
        self.dbgM2 = self.bc_b.bitcast(F32) if self.dbgM is not None else None
        self.c_eps = A("c_eps", [128, 6], F32)
        self.S_f32 = A("S_f32", [128, NHEAD * DV], F32)
        self.S_bf = A("S_bf", [128, NHEAD * DV], BF16)
        self.Speer = A("Speer", [128, NHEAD * DV], BF16)
        self.pdh = A("pdh", [128, NHEAD], F32)
        self.pdsnap = A("pdsnap", [128, 2 * NHEAD], F32)
        self.eblh = A("eblh", [128, NHEAD * 8], F32)
        self.g1_bf = A("g1_bf", [16, 512], BF16)
        self.wgk1 = A("wgk1", [128, NDG * 16], BF16)
        self.wgk2 = A("wgk2", [16, 512], BF16)

    def _psum(self):
        nc = self.nc
        self.PS_A = nc.alloc_psum_tensor("PS_A", [128, 1024], F32)
        self.PS_B = nc.alloc_psum_tensor("PS_B", [128, 1024], F32)
        self.PS_C = nc.alloc_psum_tensor("PS_C", [128, 2048], F32)

    def build(self):
        with self.nc.Block("main") as blk:
            self.blk = blk
            self.e_load()
            if "A" in STAGES:
                self.e_attn_a()
            if "B" in STAGES:
                self.e_ffn("B")
            if "C" in STAGES:
                self.e_attn_b()
            if "E" in STAGES:
                self.e_ffn("E")
            self.e_store()
        return self.nc

    def V(self, f):
        self.blk.vector(f)

    def SC(self, f):
        self.blk.scalar(f)

    def TE(self, f):
        self.blk.tensor(f)

    def PO(self, f):
        self.blk.gpsimd(lambda g: f(_QProxy(self, g, "DPO")))

    def SY(self, f):
        self.blk.sync(lambda g: f(_QProxy(self, g, "DSY")))

    def e_load(self):
        T, t = self.T, self.t

        def f_tbl(s):
            self.act_table(s, 6)
        self.SC(f_tbl)

        def f_sy(sy):
            T["x_ld"] = self.dma_gw("DSY", sy, self.x_res[:, 0:NDG * TOK],
                                    t["xT"][:, :], NDG)
            for dst, src in [
                (self.c_sgnf, t["sgnf"]), (self.c_Dcol, t["D_col"]),
                (self.c_Are, t["A_re"]), (self.c_Aim, t["A_im"]),
                (self.c_Bre, t["B_re"]), (self.c_Bim, t["B_im"]),
                (self.c_tril16, t["trilT16"]),
                (self.flag_src, t["nonce"]), (self.c_pmask, t["pmaskc"]),
                (self.c_b1c, t["b1c"]), (self.c_b2c, t["b2c"]),
                (self.c_anw, t["attn_nw"]), (self.c_mnw, t["mlp_nw"]),
                (self.c_gnw, t["gnw"]), (self.c_bgk2c, t["bgk2c"]),
            ]:
                tk = self.mkd("DSY", sy.dma_start(dst[:], src[:]))
            T["consts_f32"] = tk
        self.SY(f_sy)

        def f_po(g):
            tk = None
            for dst, src in [
                (self.c_bin, t["b_in_bc"]),
                (self.c_DfT, t["DfT"]), (self.c_Dnyq, t["Dnyq"]),
                (self.c_Einv, t["Einv"]), (self.c_Enyq, t["Enyq"]),
                (self.c_utri_a, t["utri_a"]), (self.c_utri_b, t["utri_b"]),
                (self.c_mask, t["mask_pair"]),
                (self.c_gam, t["gamma_bc"]), (self.c_bet, t["beta_bc"]),
                (self.c_Dbc, t["D_bc"]), (self.c_CTre, t["CT_re"]),
                (self.c_CTim, t["CT_im"]), (self.c_bgk2bc, t["bgk2_bc"]),
            ]:
                tk = self.mkd("DPO", g.dma_start(dst[:], src[:]))
            T["consts_bf"] = tk
            T["WinT_ld"] = self.dma_gw("DPO", g, self.AR2[:, 0:NDG * 1024],
                                       t["WinT"][:, :], NDG)
        self.PO(f_po)

        def f_v(v):
            self.mk("DVE", v.memset(self.onesc[:], 1.0))
            self.mk("DVE", v.memset(self.onescf[:], 1.0))
            self.mk("DVE", v.memset(self.onesr[:], 1.0))
            self.mk("DVE", v.memset(self.c_eps[:, 0:1], LN_EPS))
            self.mk("DVE", v.memset(self.c_eps[:, 1:2], RMS_EPS))
            self.mk("DVE", v.memset(self.c_eps[:, 2:3], 1.0))
            self.mk("DVE", v.memset(self.c_eps[:, 3:4], float(LN_EPS * RMS_EPS)))
            T["ones"] = self.mk("DVE", v.memset(self.z64[:], 0.0))
        self.V(f_v)
        self.nc.const_aps.register(F32, LN_EPS, self.c_eps[:, 0:1])
        self.nc.const_aps.register(F32, RMS_EPS, self.c_eps[:, 1:2])
        self.nc.const_aps.register(F32, 1.0, self.c_eps[:, 2:3])
        self.nc.const_aps.register(F32, float(LN_EPS * RMS_EPS),
                                   self.c_eps[:, 3:4])

    def ln_feat(self, tag, off, n, dst_sl, src_dep, rms=False, wcol=None,
                eps=LN_EPS, src_sl=None):
        T = self.T
        nt = n // 512
        assert 1 <= nt <= 2

        def srcap(g):
            if src_sl is not None:
                return src_sl[g]
            return self.x_res[:, g * TOK + off:g * TOK + off + n]

        for g in range(NDG):
            sqo = (g % 2) * n
            def f_s(s, g=g, sqo=sqo):
                self.w(s, "ACT", src_dep, T.get(tag + f"_mmsq{g - 2}"))
                if g == 0:
                    self.w(s, "ACT", self.scr_free)
                T[tag + f"_sq{g}"] = self.mk("ACT", s.activation(
                    self.bc_a[:, sqo:sqo + n], srcap(g), AF.Square))
            self.SC(f_s)

            def f_t(te, g=g, sqo=sqo):
                self.w(te, "TE", T["ones"], T[tag + f"_sq{g}"])
                if g == 0:
                    self.w(te, "TE", self.pslast["A"], self.pslast["C"])
                if not rms:
                    mm = None
                    for tt in range(nt):
                        dst = (self.PS_A[0:1, tt * 512:tt * 512 + 512] if tt < 2
                               else self.PS_B[0:1, (tt - 2) * 512:
                                              (tt - 2) * 512 + 512])
                        mm = te.matmul(dst, self.onescf[:],
                                       srcap(g)[:, tt * 512:tt * 512 + 512],
                                       start=(g == 0), stop=(g == NDG - 1))
                    T[tag + f"_mmx{g}"] = self.mk("TE", mm)
                mm = None
                for tt in range(nt):
                    mm = te.matmul(self.PS_C[0:1, tt * 512:tt * 512 + 512],
                                   self.onesc[:],
                                   self.bc_a[:, sqo + tt * 512:
                                             sqo + tt * 512 + 512],
                                   start=(g == 0), stop=(g == NDG - 1))
                T[tag + f"_mmsq{g}"] = self.mk("TE", mm)
            self.TE(f_t)
        T[tag + "_mmsq"] = T[tag + f"_mmsq{NDG - 1}"]
        if not rms:
            T[tag + "_mmx"] = T[tag + f"_mmx{NDG - 1}"]

        sq_row = self.sq_s[0:1, 0:n]

        if not rms:
            def f_r2(s):
                self.w(s, "ACT", T[tag + "_mmsq"], T[tag + "_mmx"])
                t1 = self.mk("ACT", s.mul(self.RS1[0:1, 0:n],
                                          self.PS_A[0:1, 0:n], 1.0 / D))
                self.w(s, "ACT", t1)
                T[tag + "_mubf"] = self.mk("ACT", s.copy(
                    self.bc_b[0:1, 0:n], self.RS1[0:1, 0:n]))
                T[tag + "_musq"] = self.mk("ACT", s.activation(
                    sq_row, self.RS1[0:1, 0:n], AF.Square))
            self.SC(f_r2)

            def f_t2m(te):
                self.w(te, "TE", T[tag + "_mubf"])
                mm = None
                for tt in range(nt):
                    mm = te.matmul(self.PS_A[:, tt * 512:tt * 512 + 512],
                                   self.onesr[:],
                                   self.bc_b[0:1, tt * 512:tt * 512 + 512],
                                   start=True, stop=True)
                T[tag + "_bcM"] = self.mk("TE", mm)
            self.TE(f_t2m)

            def f_s3m(s):
                self.w(s, "ACT", T[tag + "_bcM"])
                tk = self.mk("ACT", s.copy(self.bc_b[:, 0:n],
                                           self.PS_A[:, 0:n]))
                T[tag + "_bcMe"] = tk
                self.pslast["A"] = tk
                self.pslast["B"] = tk
            self.SC(f_s3m)

            def f_vv(v):
                self.w(v, "DVE", T[tag + "_musq"])
                T[tag + "_var"] = self.mk("DVE", v.scalar_tensor_tensor(
                    self.RS1[0:1, 0:n], self.PS_C[0:1, 0:n], 1.0 / D,
                    sq_row, ALU.mult, ALU.subtract))
            self.V(f_vv)
        else:
            def f_vv(v):
                self.w(v, "DVE", T[tag + "_mmsq"])
                T[tag + "_var"] = self.mk("DVE", v.tensor_scalar(
                    self.RS1[0:1, 0:n], self.PS_C[0:1, 0:n], 1.0 / D,
                    None, ALU.mult))
            self.V(f_vv)

        def f_s2(s):
            self.w(s, "ACT", T[tag + "_var"])
            t1 = self.mk("ACT", s.activation(sq_row, self.RS1[0:1, 0:n],
                                             AF.Ln, bias=float(eps)))
            self.w(s, "ACT", t1)
            T[tag + "_rstd"] = self.mk("ACT", s.activation(
                self.bc_a[0:1, 0:n], sq_row, AF.Exp, scale=-0.5))
        self.SC(f_s2)

        def f_t2(te):
            self.w(te, "TE", T[tag + "_rstd"], self.pslast["C"])
            mm = None
            for tt in range(nt):
                mm = te.matmul(self.PS_C[:, tt * 512:tt * 512 + 512],
                               self.onesr[:],
                               self.bc_a[0:1, tt * 512:tt * 512 + 512],
                               start=True, stop=True)
            T[tag + "_bcR"] = self.mk("TE", mm)
        self.TE(f_t2)

        def f_s3(s):
            self.w(s, "ACT", T[tag + "_bcR"])
            tk = self.mk("ACT", s.copy(self.bc_a[:, 0:n], self.PS_C[:, 0:n]))
            T[tag + "_bc"] = tk
            self.pslast["C"] = tk
        self.SC(f_s3)

        def f_v4(v):
            self.w(v, "DVE", T[tag + "_bc"])
            tk = None
            for g in range(NDG):
                dst = dst_sl[g]
                three_d = len(dst.shape) == 3

                def r3(ap):
                    return ap.rearrange("p (a b) -> p a b", b=dst.shape[2])                         if three_d else ap
                if rms:
                    tk = self.mk("DVE", v.scalar_tensor_tensor(
                        dst, r3(srcap(g)),
                        wcol[:, g:g + 1] if wcol is not None else 1.0,
                        r3(self.bc_a[:, 0:n]), ALU.mult, ALU.mult))
                else:
                    self.w(v, "DVE", tk)
                    t1 = self.mk("DVE", v.tensor_tensor(
                        self.sq_s[:, 0:n], srcap(g), self.bc_b[:, 0:n],
                        ALU.subtract))
                    self.w(v, "DVE", t1)
                    tk = self.mk("DVE", v.tensor_tensor(
                        dst, r3(self.sq_s[:, 0:n]), r3(self.bc_a[:, 0:n]),
                        ALU.mult))
            T[tag + "_done"] = tk
            self.scr_free = tk
        self.V(f_v4)

    def ln_fused(self, tag, off, n, dst_sl, src_dep, wcol, hid_sl=None,
                 eps_ln=LN_EPS, eps_rms=RMS_EPS, src_sl=None):
        T = self.T
        assert n == 512

        def srcap(g):
            if src_sl is not None:
                return src_sl[g]
            return self.x_res[:, g * TOK + off:g * TOK + off + n]

        for g in range(NDG):
            sqo = (g % 2) * n
            def f_s(s, g=g, sqo=sqo):
                self.w(s, "ACT", src_dep, T.get(tag + f"_mmsq{g - 2}"))
                if g == 0:
                    self.w(s, "ACT", self.scr_free)
                T[tag + f"_sq{g}"] = self.mk("ACT", s.activation(
                    self.bc_a[:, sqo:sqo + n], srcap(g), AF.Square))
            self.SC(f_s)

            def f_t(te, g=g, sqo=sqo):
                self.w(te, "TE", T["ones"], T[tag + f"_sq{g}"])
                if g == 0:
                    self.w(te, "TE", self.pslast["A"], self.pslast["C"])
                mm = te.matmul(self.PS_A[0:1, 0:n], self.onescf[:],
                               srcap(g),
                               start=(g == 0), stop=(g == NDG - 1))
                T[tag + f"_mmx{g}"] = self.mk("TE", mm)
                mm = te.matmul(self.PS_C[0:1, 0:n], self.onesc[:],
                               self.bc_a[:, sqo:sqo + n],
                               start=(g == 0), stop=(g == NDG - 1))
                T[tag + f"_mmsq{g}"] = self.mk("TE", mm)
            self.TE(f_t)

        sq_row = self.sq_s[0:1, 0:n]

        def f_r2(s):
            self.w(s, "ACT", T[tag + f"_mmsq{NDG - 1}"],
                   T[tag + f"_mmx{NDG - 1}"])
            t1 = self.mk("ACT", s.mul(self.RS1[0:1, 0:n],
                                      self.PS_A[0:1, 0:n], 1.0 / D))
            self.w(s, "ACT", t1)
            T[tag + "_mubf"] = self.mk("ACT", s.copy(
                self.bc_b[0:1, 0:n], self.RS1[0:1, 0:n]))
            T[tag + "_musq"] = self.mk("ACT", s.activation(
                sq_row, self.RS1[0:1, 0:n], AF.Square))
        self.SC(f_r2)

        def f_t2m(te):
            self.w(te, "TE", T[tag + "_mubf"])
            mm = te.matmul(self.PS_A[:, 0:n], self.onesr[:],
                           self.bc_b[0:1, 0:n], start=True, stop=True)
            T[tag + "_bcM"] = self.mk("TE", mm)
        self.TE(f_t2m)

        def f_s3m(s):
            self.w(s, "ACT", T[tag + "_bcM"])
            tk = self.mk("ACT", s.copy(self.bc_b[:, 0:n], self.PS_A[:, 0:n]))
            T[tag + "_bcMe"] = tk
            self.pslast["A"] = tk
        self.SC(f_s3m)

        def f_vv(v):
            self.w(v, "DVE", T[tag + "_musq"])
            T[tag + "_var"] = self.mk("DVE", v.scalar_tensor_tensor(
                self.RS1[0:1, 0:n], self.PS_C[0:1, 0:n], 1.0 / D,
                sq_row, ALU.mult, ALU.subtract))
        self.V(f_vv)

        def f_s2(s):
            self.w(s, "ACT", T[tag + "_var"])
            t1 = self.mk("ACT", s.activation(
                sq_row, self.RS1[0:1, 0:n], AF.Ln,
                bias=float(eps_ln * eps_rms), scale=float(1.0 + eps_rms)))
            if hid_sl is not None:
                t1 = self.mk("ACT", s.activation(
                    self.sq_s[0:1, n:2 * n], self.RS1[0:1, 0:n], AF.Ln,
                    bias=float(eps_ln)))
            self.w(s, "ACT", t1)
            tk = self.mk("ACT", s.activation(
                self.bc_a[0:1, 0:n], sq_row, AF.Exp, scale=-0.5))
            if hid_sl is not None:
                tk = self.mk("ACT", s.activation(
                    self.bc_a[0:1, n:2 * n], self.sq_s[0:1, n:2 * n],
                    AF.Exp, scale=-0.5))
            T[tag + "_rstd"] = tk
        self.SC(f_s2)

        nr = n if hid_sl is None else 2 * n

        def f_t2(te):
            self.w(te, "TE", T[tag + "_rstd"], self.pslast["C"])
            mm = te.matmul(self.PS_C[:, 0:n], self.onesr[:],
                           self.bc_a[0:1, 0:n], start=True, stop=True)
            if hid_sl is not None:
                mm = te.matmul(self.PS_C[:, n:2 * n], self.onesr[:],
                               self.bc_a[0:1, n:2 * n], start=True, stop=True)
            T[tag + "_bcR"] = self.mk("TE", mm)
        self.TE(f_t2)

        def f_s3(s):
            self.w(s, "ACT", T[tag + "_bcR"])
            tk = self.mk("ACT", s.copy(self.bc_a[:, 0:nr],
                                       self.PS_C[:, 0:nr]))
            T[tag + "_bc"] = tk
            self.pslast["C"] = tk
        self.SC(f_s3)

        def f_v4(v):
            self.w(v, "DVE", T[tag + "_bc"])
            tk = None
            for g in range(NDG):
                self.w(v, "DVE", tk)
                t1 = self.mk("DVE", v.tensor_tensor(
                    self.sq_s[:, 0:n], srcap(g), self.bc_b[:, 0:n],
                    ALU.subtract))
                self.w(v, "DVE", t1)
                tk = self.mk("DVE", v.scalar_tensor_tensor(
                    dst_sl[g], self.sq_s[:, 0:n],
                    wcol[:, g:g + 1] if wcol is not None else 1.0,
                    self.bc_a[:, 0:n], ALU.mult, ALU.mult))
                T[tag + f"_ap{g}"] = tk
                if hid_sl is not None:
                    tk = self.mk("DVE", v.tensor_tensor(
                        hid_sl[g], self.sq_s[:, 0:n],
                        self.bc_a[:, n:2 * n], ALU.mult))
            T[tag + "_done"] = tk
            self.scr_free = tk
        self.V(f_v4)

    def fwd_dft(self, tag, src_ar, src_off, dst_ar, dst_off, nyq_dst, dep_toks):
        T = self.T
        for fh in range(2):
            def f_t(te, fh=fh):
                self.w(te, "TE", T["consts_bf"], self.pslast["A"],
                       self.pslast["B"], *dep_toks)
                for comp in range(2):
                    ps = self.PS_A if comp == 0 else self.PS_B
                    mm = None
                    for mh in range(2):
                        for tt in range(2):
                            mm = te.matmul(
                                ps[:, mh * 512:mh * 512 + 512],
                                self.c_DfT[:, (tt * 4 + comp * 2 + fh) * 128:
                                           (tt * 4 + comp * 2 + fh + 1) * 128],
                                src_ar[:, src_off + tt * 1024 + mh * 512:
                                       src_off + tt * 1024 + mh * 512 + 512],
                                start=(tt == 0), stop=(tt == 1))
                    nm = "re" if comp == 0 else "im"
                    T[tag + f"_mm_{nm}{fh}"] = self.mk("TE", mm)
            self.TE(f_t)

            def f_s(s, fh=fh):
                tk = None
                for comp, nm in [(0, "re"), (1, "im")]:
                    ps = self.PS_A if comp == 0 else self.PS_B
                    self.w(s, "ACT", T[tag + f"_mm_{nm}{fh}"])
                    tk = self.mk("ACT", s.copy(
                        dst_ar[:, dst_off + (comp * 2 + fh) * 1024:
                               dst_off + (comp * 2 + fh + 1) * 1024],
                        ps[:, 0:1024]))
                T[tag + f"_ev{fh}"] = tk
                self.pslast["A"] = tk
                self.pslast["B"] = tk
            self.SC(f_s)

        if nyq_dst is not None:
            def f_tn(te):
                self.w(te, "TE", T[tag + "_ev1"])
                mm = None
                for mh in range(2):
                    for tt in range(2):
                        mm = te.matmul(self.PS_A[0:1, mh * 512:mh * 512 + 512],
                                       self.c_Dnyq[:, tt:tt + 1],
                                       src_ar[:, src_off + tt * 1024 + mh * 512:
                                              src_off + tt * 1024 + mh * 512 + 512],
                                       start=(tt == 0), stop=(tt == 1))
                T[tag + "_mm_nyq"] = self.mk("TE", mm)
            self.TE(f_tn)

            def f_sn(s):
                self.w(s, "ACT", T[tag + "_mm_nyq"])
                tk = self.mk("ACT", s.copy(nyq_dst[0:1, 0:1024],
                                           self.PS_A[0:1, 0:1024]))
                T[tag + "_done"] = tk
                self.pslast["A"] = tk
            self.SC(f_sn)
        else:
            T[tag + "_done"] = T[tag + "_ev1"]

    def e_attn_a(self):
        T = self.T

        def v_pows(v):
            v = _SerialEng(self, v, "DVE")
            self.w(v, "DVE", T["consts_f32"])
            P = self.pows
            t0 = self.mk("DVE", v.memset(P[:, 0:1024], 0.0))
            self.w(v, "DVE", t0)
            self.mk("DVE", v.memset(P[:, 0:1], 1.0))
            self.mk("DVE", v.tensor_copy(self.apow[:, 0:1], self.c_Are[:]))
            tk = self.mk("DVE", v.tensor_copy(self.apow[:, 1:2], self.c_Aim[:]))
            self.w(v, "DVE", tk)
            for k in range(8):
                L = 1 << k
                t1 = self.mk("DVE", v.tensor_scalar(
                    P[:, 512:512 + L], P[:, 0:L], self.apow[:, 0:1], None, ALU.mult))
                t2 = self.mk("DVE", v.tensor_scalar(
                    P[:, 512 + L:512 + 2 * L], P[:, 256:256 + L],
                    self.apow[:, 1:2], None, ALU.mult))
                self.w(v, "DVE", t2)
                self.mk("DVE", v.tensor_tensor(
                    P[:, L:2 * L], P[:, 512:512 + L],
                    P[:, 512 + L:512 + 2 * L], ALU.subtract))
                t1 = self.mk("DVE", v.tensor_scalar(
                    P[:, 512:512 + L], P[:, 0:L], self.apow[:, 1:2], None, ALU.mult))
                t2 = self.mk("DVE", v.tensor_scalar(
                    P[:, 512 + L:512 + 2 * L], P[:, 256:256 + L],
                    self.apow[:, 0:1], None, ALU.mult))
                self.w(v, "DVE", t2)
                tk = self.mk("DVE", v.tensor_tensor(
                    P[:, 256 + L:256 + 2 * L], P[:, 512:512 + L],
                    P[:, 512 + L:512 + 2 * L], ALU.add))
                if k < 7:
                    self.mk("DVE", v.tensor_tensor(
                        self.apow[:, 2:3], self.apow[:, 0:1], self.apow[:, 0:1],
                        ALU.mult))
                    self.mk("DVE", v.tensor_tensor(
                        self.apow[:, 3:4], self.apow[:, 1:2], self.apow[:, 1:2],
                        ALU.mult))
                    t3 = self.mk("DVE", v.tensor_tensor(
                        self.apow[:, 4:5], self.apow[:, 0:1], self.apow[:, 1:2],
                        ALU.mult))
                    self.w(v, "DVE", t3)
                    self.mk("DVE", v.tensor_tensor(
                        self.apow[:, 0:1], self.apow[:, 2:3], self.apow[:, 3:4],
                        ALU.subtract))
                    tk = self.mk("DVE", v.tensor_scalar(
                        self.apow[:, 1:2], self.apow[:, 4:5], 2.0, None, ALU.mult))
                    self.w(v, "DVE", tk)
            self.w(v, "DVE", tk)
            t1 = self.mk("DVE", v.tensor_scalar(
                P[:, 512:768], P[:, 0:256], self.c_Bre[:], None, ALU.mult))
            t2 = self.mk("DVE", v.tensor_scalar(
                P[:, 768:1024], P[:, 256:512], self.c_Bim[:], None, ALU.mult))
            self.w(v, "DVE", t2)
            t3 = self.mk("DVE", v.tensor_tensor(
                P[:, 512:768], P[:, 512:768], P[:, 768:1024], ALU.subtract))
            self.w(v, "DVE", t3)
            self.mk("DVE", v.tensor_copy(self.c_pb[:, 0:256], P[:, 512:768]))
            t1 = self.mk("DVE", v.tensor_scalar(
                P[:, 512:768], P[:, 0:256], self.c_Bim[:], None, ALU.mult))
            t2 = self.mk("DVE", v.tensor_scalar(
                P[:, 768:1024], P[:, 256:512], self.c_Bre[:], None, ALU.mult))
            self.w(v, "DVE", t2)
            t3 = self.mk("DVE", v.tensor_tensor(
                P[:, 512:768], P[:, 512:768], P[:, 768:1024], ALU.add))
            self.w(v, "DVE", t3)
            T["pb"] = self.mk("DVE", v.tensor_scalar(
                self.c_pb[:, 256:512], P[:, 512:768], -1.0, None, ALU.mult))
        self.V(v_pows)

        def t_kr(te):
            self.w(te, "TE", T["pb"], T["consts_bf"],
                   self.pslast["A"], self.pslast["B"])
            mm = None
            for lag in range(2):
                ps = self.PS_A if lag == 0 else self.PS_B
                for mh in range(2):
                    te.matmul(ps[:, mh * 512:mh * 512 + 512],
                              self.c_pb[:, lag * 128:lag * 128 + 128],
                              self.c_CTre[:, mh * 512:mh * 512 + 512],
                              start=True, stop=False)
                    mm = te.matmul(ps[:, mh * 512:mh * 512 + 512],
                                   self.c_pb[:, 256 + lag * 128:256 + lag * 128 + 128],
                                   self.c_CTim[:, mh * 512:mh * 512 + 512],
                                   start=False, stop=True)
            T["kr_mm"] = self.mk("TE", mm)
        self.TE(t_kr)

        def v_kr(v):
            self.w(v, "DVE", T["kr_mm"])
            t1 = self.mk("DVE", v.tensor_single_scalar(
                self.PS_A[0:1, 0:1024], self.PS_A[0:1, 0:1024], 1.0, ALU.add))
            self.w(v, "DVE", t1)
            self.mk("DVE", v.tensor_copy(self.AR2[:, YH:YH + 1024],
                                         self.PS_A[:, 0:1024]))
            tk = self.mk("DVE", v.tensor_copy(
                self.AR2[:, YH + 1024:YH + 2048], self.PS_B[:, 0:1024]))
            T["kr_bf"] = tk
            self.pslast["A"] = tk
            self.pslast["B"] = tk
        self.V(v_kr)

        self.fwd_dft("kh", self.AR2, YH, self.AR2, KH, self.k_nyq, [T["kr_bf"]])
        self.yh_free = T["kh_mm_nyq"]

        xn_view = self.AR1[:, 0:16384].rearrange("p (lt e) -> p lt e", e=1024)
        xn_sl = [xn_view[:, :, g * 128:(g + 1) * 128] for g in range(NDG)]
        def v_ctbar(v):
            self.w(v, "DVE", T["kr_mm"])
        self.V(v_ctbar)
        for hh in range(2):
            dst_h = [xn_view[:, hh * 8:(hh + 1) * 8, g * 128:(g + 1) * 128]
                     for g in range(NDG)]
            self.ln_feat(f"lnA{hh}", hh * 1024, 1024, dst_h, T["x_ld"])
        T["lnA_done"] = T["lnA1_done"]

        for lt in range(NLT):
            def t_win(te, lt=lt):
                self.w(te, "TE", T["WinT_ld"],
                       T["lnA0_done"] if lt < 8 else T["lnA1_done"],
                       self.pslast["A"])
                mm = None
                for eh in range(2):
                    for dt in range(NDG):
                        mm = te.matmul(
                            self.PS_A[:, eh * 512:eh * 512 + 512],
                            self.AR1[:, lt * 1024 + dt * 128:
                                     lt * 1024 + dt * 128 + 128],
                            self.AR2[:, dt * 1024 + eh * 512:
                                     dt * 1024 + eh * 512 + 512],
                            start=(dt == 0), stop=(dt == NDG - 1))
                T[f"win{lt}"] = self.mk("TE", mm)
            self.TE(t_win)

            def v_iln(v, lt=lt):
                self.w(v, "DVE", T[f"win{lt}"], T.get(f"xa{lt - 1}"))
                tk = self.mk("DVE", v.tensor_tensor(
                    self.sc_a[:, 0:1024], self.PS_A[:, 0:1024],
                    self.c_bin[:, 0:1024], ALU.add))
                self.pslast["A"] = tk
                self.w(v, "DVE", tk)
                t1 = self.mk("DVE", v.tensor_reduce(
                    self.colmu[:, 0:1], self.sc_a[:, 0:1024], AX.X, ALU.add))
                self.w(v, "DVE", t1)
                t1 = self.mk("DVE", v.tensor_scalar(
                    self.colmu[:, 0:1], self.colmu[:, 0:1], 1.0 / D, None,
                    ALU.mult))
                self.w(v, "DVE", t1)
                T[f"ymu{lt}"] = self.mk("DVE", v.tensor_scalar(
                    self.sq_s[:, 0:1024], self.sc_a[:, 0:1024],
                    self.colmu[:, 0:1], None, ALU.subtract))
            self.V(v_iln)

            def s_iln(s, lt=lt):
                self.w(s, "ACT", T[f"ymu{lt}"])
                T[f"ysq{lt}"] = self.mk("ACT", s.activation(
                    self.sc_a[:, 0:1024], self.sq_s[:, 0:1024], AF.Square))
            self.SC(s_iln)

            def v_iln2(v, lt=lt):
                self.w(v, "DVE", T[f"ysq{lt}"])
                T[f"yv{lt}"] = self.mk("DVE", v.tensor_reduce(
                    self.colv[:, 0:1], self.sc_a[:, 0:1024], AX.X, ALU.add))
            self.V(v_iln2)

            def s_iln2(s, lt=lt):
                self.w(s, "ACT", T[f"yv{lt}"])
                t1 = self.mk("ACT", s.activation(self.colv[:, 0:1],
                                                 self.colv[:, 0:1],
                                                 AF.Ln, bias=LN_EPS,
                                                 scale=1.0 / D))
                self.w(s, "ACT", t1)
                T[f"yr{lt}"] = self.mk("ACT", s.activation(
                    self.colv[:, 0:1], self.colv[:, 0:1], AF.Exp, scale=-0.5))
            self.SC(s_iln2)

            def v_iln3(v, lt=lt):
                self.w(v, "DVE", T[f"yr{lt}"])
                t1 = self.mk("DVE", v.scalar_tensor_tensor(
                    self.sc_a[:, 0:1024], self.sq_s[:, 0:1024],
                    self.colv[:, 0:1], self.c_gam[:, 0:1024],
                    ALU.mult, ALU.mult))
                self.w(v, "DVE", t1)
                T[f"xa{lt}"] = self.mk("DVE", v.tensor_tensor(
                    self.AR1[:, XA + lt * 1024:XA + (lt + 1) * 1024],
                    self.sc_a[:, 0:1024], self.c_bet[:, 0:1024], ALU.add))
            self.V(v_iln3)
        T["xa_all"] = T[f"xa{NLT - 1}"]

        for c in range(NCH):
            def v_xadc(v, c=c):
                self.w(v, "DVE", T[f"xa{c * 2 + 1}"], T.get(f"cs_mm{c - 1}"))
                t1 = self.mk("DVE", v.tensor_tensor(
                    self.xad_s[:, 0:1024],
                    self.AR1[:, (c * 2) * 1024:(c * 2 + 1) * 1024],
                    self.c_Dbc[:, 0:1024], ALU.mult))
                T[f"xadc{c}"] = self.mk("DVE", v.tensor_tensor(
                    self.xad_s[:, 1024:2048],
                    self.AR1[:, (c * 2 + 1) * 1024:(c * 2 + 2) * 1024],
                    self.c_Dbc[:, 0:1024], ALU.mult))
            self.V(v_xadc)

            def t_csc(te, c=c):
                self.w(te, "TE", T[f"xadc{c}"], T["ones"])
                if c == 0:
                    self.w(te, "TE", self.pslast["B"])
                mm = None
                for mt in range(NDG):
                    for tt in range(2):
                        mm = te.matmul(
                            self.PS_B[:, c * 8 + mt:c * 8 + mt + 1],
                            self.xad_s[:, tt * 1024 + mt * 128:
                                       tt * 1024 + mt * 128 + 128],
                            self.onesc[:], start=(tt == 0), stop=(tt == 1))
                T[f"cs_mm{c}"] = self.mk("TE", mm)
            self.TE(t_csc)
        T["cs_mm"] = T[f"cs_mm{NCH - 1}"]

        def v_cs(v):
            self.w(v, "DVE", T["cs_mm"])
            tk = self.mk("DVE", v.tensor_copy(
                self.cols[:, 0:64].rearrange("p (m c) -> p m c", c=8),
                self.PS_B[:, 0:64].rearrange("p (c m) -> p m c", m=8)))
            self.pslast["B"] = tk
            self.w(v, "DVE", tk)
            self.mk("DVE", v.tensor_reduce(
                self.colsT[:, 0:8],
                self.cols[:, 0:64].rearrange("p (m c) -> p m c", c=8),
                AX.X, ALU.add))
            for mt in range(NDG):
                self.mk("DVE", v.tensor_tensor_scan(
                    self.colsC[:, mt * 8:mt * 8 + 8],
                    self.cols[:, mt * 8:mt * 8 + 8], self.z64[:, 0:8],
                    0.0, ALU.add, ALU.add))
            self.w(v, "DVE", ("DVE", self.cnt["DVE"]))
            t1 = self.mk("DVE", v.tensor_tensor(
                self.colsC[:, 0:64], self.colsC[:, 0:64], self.cols[:, 0:64],
                ALU.subtract))
            self.w(v, "DVE", t1)
            tk2 = self.mk("DVE", v.tensor_copy(self.colsD[:, 0:64],
                                               self.colsC[:, 0:64]))
            T["dc_local"] = tk2
            T["T_col"] = tk2
        self.V(v_cs)

        def po_sendA(g):
            pid = g.partition_id()
            slot = pid % 2
            self.w(g, "POOL", T["xa_all"], T["T_col"])
            t1 = self.mkd("DPO", g.dma_start(
                self.sh_tail[bass.ts(slot, 1), :, :].rearrange("a p c -> p (a c)"),
                self.AR1[:, XA + 14 * 1024:XA + 16 * 1024]))
            t2 = self.mkd("DPO", g.dma_start(
                self.sh_T[bass.ts(slot, 1), :, :].rearrange("a p c -> p (a c)"),
                self.colsT[:, 0:8]))
            self.w(g, "POOL", t1, t2)
            self.mkd("DPO", g.dma_start(
                self.sh_flagA[bass.ts(slot, 1), :], self.flag_src[:]))
        self.PO(po_sendA)

        bufof = {0: 2}
        for c in range(1, NCH):
            bufof[c] = (c - 1) % 2

        self.fwd_dft("fd0", self.AR1, XA + 0, self.AR2, XBUF[2], None,
                     [T["xa0"], T["xa1"]])
        for c in range(1, NCH):
            deps = [T[f"xa{c * 2}"], T[f"xa{c * 2 + 1}"]]
            if c >= 3:
                deps.append(T[f"pw{c - 1}_done"])
            self.fwd_dft(f"fd{c}", self.AR1, XA + c * 2048, self.AR2,
                         XBUF[bufof[c]], None, deps)
            self.pw_inv_evac(c, XBUF[bufof[c]], XBUF[bufof[c - 1]],
                             f"fd{c}", f"fd{c - 1}",
                             (self.AR1, XA + c * 2048),
                             (self.AR1, XA + (c - 1) * 2048))

        def po_recvA(g):
            pid = g.partition_id()
            peer = (pid + 1) % 2
            self.w(g, "POOL", T[f"win{NLT - 1}"], T["consts_f32"])
            with g.register("cnt_a") as r_cnt:
                g.reg_mov(r_cnt, 0)

                def poll():
                    g.dma_start(self.flag_rd[:],
                                self.sh_flagA[bass.ts(peer, 1), :]).then_inc(
                        self.psem, 16)
                    g.reg_add(r_cnt, r_cnt, 16)
                    g.wait_ge(self.psem, r_cnt)

                poll()
                with g.While(lambda: g.value_load(self.flag_rd[0:1, 0:1])
                             != g.value_load(self.flag_src[0:1, 0:1])):
                    poll()
            t1 = self.mkd("DPO", g.dma_start(
                self.AR2[:, 0:2048],
                self.sh_tail[bass.ts(peer, 1), :, :].rearrange("a p c -> p (a c)")))
            t2 = self.mkd("DPO", g.dma_start(
                self.tpeer[:, 0:8],
                self.sh_T[bass.ts(peer, 1), :, :].rearrange("a p c -> p (a c)")))
            self.w(g, "POOL", t1, t2)
            T["peerA"] = self.mk("POOL", g.engine_nop())
        self.PO(po_recvA)

        def v_peer(v):
            self.w(v, "DVE", T["peerA"])
            T["ptail"] = self.mk("DVE", v.tensor_scalar(
                self.AR2[:, 0:2048], self.AR2[:, 0:2048],
                self.c_pmask[:, 0:1], None, ALU.mult))
            T["pT"] = self.mk("DVE", v.tensor_scalar(
                self.tpeer[:, 0:8], self.tpeer[:, 0:8],
                self.c_pmask[:, 0:1], None, ALU.mult))
        self.V(v_peer)

        self.fwd_dft("fdp", self.AR2, 0, self.AR2, XBUF[0], None,
                     [T["ptail"], T[f"pw{NCH - 1}_done"]])
        self.pw_inv_evac(0, XBUF[2], XBUF[0], "fd0", "fdp",
                         (self.AR1, XA + 0), (self.AR2, 0))

        def v_pdc(v):
            self.w(v, "DVE", T["pT"], T["evac0_done"])
            t1 = self.mk("DVE", v.tensor_copy(self.pcol[:, 0:8],
                                              self.tpeer[:, 0:8]))
            self.w(v, "DVE", t1)
            tk = None
            for mt in range(NDG):
                tk = self.mk("DVE", v.tensor_scalar(
                    self.x_res[:, mt * TOK:(mt + 1) * TOK],
                    self.x_res[:, mt * TOK:(mt + 1) * TOK],
                    self.pcol[:, mt:mt + 1], None, ALU.add))
            T["attnA_done"] = tk
        self.V(v_pdc)

    def pw_inv_evac(self, c, cur_off, prev_off, cur_tag, prev_tag,
                    src_cur, src_prev):
        T = self.T

        def f_tn(te, c=c):
            self.w(te, "TE", T[cur_tag + "_done"], T[prev_tag + "_done"],
                   self.pslast["A"])
            mm = None
            for mh in range(2):
                parts = [(ar, off, tt) for (ar, off) in (src_prev, src_cur)
                         for tt in range(2)]
                for i, (ar, off, tt) in enumerate(parts):
                    mm = te.matmul(
                        self.PS_A[0:1, mh * 512:mh * 512 + 512],
                        self.c_Dnyq[:, tt:tt + 1],
                        ar[:, off + tt * 1024 + mh * 512:
                           off + tt * 1024 + mh * 512 + 512],
                        start=(i == 0), stop=(i == 3))
            T[f"nyqmm{c}"] = self.mk("TE", mm)
        self.TE(f_tn)

        def f_sn(s, c=c):
            self.w(s, "ACT", T[f"nyqmm{c}"])
            tk = self.mk("ACT", s.copy(self.y_nyq, self.PS_A[0:1, 0:1024]))
            T[f"nyqev{c}"] = tk
            self.pslast["A"] = tk
        self.SC(f_sn)

        def f_pw(v, c=c):
            self.w(v, "DVE", T[cur_tag + "_done"], T[prev_tag + "_done"],
                   self.yh_free)
            self.mk("DVE", v.tensor_tensor(
                self.xad_s[:, 0:1024],
                self.AR1[:, (c * 2) * 1024:(c * 2 + 1) * 1024],
                self.c_Dbc[:, 0:1024], ALU.mult))
            self.mk("DVE", v.tensor_tensor(
                self.xad_s[:, 1024:2048],
                self.AR1[:, (c * 2 + 1) * 1024:(c * 2 + 2) * 1024],
                self.c_Dbc[:, 0:1024], ALU.mult))
            sttk = []
            for fh in range(2):
                sttk.append(self.mk("DVE", v.scalar_tensor_tensor(
                    self.bc_a[:, fh * 1024:(fh + 1) * 1024],
                    self.AR2[:, cur_off + (0 * 2 + fh) * 1024:
                             cur_off + (0 * 2 + fh + 1) * 1024],
                    self.c_sgnf[:, 0:1],
                    self.AR2[:, prev_off + (0 * 2 + fh) * 1024:
                             prev_off + (0 * 2 + fh + 1) * 1024],
                    ALU.mult, ALU.add)))
                sttk.append(self.mk("DVE", v.scalar_tensor_tensor(
                    self.bc_b[:, fh * 1024:(fh + 1) * 1024],
                    self.AR2[:, cur_off + (1 * 2 + fh) * 1024:
                             cur_off + (1 * 2 + fh + 1) * 1024],
                    self.c_sgnf[:, 0:1],
                    self.AR2[:, prev_off + (1 * 2 + fh) * 1024:
                             prev_off + (1 * 2 + fh + 1) * 1024],
                    ALU.mult, ALU.add)))
            self.w(v, "DVE", sttk[-1])
            tadd = None
            for fh in range(2):
                xr = self.bc_a[:, fh * 1024:(fh + 1) * 1024]
                xi = self.bc_b[:, fh * 1024:(fh + 1) * 1024]
                kr = self.AR2[:, KH + (0 * 2 + fh) * 1024:KH + (0 * 2 + fh + 1) * 1024]
                ki = self.AR2[:, KH + (1 * 2 + fh) * 1024:KH + (1 * 2 + fh + 1) * 1024]
                yr = self.AR2[:, YH + (0 * 2 + fh) * 1024:YH + (0 * 2 + fh + 1) * 1024]
                yi = self.AR2[:, YH + (1 * 2 + fh) * 1024:YH + (1 * 2 + fh + 1) * 1024]
                t1 = self.mk("DVE", v.tensor_tensor(yr, xr, kr, ALU.mult))
                self.w(v, "DVE", tadd)
                t2 = self.mk("DVE", v.tensor_tensor(self.pw_t[:, 0:1024], xi, ki,
                                                    ALU.mult))
                self.w(v, "DVE", t2)
                ts = self.mk("DVE", v.tensor_tensor(yr, yr, self.pw_t[:, 0:1024],
                                                    ALU.subtract))
                t1 = self.mk("DVE", v.tensor_tensor(yi, xr, ki, ALU.mult))
                self.w(v, "DVE", ts)
                t2 = self.mk("DVE", v.tensor_tensor(self.pw_t[:, 0:1024], xi, kr,
                                                    ALU.mult))
                self.w(v, "DVE", t2)
                tadd = self.mk("DVE", v.tensor_tensor(yi, yi, self.pw_t[:, 0:1024],
                                                      ALU.add))
            self.w(v, "DVE", T[f"nyqev{c}"])
            T[f"pw{c}_done"] = self.mk("DVE", v.tensor_tensor(
                self.y_nyq, self.y_nyq, self.k_nyq, ALU.mult))
        self.V(f_pw)

        def f_inv(te, c=c):
            self.w(te, "TE", T[f"pw{c}_done"], self.pslast["C"])
            mm = None
            for mt in range(NDG):
                first = True
                for comp in range(2):
                    for fh in range(2):
                        mm = te.matmul(
                            self.PS_C[:, mt * 256:mt * 256 + 256],
                            self.AR2[:, YH + (comp * 2 + fh) * 1024 + mt * 128:
                                     YH + (comp * 2 + fh) * 1024 + mt * 128 + 128],
                            self.c_Einv[:, (comp * 2 + fh) * 256:
                                        (comp * 2 + fh + 1) * 256],
                            start=first, stop=False)
                        first = False
                mm = te.matmul(self.PS_C[:, mt * 256:mt * 256 + 256],
                               self.nyqrow[0:1, mt * 128:mt * 128 + 128],
                               self.c_Enyq[0:1, 0:256], start=False, stop=False)
                mm = te.matmul(self.PS_C[:, mt * 256:mt * 256 + 256],
                               self.xad_s[:, mt * 128:mt * 128 + 128],
                               self.c_utri_a[:, 0:256], start=False, stop=False)
                mm = te.matmul(self.PS_C[:, mt * 256 + 128:mt * 256 + 256],
                               self.xad_s[:, 1024 + mt * 128:1024 + mt * 128 + 128],
                               self.c_utri_b[:, 0:128], start=False, stop=True)
            T[f"inv{c}_mm"] = self.mk("TE", mm)
            self.yh_free = T[f"inv{c}_mm"]
        self.TE(f_inv)

        def f_ev(v, c=c):
            self.w(v, "DVE", T[f"inv{c}_mm"], T["dc_local"])
            tk = None
            for mt in range(NDG):
                tk = self.mk("DVE", v.scalar_tensor_tensor(
                    self.x_res[:, mt * TOK + c * 256:mt * TOK + c * 256 + 256],
                    self.PS_C[:, mt * 256:mt * 256 + 256],
                    self.colsD[:, mt * 8 + c:mt * 8 + c + 1],
                    self.x_res[:, mt * TOK + c * 256:mt * TOK + c * 256 + 256],
                    ALU.add, ALU.add))
            T[f"evac{c}_done"] = tk
            self.pslast["C"] = tk
        self.V(f_ev)

    def e_ffn(self, stage):
        T = self.T
        t = self.t
        tag = "ffn" + stage
        if stage == "E":
            def f_tbl(s):
                self.act_table(s, 6)
            self.SC(f_tbl)
        if stage == "B":
            pred = [T["attnA_done"], T["inv0_mm"], T["fdp_done"],
                    T["pw0_done"], T["evac0_done"]]
        else:
            pred = [T["attnB_done"]]

        def load_quarter(g, q):
            jq = q % 4
            buf = (q % 2) * 8192
            T[tag + f"_w1q{q}"] = self.dma_gw(
                "DPO", g, self.AR1[:, buf:buf + 8192],
                t["w1T"][:, jq * 1024:(jq + 1) * 1024], NDG)

        def f_po0(g):
            self.w(g, "POOL", *pred)
            load_quarter(g, 0)
            load_quarter(g, 1)
            T[tag + "_w2"] = self.dma_gw("DPO", g, self.AR2q[:, 0:32768],
                                         t["w2T"][:, :], 32)
        self.PO(f_po0)

        if stage == "B":
            def f_bq_s(s):
                self.w(s, "ACT", T["consts_f32"])
                T["bq16"] = self.mk("ACT", s.copy(self.c_bq16[:], self.c_b2c[:]))
            self.SC(f_bq_s)

            def f_bq_v(v):
                self.w(v, "DVE", T["bq16"])
                T["bqf"] = self.mk("DVE", v.tensor_copy(self.c_bq[:], self.c_bq16[:]))
            self.V(f_bq_v)

        tmp16 = self.sq_s.bitcast(F16)[:, 0:512]
        last = None
        for q in range(16):
            lb, jq = q // 4, q % 4
            if jq == 0:
                xnb_sl = [self.AR2[:, 24576 + dt * 512:24576 + (dt + 1) * 512]
                          for dt in range(NDG)]
                dep = pred[0] if lb == 0 else T[tag + f"_x{lb - 1}"]
                self.ln_feat(tag + f"ln{lb}", lb * 512, 512, xnb_sl, dep)
            buf = (q % 2) * 8192
            for jj in range(8):
                jt = jq * 8 + jj
                def f_t1(te, jt=jt, jj=jj, lb=lb, q=q, buf=buf):
                    self.w(te, "TE", T[tag + f"_w1q{q}"], T[tag + f"ln{lb}_done"],
                           T.get(tag + f"_h1f{lb}_{jt - 2}"))
                    if jt == 0:
                        self.w(te, "TE", self.pslast["A"])
                    mm = None
                    for dt in range(NDG):
                        mm = te.matmul(
                            self.PS_A[:, (jt % 2) * 512:(jt % 2) * 512 + 512],
                            self.AR1[:, buf + dt * 1024 + jj * 128:
                                     buf + dt * 1024 + jj * 128 + 128],
                            self.AR2[:, 24576 + dt * 512:24576 + (dt + 1) * 512],
                            start=(dt == 0), stop=(dt == NDG - 1))
                    T[tag + f"_h1{lb}_{jt}"] = self.mk("TE", mm)
                self.TE(f_t1)

                if jt % 2 == 0:
                    def f_e1(s, jt=jt, lb=lb):
                        self.w(s, "ACT", T[tag + f"_h1{lb}_{jt}"])
                        T[tag + f"_h1f{lb}_{jt}"] = self.mk("ACT", s.activation(
                            self.AR2q[:, 32768 + jt * 512:32768 + (jt + 1) * 512],
                            self.PS_A[:, (jt % 2) * 512:(jt % 2) * 512 + 512],
                            AF.Relu, bias=self.c_b1c[:, jt:jt + 1]))
                    self.SC(f_e1)
                else:
                    def f_e1(v, jt=jt, lb=lb):
                        self.w(v, "DVE", T[tag + f"_h1{lb}_{jt}"])
                        T[tag + f"_h1f{lb}_{jt}"] = self.mk("DVE", v.tensor_scalar(
                            self.AR2q[:, 32768 + jt * 512:32768 + (jt + 1) * 512],
                            self.PS_A[:, (jt % 2) * 512:(jt % 2) * 512 + 512],
                            self.c_b1c[:, jt:jt + 1], 0.0, ALU.add, ALU.max))
                    self.V(f_e1)
            if q + 2 < 16:
                def f_poq(g, q=q, lb=lb, jq=jq):
                    self.w(g, "POOL", T[tag + f"_h1{lb}_{jq * 8 + 7}"])
                    load_quarter(g, q + 2)
                self.PO(f_poq)

            if jq == 3:
                hq_done = [T[tag + f"_h1f{lb}_{jt}"] for jt in range(30, 32)]
                for dt in range(NDG):
                    def f_t2(te, dt=dt, lb=lb):
                        self.w(te, "TE", T[tag + "_w2"], *hq_done)
                        self.w(te, "TE", T.get(tag + f"_o16_{lb}_{dt - 2}"))
                        if dt == 0:
                            self.w(te, "TE", self.pslast["B"])
                        mm = None
                        for j in range(16):
                            mm = te.matmul(
                                self.PS_B[:, (dt % 2) * 512:(dt % 2) * 512 + 512],
                                self.AR2q[:, j * 2048:(j + 1) * 2048].rearrange(
                                    "q (r c) -> q r c", r=2)[:, :, dt * 128:
                                                             (dt + 1) * 128],
                                self.AR2q[:, 32768 + j * 1024:
                                          32768 + (j + 1) * 1024].rearrange(
                                    "q (r c) -> q r c", r=2),
                                start=(j == 0), stop=(j == 15),
                                perf_mode=mybir.MatmulPerfMode.DoubleRow)
                        T[tag + f"_o{lb}_{dt}"] = self.mk("TE", mm)
                    self.TE(f_t2)

                    def f_e2(v, dt=dt, lb=lb):
                        self.w(v, "DVE", T[tag + f"_o{lb}_{dt}"], T.get("bqf"),
                               T.get(tag + f"_xr{lb}_{dt - 1}") if dt else
                               T.get(tag + f"_xr{lb - 1}_{NDG - 1}"))
                        T[tag + f"_o16_{lb}_{dt}"] = self.mk("DVE", v.tensor_scalar(
                            tmp16,
                            self.PS_B[:, (dt % 2) * 512:(dt % 2) * 512 + 512],
                            self.c_bq[:, dt:dt + 1], None, ALU.add))
                    self.V(f_e2)

                    def f_e3(v, dt=dt, lb=lb):
                        self.w(v, "DVE", T[tag + f"_o16_{lb}_{dt}"])
                        T[tag + f"_xr{lb}_{dt}"] = self.mk("DVE", v.tensor_tensor(
                            self.x_res[:, dt * TOK + lb * 512:
                                       dt * TOK + lb * 512 + 512],
                            self.x_res[:, dt * TOK + lb * 512:
                                       dt * TOK + lb * 512 + 512],
                            tmp16, ALU.add))
                    self.V(f_e3)
                T[tag + f"_x{lb}"] = T[tag + f"_xr{lb}_{NDG - 1}"]
                last = T[tag + f"_x{lb}"]
        T[tag + "_done"] = last

    def e_attn_b(self):
        T = self.T
        t = self.t
        A2, A2f, A1 = self.AR2, self.AR2f, self.AR1
        HB, QT, KT, KTM, VTM = 0, 4096, 6144, 8192, 10240
        GLF, B32, OB, QEB, EM = 14336, 16384, 20480, 24576, 26624
        pred = T["ffnB_done"]

        def f_w1(g):
            self.w(g, "POOL", pred)
            self.dma_gw("DPO", g, A1[:, 0:4096], t["Wq"][:, :], NDG)
            self.dma_gw("DPO", g, A1[:, 4096:8192], t["Wk"][:, :], NDG)
            self.dma_gw("DPO", g, A1[:, 8192:16384], t["Wv"][:, :], NDG)
            self.dma_gw("DPO", g, self.wgk1[:, 0:128], t["Wgk1"][:, :], NDG)
            tk = self.mkd("DPO", g.dma_start(self.wgk2[:], t["Wgk2"][:]))
            T["wC1"] = tk
        self.PO(f_w1)

        def f_init(v):
            self.w(v, "DVE", T["consts_f32"])
            self.mk("DVE", v.memset(self.S_f32[:], 0.0))
            self.mk("DVE", v.memset(self.S_bf[:], 0.0))
            T["Sinit"] = self.mk("DVE", v.memset(self.pdh[:], 1.0))
        self.V(f_init)

        for lb in range(4):
            base = lb * 512
            xb_sl = [A2[:, HB + dt * 512:HB + (dt + 1) * 512] for dt in range(NDG)]
            dep = pred if lb == 0 else T[f"c1x{lb - 1}"]
            self.ln_fused(f"rmC{lb}", base, 512, xb_sl, dep, self.c_anw)
            htok = T[f"rmC{lb}_done"]

            def proj_fm(name, wt_off, n_ot, dst_off, ps, wstride, psname):
                for ot in range(n_ot):
                    def f_t(te, ot=ot, wt_off=wt_off, ps=ps, name=name,
                            wstride=wstride, psname=psname):
                        self.w(te, "TE", T["wC1"],
                               T.get(f"{name}{lb}_e{ot - 2}"))
                        if ot < 2:
                            self.w(te, "TE", self.pslast[psname])
                        mm = None
                        for dt in range(NDG):
                            self.w(te, "TE", T[f"rmC{lb}_ap{dt}"])
                            mm = te.matmul(
                                ps[:, (ot % 2) * 512:(ot % 2) * 512 + 512],
                                A1[:, wt_off + dt * wstride + ot * 128:
                                   wt_off + dt * wstride + ot * 128 + 128],
                                A2[:, HB + dt * 512:HB + (dt + 1) * 512],
                                start=(dt == 0), stop=(dt == NDG - 1))
                        T[f"{name}{lb}_m{ot}"] = self.mk("TE", mm)
                    self.TE(f_t)

                    def f_e(v, ot=ot, dst_off=dst_off, ps=ps, name=name):
                        self.w(v, "DVE", T[f"{name}{lb}_m{ot}"])
                        T[f"{name}{lb}_e{ot}"] = self.mk("DVE", v.tensor_copy(
                            A2[:, dst_off + ot * 512:dst_off + (ot + 1) * 512],
                            ps[:, (ot % 2) * 512:(ot % 2) * 512 + 512]))
                    self.V(f_e)
            proj_fm("q", 0, 4, QT, self.PS_A, 512, "A")
            self.pslast["A"] = T[f"q{lb}_e3"]
            proj_fm("k", 4096, 4, KT, self.PS_A, 512, "A")
            self.pslast["A"] = T[f"k{lb}_e3"]

            for lt in range(4):
                def f_t(te, lt=lt):
                    self.w(te, "TE", T["wC1"], htok, T.get(f"kt{lb}_e{lt - 2}"))
                    if lt < 2:
                        self.w(te, "TE", self.pslast["A"])
                    mm = None
                    for dt in range(NDG):
                        mm = te.matmul(
                            self.PS_A[:, (lt % 2) * 512:(lt % 2) * 512 + 512],
                            A2[:, HB + dt * 512 + lt * 128:
                               HB + dt * 512 + lt * 128 + 128],
                            A1[:, 4096 + dt * 512:4096 + (dt + 1) * 512],
                            start=(dt == 0), stop=(dt == NDG - 1))
                    T[f"kt{lb}_m{lt}"] = self.mk("TE", mm)
                self.TE(f_t)

                def f_e(v, lt=lt):
                    self.w(v, "DVE", T[f"kt{lb}_m{lt}"])
                    T[f"kt{lb}_e{lt}"] = self.mk("DVE", v.tensor_copy(
                        A2[:, KTM + lt * 512:KTM + (lt + 1) * 512],
                        self.PS_A[:, (lt % 2) * 512:(lt % 2) * 512 + 512]))
                self.V(f_e)
            self.pslast["A"] = T[f"kt{lb}_e3"]
            for lt in range(4):
                for eh in range(2):
                    i = lt * 2 + eh

                    def f_t(te, lt=lt, eh=eh, i=i):
                        self.w(te, "TE", T["wC1"], htok,
                               T.get(f"vt{lb}_e{i - 2}"))
                        if i < 2:
                            self.w(te, "TE", self.pslast["B"])
                        mm = None
                        for dt in range(NDG):
                            mm = te.matmul(
                                self.PS_B[:, (i % 2) * 512:(i % 2) * 512 + 512],
                                A2[:, HB + dt * 512 + lt * 128:
                                   HB + dt * 512 + lt * 128 + 128],
                                A1[:, 8192 + dt * 1024 + eh * 512:
                                   8192 + dt * 1024 + (eh + 1) * 512],
                                start=(dt == 0), stop=(dt == NDG - 1))
                        T[f"vt{lb}_m{i}"] = self.mk("TE", mm)
                    self.TE(f_t)

                    def f_e(v, lt=lt, eh=eh, i=i):
                        self.w(v, "DVE", T[f"vt{lb}_m{i}"])
                        T[f"vt{lb}_e{i}"] = self.mk("DVE", v.tensor_copy(
                            A2[:, VTM + lt * 1024 + eh * 512:
                               VTM + lt * 1024 + (eh + 1) * 512],
                            self.PS_B[:, (i % 2) * 512:(i % 2) * 512 + 512]))
                    self.V(f_e)

            self.pslast["B"] = T[f"vt{lb}_e7"]
            def f_g1(te, lb=lb):
                self.w(te, "TE", T["wC1"], htok, self.pslast["B"])
                mm = None
                for dt in range(NDG):
                    mm = te.matmul(self.PS_B[0:16, 0:512],
                                   self.wgk1[:, dt * 16:(dt + 1) * 16],
                                   A2[:, HB + dt * 512:HB + (dt + 1) * 512],
                                   start=(dt == 0), stop=(dt == NDG - 1))
                T[f"g1m{lb}"] = self.mk("TE", mm)
            self.TE(f_g1)

            def f_g1e(v, lb=lb):
                self.w(v, "DVE", T[f"g1m{lb}"])
                T[f"g1e{lb}"] = self.mk("DVE", v.tensor_copy(
                    self.g1_bf[:], self.PS_B[0:16, 0:512]))
                self.pslast["B"] = T[f"g1e{lb}"]
            self.V(f_g1e)

            for ot in range(4):
                def f_t(te, ot=ot, lb=lb):
                    self.w(te, "TE", T[f"g1e{lb}"], T.get(f"zfd{lb}_{ot - 2}"))
                    if ot < 2:
                        self.w(te, "TE", self.pslast["A"])
                    mm = te.matmul(self.PS_A[:, (ot % 2) * 512:(ot % 2) * 512 + 512],
                                   self.wgk2[:, ot * 128:(ot + 1) * 128],
                                   self.g1_bf[:], start=True, stop=True)
                    T[f"zfm{lb}_{ot}"] = self.mk("TE", mm)
                self.TE(f_t)

                def f_ls(v, ot=ot, lb=lb):
                    self.w(v, "DVE", T[f"zfm{lb}_{ot}"],
                           T.get(f"glf{lb}_{ot - 1}"))
                    s = self.sq_s
                    t1 = self.mk("DVE", v.tensor_scalar(
                        s[:, 0:512],
                        self.PS_A[:, (ot % 2) * 512:(ot % 2) * 512 + 512],
                        self.c_bgk2c[:, ot:ot + 1], None, ALU.add))
                    T[f"zfd{lb}_{ot}"] = t1
                    self.w(v, "DVE", t1)
                    T[f"zfmin{lb}_{ot}"] = self.mk("DVE", v.tensor_scalar(
                        s[:, 1536:2048], s[:, 0:512], 0.0, None, ALU.min))
                self.V(f_ls)

                def f_ls2(sc, ot=ot, lb=lb):
                    self.w(sc, "ACT", T[f"zfd{lb}_{ot}"])
                    t0 = self.mk("ACT", sc.activation(
                        self.sq_s[:, 512:1024], self.sq_s[:, 0:512], AF.Abs))
                    self.w(sc, "ACT", t0)
                    t1 = self.mk("ACT", sc.activation(
                        self.sq_s[:, 1024:1536], self.sq_s[:, 512:1024],
                        AF.Exp, scale=-1.0))
                    self.w(sc, "ACT", t1)
                    T[f"zfe{lb}_{ot}"] = self.mk("ACT", sc.activation(
                        self.sq_s[:, 512:1024], self.sq_s[:, 1024:1536],
                        AF.Ln, bias=1.0))
                self.SC(f_ls2)

                def f_ls3(v, ot=ot, lb=lb):
                    self.w(v, "DVE", T[f"zfe{lb}_{ot}"],
                           T[f"zfmin{lb}_{ot}"])
                    T[f"glf{lb}_{ot}"] = self.mk("DVE", v.tensor_tensor(
                        A2[:, GLF + ot * 512:GLF + (ot + 1) * 512],
                        self.sq_s[:, 1536:2048], self.sq_s[:, 512:1024],
                        ALU.subtract))
                self.V(f_ls3)

            for lt in range(4):
                def f_t(te, lt=lt, lb=lb):
                    self.w(te, "TE", T[f"g1e{lb}"], T.get(f"ztd{lb}_{lt - 2}"))
                    if lt < 2:
                        self.w(te, "TE", self.pslast["B"])
                    mm = te.matmul(self.PS_B[:, (lt % 2) * 512:(lt % 2) * 512 + 512],
                                   self.g1_bf[0:16, lt * 128:(lt + 1) * 128],
                                   self.wgk2[:], start=True, stop=True)
                    T[f"ztm{lb}_{lt}"] = self.mk("TE", mm)
                self.TE(f_t)

                def f_ls(v, lt=lt, lb=lb):
                    self.w(v, "DVE", T[f"ztm{lb}_{lt}"],
                           T.get(f"glt{lb}_{lt - 1}"))
                    s = self.sq_s
                    gt = A2f[:, 8192 + lt * 512:8192 + (lt + 1) * 512]
                    t1 = self.mk("DVE", v.tensor_tensor(
                        gt, self.PS_B[:, (lt % 2) * 512:(lt % 2) * 512 + 512],
                        self.c_bgk2bc[:, 0:512], ALU.add))
                    T[f"ztd{lb}_{lt}"] = t1
                    self.w(v, "DVE", t1)
                    T[f"ztmin{lb}_{lt}"] = self.mk("DVE", v.tensor_scalar(
                        s[:, 1024:1536], gt, 0.0, None, ALU.min))
                self.V(f_ls)

                def f_ls2(sc, lt=lt, lb=lb):
                    self.w(sc, "ACT", T[f"ztd{lb}_{lt}"])
                    gt = A2f[:, 8192 + lt * 512:8192 + (lt + 1) * 512]
                    t0 = self.mk("ACT", sc.activation(
                        self.sq_s[:, 0:512], gt, AF.Abs))
                    self.w(sc, "ACT", t0)
                    t1 = self.mk("ACT", sc.activation(
                        self.sq_s[:, 512:1024], self.sq_s[:, 0:512],
                        AF.Exp, scale=-1.0))
                    self.w(sc, "ACT", t1)
                    T[f"zte{lb}_{lt}"] = self.mk("ACT", sc.activation(
                        self.sq_s[:, 0:512], self.sq_s[:, 512:1024],
                        AF.Ln, bias=1.0))
                self.SC(f_ls2)

                def f_ls3(v, lt=lt, lb=lb):
                    self.w(v, "DVE", T[f"zte{lb}_{lt}"],
                           T[f"ztmin{lb}_{lt}"])
                    gt = A2f[:, 8192 + lt * 512:8192 + (lt + 1) * 512]
                    T[f"glt{lb}_{lt}"] = self.mk("DVE", v.tensor_tensor(
                        gt, self.sq_s[:, 1024:1536], self.sq_s[:, 0:512],
                        ALU.subtract))
                self.V(f_ls3)

            self.pslast["A"] = T[f"zfd{lb}_3"]
            self.pslast["B"] = T[f"ztd{lb}_3"]
            def f_bl(v, lb=lb):
                self.w(v, "DVE", *[T[f"glf{lb}_{h}"] for h in range(4)])
                tk = None
                for h in range(4):
                    tk = self.mk("DVE", v.tensor_reduce(
                        self.cols[:, h * 8:h * 8 + 8],
                        A2[:, GLF + h * 512:GLF + (h + 1) * 512].rearrange(
                            "p (c w) -> p c w", w=64), AX.X, ALU.add))
                T[f"blr{lb}"] = tk
            self.V(f_bl)

            def f_ble(sc, lb=lb):
                self.w(sc, "ACT", T[f"blr{lb}"])
                T[f"ebl{lb}"] = self.mk("ACT", sc.activation(
                    self.eblh[:, 0:32], self.cols[:, 0:32], AF.Exp,
                    scale=1.0 / 16.0))
            self.SC(f_ble)

            def f_btm(te, lb=lb):
                self.w(te, "TE", T["consts_f32"], self.pslast["A"],
                       self.pslast["B"],
                       *[T[f"glt{lb}_{lt}"] for lt in range(4)])
                for lt in range(4):
                    ps = self.PS_A if lt < 2 else self.PS_B
                    mm = te.matmul(ps[:, (lt % 2) * 512:(lt % 2) * 512 + 512],
                                   self.c_tril16[:],
                                   A2f[:, 8192 + lt * 512:8192 + (lt + 1) * 512],
                                   start=True, stop=True)
                    T[f"btm{lb}_{lt}"] = self.mk("TE", mm)
            self.TE(f_btm)

            def f_emt(sc, lb=lb):
                self.w(sc, "ACT", T[f"btm{lb}_3"])
                for lt in range(4):
                    ps = self.PS_A if lt < 2 else self.PS_B
                    T[f"emt{lb}_{lt}"] = self.mk("ACT", sc.activation(
                        A2[:, B32 + 2048 + lt * 512:B32 + 2048 + (lt + 1) * 512],
                        ps[:, (lt % 2) * 512:(lt % 2) * 512 + 512],
                        AF.Exp, scale=-1.0))
            self.SC(f_emt)

            self.pslast["A"] = T[f"emt{lb}_1"]
            self.pslast["B"] = T[f"emt{lb}_3"]
            def f_scan(v, lb=lb):
                self.w(v, "DVE", *[T[f"glf{lb}_{h}"] for h in range(4)])
                self.w(v, "DVE", T[f"glt{lb}_3"], T[f"glf{lb}_3"])
                tk = None
                for h in range(4):
                    for cc in range(8):
                        o = h * 512 + cc * 64
                        tk = self.mk("DVE", v.tensor_tensor_scan(
                            self.sq_s[:, o:o + 64],
                            A2[:, GLF + o:GLF + o + 64],
                            self.z64[:, 0:64], 0.0, ALU.add, ALU.add))
                T[f"bscan{lb}"] = tk
            self.V(f_scan)

            def f_exp(sc, lb=lb):
                self.w(sc, "ACT", T[f"bscan{lb}"], T[f"ebl{lb}"])
                t1 = self.mk("ACT", sc.activation(
                    A2[:, GLF:GLF + 2048], self.sq_s[:, 0:2048],
                    AF.Exp, scale=1.0 / 16.0))
                T[f"eb{lb}"] = t1
                T[f"em{lb}"] = self.mk("ACT", sc.activation(
                    A2[:, EM:EM + 2048], self.sq_s[:, 0:2048],
                    AF.Exp, scale=-1.0 / 16.0))
            self.SC(f_exp)

            def f_qe(v, lb=lb):
                self.w(v, "DVE", T[f"eb{lb}"], T[f"em{lb}"],
                       *[T[f"emt{lb}_{lt}"] for lt in range(4)])
                t1 = self.mk("DVE", v.scalar_tensor_tensor(
                    A2[:, QT:QT + 2048], A2[:, QT:QT + 2048],
                    float(DK) ** -0.5, A2[:, GLF:GLF + 2048],
                    ALU.mult, ALU.mult))
                t2 = self.mk("DVE", v.tensor_tensor(
                    A2[:, KT:KT + 2048], A2[:, KT:KT + 2048],
                    A2[:, EM:EM + 2048], ALU.mult))
                t3 = self.mk("DVE", v.tensor_tensor(
                    A2[:, KTM:KTM + 2048], A2[:, KTM:KTM + 2048],
                    A2[:, B32 + 2048:B32 + 4096], ALU.mult))
                self.w(v, "DVE", t1, t2, t3)
                T[f"qeke{lb}"] = t3
            self.V(f_qe)

            def emit_front(p, lb=lb):
                cc0, cc1 = 2 * p, 2 * p + 1
                def f_qep(v, p=p, lb=lb, cc0=cc0, cc1=cc1):
                    self.w(v, "DVE", T[f"qeke{lb}"], T["Sinit"],
                           T.get(f"spill{lb - 1}"),
                           T.get(f"qpc{lb}_{p - 1}") if p else
                           T.get(f"qpc{lb - 1}_3"))
                    eblr = self.eblh[:, 0:32].rearrange(
                        "p (h c) -> p h c", c=8)
                    t0 = self.mk("DVE", v.tensor_copy(
                        self.pdsnap[:, 0:4], self.pdh[:, 0:4]))
                    self.w(v, "DVE", t0)
                    t1 = self.mk("DVE", v.tensor_tensor(
                        self.pdh[:, 0:4], self.pdh[:, 0:4],
                        eblr[:, :, cc0], ALU.mult))
                    self.w(v, "DVE", t1)
                    t2 = self.mk("DVE", v.tensor_copy(
                        self.pdsnap[:, 4:8], self.pdh[:, 0:4]))
                    self.w(v, "DVE", t2)
                    t3 = self.mk("DVE", v.tensor_tensor(
                        self.pdh[:, 0:4], self.pdh[:, 0:4],
                        eblr[:, :, cc1], ALU.mult))
                    T[f"qeps{lb}_{p}"] = t2
                    T[f"qep{lb}_{p}"] = t3
                self.V(f_qep)
                def f_qpc(sc, p=p, lb=lb, cc0=cc0, cc1=cc1):
                    self.w(sc, "ACT", T[f"qeps{lb}_{p}"], T[f"qeke{lb}"],
                           T.get(f"spill{lb - 1}"))
                    tk = None
                    for h in range(4):
                        for k, cc in enumerate((cc0, cc1)):
                            tk = self.mk("ACT", sc.mul(
                                A2[:, QEB + h * 512 + cc * 64:
                                   QEB + h * 512 + cc * 64 + 64],
                                A2[:, QT + h * 512 + cc * 64:
                                   QT + h * 512 + cc * 64 + 64],
                                self.pdsnap[:, k * 4 + h:k * 4 + h + 1]))
                    T[f"qpc{lb}_{p}"] = tk
                self.SC(f_qpc)
                def f_at2(te, p=p, lb=lb):
                    self.w(te, "TE", T[f"qeke{lb}"], self.pslast["A"])
                    mm = None
                    for h in range(4):
                        mm = te.matmul(
                            self.PS_A[:, h * 128:(h + 1) * 128],
                            A2[:, KT + h * 512 + p * 128:
                               KT + h * 512 + p * 128 + 128],
                            A2[:, QT + h * 512 + p * 128:
                               QT + h * 512 + p * 128 + 128],
                            start=True, stop=True)
                    T[f"at2{lb}_{p}"] = self.mk("TE", mm)
                self.TE(f_at2)
                def f_msk(v, p=p, lb=lb):
                    self.w(v, "DVE", T[f"at2{lb}_{p}"])
                    tk = None
                    for h in range(4):
                        tk = self.mk("DVE", v.tensor_tensor(
                            self.xcs[:, h * 128:(h + 1) * 128],
                            self.PS_A[:, h * 128:(h + 1) * 128],
                            self.c_mask[:], ALU.mult))
                    T[f"at2m{lb}_{p}"] = tk
                    self.pslast["A"] = tk
                self.V(f_msk)

            def emit_mid(p, lb=lb):
                cc0, cc1 = 2 * p, 2 * p + 1
                def f_me(te, p=p, lb=lb):
                    self.w(te, "TE", T[f"qeke{lb}"], self.pslast["B"])
                    mm = None
                    for h in range(4):
                        mm = te.matmul(
                            self.PS_B[:, h * 256:(h + 1) * 256],
                            A2[0:64, KTM + p * 512 + h * 128:
                               KTM + p * 512 + h * 128 + 128],
                            A2[0:64, VTM + p * 1024 + h * 256:
                               VTM + p * 1024 + (h + 1) * 256],
                            start=True, stop=True)
                    T[f"me{lb}_{p}"] = self.mk("TE", mm)
                self.TE(f_me)
                def f_og1(te, p=p, lb=lb):
                    self.w(te, "TE", T[f"at2m{lb}_{p}"], self.pslast["C"],
                           self.pslast["A"],
                           T.get(f"soc{lb}_{p - 1}") if p else
                           T.get(f"soc{lb - 1}_3"))
                    mm = None
                    for h in range(4):
                        for eh in range(2):
                            i = h * 2 + eh
                            col = i * 128
                            te.matmul(self.PS_C[:, col:col + 128],
                                      A2[:, VTM + p * 1024 + h * 256 + eh * 128:
                                         VTM + p * 1024 + h * 256 + eh * 128 + 128],
                                      self.xcs[:, h * 128:(h + 1) * 128],
                                      start=(i % 4 == 0), stop=False)
                            mm = te.matmul(
                                self.PS_C[:, col:col + 64],
                                self.S_bf[:, h * 256 + eh * 128:
                                          h * 256 + eh * 128 + 128],
                                A2[:, QT + h * 512 + p * 128:
                                   QT + h * 512 + p * 128 + 64],
                                start=False, stop=False)
                    T[f"og1{lb}_{p}"] = self.mk("TE", mm)
                self.TE(f_og1)
                def f_se(v, p=p, lb=lb, cc0=cc0):
                    self.w(v, "DVE", T[f"me{lb}_{p}"], T[f"og1{lb}_{p}"],
                           T.get(f"soc{lb}_{p - 1}") if p else
                           T.get(f"soc{lb - 1}_3"))
                    if self.dbgM is not None and lb == 3 and p == 2:
                        self.mk("DVE", v.tensor_copy(self.dbgM[:, 0:1024],
                                                     self.PS_B[:, 0:1024]))
                        tdbg = self.mk("DVE", v.tensor_copy(
                            self.dbgM2[:, 0:1024], self.S_f32[:]))
                        self.w(v, "DVE", tdbg)
                        T["dbgM_tok"] = tdbg
                    t1 = self.mk("DVE", v.tensor_tensor(
                        self.sq_s[:, 0:1024], self.S_f32[:, 0:1024],
                        self.PS_B[:, 0:1024], ALU.add))
                    self.w(v, "DVE", t1)
                    t2 = None
                    for h in range(4):
                        t2 = self.mk("DVE", v.tensor_scalar(
                            self.S_f32[:, h * 256:(h + 1) * 256],
                            self.sq_s[:, h * 256:(h + 1) * 256],
                            self.eblh[:, h * 8 + cc0:h * 8 + cc0 + 1],
                            None, ALU.mult))
                        T[f"seu{lb}_{p}_{h}"] = t2
                    T[f"se{lb}_{p}"] = t2
                    self.pslast["B"] = t2
                self.V(f_se)
                def f_sec(sc, p=p, lb=lb):
                    self.w(sc, "ACT", T[f"og1{lb}_{p}"])
                    tk = None
                    for h in range(4):
                        self.w(sc, "ACT", T[f"seu{lb}_{p}_{h}"])
                        tk = self.mk("ACT", sc.copy(
                            self.S_bf[:, h * 256:(h + 1) * 256],
                            self.S_f32[:, h * 256:(h + 1) * 256]))
                    T[f"sec{lb}_{p}"] = tk
                self.SC(f_sec)
                def f_mo(te, p=p, lb=lb):
                    self.w(te, "TE", T[f"se{lb}_{p}"])
                    mm = None
                    for h in range(4):
                        mm = te.matmul(
                            self.PS_B[:, h * 256:(h + 1) * 256],
                            A2[64:128, KTM + p * 512 + h * 128:
                               KTM + p * 512 + h * 128 + 128],
                            A2[64:128, VTM + p * 1024 + h * 256:
                               VTM + p * 1024 + (h + 1) * 256],
                            start=True, stop=True)
                    T[f"mo{lb}_{p}"] = self.mk("TE", mm)
                self.TE(f_mo)

            def emit_tail(p, lb=lb):
                cc0, cc1 = 2 * p, 2 * p + 1
                def f_og2(te, p=p, lb=lb):
                    self.w(te, "TE", T[f"se{lb}_{p}"], T[f"sec{lb}_{p}"])
                    mm = None
                    for h in range(4):
                        for eh in range(2):
                            i = h * 2 + eh
                            col = i * 128
                            mm = te.matmul(
                                self.PS_C[:, col + 64:col + 128],
                                self.S_bf[:, h * 256 + eh * 128:
                                          h * 256 + eh * 128 + 128],
                                A2[:, QT + h * 512 + p * 128 + 64:
                                   QT + h * 512 + p * 128 + 128],
                                start=False, stop=(i % 4 == 3))
                    T[f"og2{lb}_{p}"] = self.mk("TE", mm)
                self.TE(f_og2)
                def f_so(v, p=p, lb=lb, cc1=cc1):
                    self.w(v, "DVE", T[f"mo{lb}_{p}"], T[f"og2{lb}_{p}"],
                           T[f"sec{lb}_{p}"])
                    t1 = self.mk("DVE", v.tensor_tensor(
                        self.sq_s[:, 0:1024], self.S_f32[:, 0:1024],
                        self.PS_B[:, 0:1024], ALU.add))
                    self.w(v, "DVE", t1)
                    t2 = None
                    for h in range(4):
                        t2 = self.mk("DVE", v.tensor_scalar(
                            self.S_f32[:, h * 256:(h + 1) * 256],
                            self.sq_s[:, h * 256:(h + 1) * 256],
                            self.eblh[:, h * 8 + cc1:h * 8 + cc1 + 1],
                            None, ALU.mult))
                        T[f"sou{lb}_{p}_{h}"] = t2
                    T[f"so{lb}_{p}"] = t2
                    self.pslast["B"] = t2
                self.V(f_so)
                def f_soc(sc, p=p, lb=lb):
                    self.w(sc, "ACT", T[f"og2{lb}_{p}"])
                    tk = None
                    for h in range(4):
                        self.w(sc, "ACT", T[f"sou{lb}_{p}_{h}"])
                        tk = self.mk("ACT", sc.copy(
                            self.S_bf[:, h * 256:(h + 1) * 256],
                            self.S_f32[:, h * 256:(h + 1) * 256]))
                    T[f"soc{lb}_{p}"] = tk
                self.SC(f_soc)
                def f_oe(v, p=p, lb=lb):
                    self.w(v, "DVE", T[f"og2{lb}_{p}"])
                    tk = self.mk("DVE", v.tensor_copy(
                        A2[:, OB:OB + 4096].rearrange(
                            "q (i w) -> q i w", w=512)[:, :, p * 128:
                                                       (p + 1) * 128],
                        self.PS_C[:, 0:1024].rearrange(
                            "q (i w) -> q i w", w=128)))
                    T[f"oe{lb}_{p}"] = tk
                    self.pslast["C"] = tk
                self.V(f_oe)

            emit_front(0)
            for p in range(4):
                emit_mid(p)
                if p < 3:
                    emit_front(p + 1)
                emit_tail(p)
            def f_sp(g, lb=lb):
                self.w(g, "POOL", T[f"oe{lb}_3"], T[f"qpc{lb}_3"])
                t1 = self.mkd("DPO", g.dma_start(
                    self.d_oS[lb, :, :], A2[:, OB:OB + 4096]))
                t2 = self.mkd("DPO", g.dma_start(
                    self.d_QE[lb, :, :], A2[:, QEB:QEB + 2048]))
                self.w(g, "POOL", t1, t2)
                T[f"spill{lb}"] = self.mk("POOL", g.engine_nop())
            self.PO(f_sp)
            if self.dbgM is not None and lb == 3:
                def f_dbg(g):
                    self.w(g, "POOL", T["dbgM_tok"])
                    tk = self.mkd("DPO", g.dma_start(self.d_M[:, 0:1024],
                                                     self.dbgM[:, 0:1024]))
                    tk = self.mkd("DPO", g.dma_start(self.d_M[:, 1024:2048],
                                                     self.dbgM2[:, 0:1024]))
                    tk = self.mkd("DPO", g.dma_start(
                        self.d_M2[:, 0:512], A2[:, 8192 + 1024:8192 + 1536]))
                    tk = self.mkd("DPO", g.dma_start(
                        self.d_M2[:, 512:1536], A2[:, 10240 + 2048:10240 + 3072]))
                    tk = self.mkd("DPO", g.dma_start(
                        self.d_M2[:, 1536:2048], A2[:, 4096 + 1024:4096 + 1536]))
                    self.w(g, "POOL", tk)
                self.PO(f_dbg)
            T[f"c1x{lb}"] = T[f"oe{lb}_3"]

        def f_send(g):
            pid = g.partition_id()
            slot = pid % 2
            self.w(g, "POOL", T["so3_3"])
            t1 = self.mkd("DPO", g.dma_start(
                self.sh_S[bass.ts(slot, 1), :, :].rearrange("a p c -> p (a c)"),
                self.S_f32[:]))
            self.w(g, "POOL", t1)
            self.mkd("DPO", g.dma_start(
                self.sh_flagS[bass.ts(slot, 1), :], self.flag_src[:]))
        self.PO(f_send)



        XB2, H2, SG2, X2B, NB2 = 0, 4096, 8192, 12288, 16384

        def f_w2(g):
            self.w(g, "POOL", T["c1x3"], T["so3_3"])
            self.dma_gw("DPO", g, A1[:, 0:8192], t["Wg"][:, :], NDG)
            T["wC2"] = self.dma_gw("DPO", g, A1[:, 8192:16384],
                                   t["Wo"][:, :], NDG)
        self.PO(f_w2)

        def f_recv(g):
            pid = g.partition_id()
            peer = (pid + 1) % 2
            with g.register("cnt_s") as r_cnt:
                g.reg_mov(r_cnt, 0)

                def poll():
                    g.dma_start(self.flag_rd[:],
                                self.sh_flagS[bass.ts(peer, 1), :]).then_inc(
                        self.psem2, 16)
                    g.reg_add(r_cnt, r_cnt, 16)
                    g.wait_ge(self.psem2, r_cnt)

                poll()
                with g.While(lambda: g.value_load(self.flag_rd[0:1, 0:1])
                             != g.value_load(self.flag_src[0:1, 0:1])):
                    poll()
            t1 = self.mkd("DPO", g.dma_start(
                self.Speer[:],
                self.sh_S[bass.ts(peer, 1), :, :].rearrange("a p c -> p (a c)")))
            self.w(g, "POOL", t1)
            T["Sprecv"] = self.mk("POOL", g.engine_nop())
        self.PO(f_recv)

        def f_sp2(v):
            self.w(v, "DVE", T["Sprecv"])
            T["Speer"] = self.mk("DVE", v.tensor_scalar(
                self.Speer[:], self.Speer[:], self.c_pmask[:, 0:1],
                None, ALU.mult))
        self.V(f_sp2)

        for lb in range(4):
            base = lb * 512

            def f_rl(g, lb=lb):
                self.w(g, "POOL", T[f"spill{lb}"],
                       T.get(f"c2x{lb - 1}"), T["so3_3"])
                t1 = self.mkd("DPO", g.dma_start(
                    A2[:, OB:OB + 4096], self.d_oS[lb, :, :]))
                t2 = self.mkd("DPO", g.dma_start(
                    A2[:, QEB:QEB + 2048], self.d_QE[lb, :, :]))
                self.w(g, "POOL", t1, t2)
                T[f"rl{lb}"] = self.mk("POOL", g.engine_nop())
            self.PO(f_rl)

            xb_sl = [A2[:, XB2 + dt * 512:XB2 + (dt + 1) * 512]
                     for dt in range(NDG)]
            h_sl = [A2[:, H2 + dt * 512:H2 + (dt + 1) * 512] for dt in range(NDG)]
            dep2 = T["wC2"] if lb == 0 else T[f"c2x{lb - 1}"]
            self.ln_fused(f"rm2C{lb}", base, 512, h_sl, dep2, self.c_anw,
                          hid_sl=xb_sl)

            for ot in range(NDG):
                def f_t(te, ot=ot, lb=lb):
                    self.w(te, "TE", T["wC2"], T.get(f"sg{lb}_{ot - 2}"))
                    if ot < 2:
                        self.w(te, "TE", self.pslast["B"])
                    mm = None
                    for dt in range(NDG):
                        self.w(te, "TE", T[f"rm2C{lb}_ap{dt}"])
                        mm = te.matmul(
                            self.PS_B[:, (ot % 2) * 512:(ot % 2) * 512 + 512],
                            A1[:, dt * 1024 + ot * 128:dt * 1024 + ot * 128 + 128],
                            A2[:, H2 + dt * 512:H2 + (dt + 1) * 512],
                            start=(dt == 0), stop=(dt == NDG - 1))
                    T[f"sgm{lb}_{ot}"] = self.mk("TE", mm)
                self.TE(f_t)

                def f_e(sc, ot=ot, lb=lb):
                    if ot == 0:
                        self.act_table(sc, 18)
                    self.w(sc, "ACT", T[f"sgm{lb}_{ot}"])
                    T[f"sg{lb}_{ot}"] = self.mk("ACT", sc.activation(
                        A2[:, SG2 + ot * 512:SG2 + (ot + 1) * 512],
                        self.PS_B[:, (ot % 2) * 512:(ot % 2) * 512 + 512],
                        AF.Silu))
                self.SC(f_e)
            self.pslast["B"] = T[f"sg{lb}_{NDG - 1}"]
            def cbank(i):
                if i < 2:
                    return self.PS_A[:, i * 512:(i + 1) * 512]
                if i < 4:
                    return self.PS_B[:, (i - 2) * 512:(i - 1) * 512]
                return self.PS_C[:, (i - 4) * 512:(i - 3) * 512]

            def f_corr(te, lb=lb):
                self.w(te, "TE", T["Speer"], T[f"rl{lb}"], self.pslast["A"],
                       self.pslast["B"], self.pslast["C"])
                mm = None
                for h in range(4):
                    for eh in range(2):
                        i = h * 2 + eh
                        mm = te.matmul(
                            cbank(i),
                            self.Speer[:, h * 256 + eh * 128:
                                       h * 256 + eh * 128 + 128],
                            A2[:, QEB + h * 512:QEB + (h + 1) * 512],
                            start=True, stop=True)
                T[f"corrm{lb}"] = self.mk("TE", mm)
            self.TE(f_corr)

            def f_corra(v, lb=lb):
                self.w(v, "DVE", T[f"corrm{lb}"])
                tk = None
                for i in range(8):
                    tk = self.mk("DVE", v.tensor_tensor(
                        A2[:, OB + i * 512:OB + (i + 1) * 512],
                        A2[:, OB + i * 512:OB + (i + 1) * 512],
                        cbank(i), ALU.add))
                T[f"corr{lb}"] = tk
            self.V(f_corra)
            ocorr = T[f"corr{lb}"]
            self.pslast["A"] = ocorr
            self.pslast["B"] = ocorr
            self.pslast["C"] = ocorr


            def f_sq(sc, lb=lb):
                self.act_table(sc, 6)
                self.w(sc, "ACT", ocorr, self.scr_free)
                T[f"gsqA{lb}"] = self.mk("ACT", sc.activation(
                    self.xcs[:, 0:2048], A2[:, OB:OB + 2048], AF.Square))
                T[f"gsqB{lb}"] = self.mk("ACT", sc.activation(
                    self.bc_b[:, 0:2048], A2[:, OB + 2048:OB + 4096],
                    AF.Square))
            self.SC(f_sq)

            def f_st(te, lb=lb):
                self.w(te, "TE", T[f"gsqA{lb}"], T[f"gsqB{lb}"], T["ones"],
                       self.pslast["C"])
                mm = None
                for h in range(4):
                    src = self.xcs if h < 2 else self.bc_b
                    for et in range(2):
                        off = (h % 2) * 1024 + et * 512
                        mm = te.matmul(self.PS_C[0:1, h * 512:h * 512 + 512],
                                       self.onesc[:], src[:, off:off + 512],
                                       start=(et == 0), stop=(et == 1))
                T[f"gst{lb}"] = self.mk("TE", mm)
            self.TE(f_st)

            def f_r(sc, lb=lb):
                self.w(sc, "ACT", T[f"gst{lb}"])
                t1 = self.mk("ACT", sc.activation(
                    self.sq_s[0:1, 0:2048], self.PS_C[0:1, 0:2048],
                    AF.Ln, bias=RMS_EPS, scale=1.0 / DV))
                self.w(sc, "ACT", t1)
                T[f"grr{lb}"] = self.mk("ACT", sc.activation(
                    self.bc_a[0:1, 0:2048], self.sq_s[0:1, 0:2048],
                    AF.Exp, scale=-0.5))
            self.SC(f_r)

            def f_bc(te, lb=lb):
                self.w(te, "TE", T[f"grr{lb}"])
                mm = None
                for h in range(4):
                    mm = te.matmul(self.PS_C[:, h * 512:(h + 1) * 512],
                                   self.onesr[:],
                                   self.bc_a[0:1, h * 512:(h + 1) * 512],
                                   start=True, stop=True)
                T[f"gbc{lb}"] = self.mk("TE", mm)
            self.TE(f_bc)

            def f_bce(sc, lb=lb):
                self.w(sc, "ACT", T[f"gbc{lb}"])
                T[f"gbce{lb}"] = self.mk("ACT", sc.copy(
                    self.bc_a[:, 0:2048], self.PS_C[:, 0:2048]))
            self.SC(f_bce)

            def f_app(v, lb=lb):
                self.w(v, "DVE", T[f"gbce{lb}"],
                       *[T[f"sg{lb}_{o}"] for o in range(NDG)])
                tks = {}
                for h in range(4):
                    for eh in range(2):
                        i = h * 2 + eh
                        self.w(v, "DVE", tks.get(h))
                        t1 = self.mk("DVE", v.scalar_tensor_tensor(
                            self.xcs[:, h * 512:(h + 1) * 512],
                            A2[:, OB + i * 512:OB + (i + 1) * 512],
                            self.c_gnw[:, eh:eh + 1],
                            self.bc_a[:, h * 512:(h + 1) * 512],
                            ALU.mult, ALU.mult))
                        self.w(v, "DVE", t1)
                        tks[h] = self.mk("DVE", v.tensor_tensor(
                            A2[:, OB + i * 512:OB + (i + 1) * 512],
                            self.xcs[:, h * 512:(h + 1) * 512],
                            A2[:, SG2 + i * 512:SG2 + (i + 1) * 512],
                            ALU.mult))
                        T[f"gath{lb}_{h}"] = tks[h]
                T[f"gat{lb}"] = tks[3]
            self.V(f_app)
            ogat = T[f"gat{lb}"]
            self.pslast["C"] = ogat
            self.scr_free = ogat

            for ot in range(NDG):
                def f_t(te, ot=ot, lb=lb):
                    self.w(te, "TE", T["wC2"], T.get(f"x2e{lb}_{ot - 2}"))
                    if ot < 2:
                        self.w(te, "TE", self.pslast["B"])
                    mm = None
                    for dvt in range(NDG):
                        self.w(te, "TE", T[f"gath{lb}_{dvt // 2}"])
                        mm = te.matmul(
                            self.PS_B[:, (ot % 2) * 512:(ot % 2) * 512 + 512],
                            A1[:, 8192 + dvt * 1024 + ot * 128:
                               8192 + dvt * 1024 + ot * 128 + 128],
                            A2[:, OB + dvt * 512:OB + (dvt + 1) * 512],
                            start=(dvt == 0), stop=(dvt == NDG - 1))
                    T[f"aom{lb}_{ot}"] = self.mk("TE", mm)
                self.TE(f_t)

                def f_x2(v, ot=ot, lb=lb, base=base):
                    self.w(v, "DVE", T[f"aom{lb}_{ot}"],
                           T.get(f"x2e{lb}_{ot - 1}"),
                           T.get(f"x2c{lb}_{ot - 1}"))
                    t1 = self.mk("DVE", v.tensor_tensor(
                        self.sq_s[:, 0:512],
                        self.PS_B[:, (ot % 2) * 512:(ot % 2) * 512 + 512],
                        A2[:, XB2 + ot * 512:XB2 + (ot + 1) * 512], ALU.add))
                    T[f"x2d{lb}_{ot}"] = t1
                    self.w(v, "DVE", t1)
                    T[f"x2e{lb}_{ot}"] = self.mk("DVE", v.tensor_tensor(
                        self.x_res[:, ot * TOK + base:ot * TOK + base + 512],
                        self.x_res[:, ot * TOK + base:ot * TOK + base + 512],
                        self.sq_s[:, 0:512], ALU.add))
                self.V(f_x2)

                def f_x2c(sc, ot=ot, lb=lb):
                    self.w(sc, "ACT", T[f"x2d{lb}_{ot}"])
                    T[f"x2c{lb}_{ot}"] = self.mk("ACT", sc.copy(
                        A2[:, X2B + ot * 512:X2B + (ot + 1) * 512],
                        self.sq_s[:, 0:512]))
                self.SC(f_x2c)

            self.pslast["B"] = T[f"x2e{lb}_{NDG - 1}"]
            x2_sl = [A2[:, X2B + dt * 512:X2B + (dt + 1) * 512]
                     for dt in range(NDG)]
            nb_sl = [A2[:, NB2 + dt * 512:NB2 + (dt + 1) * 512]
                     for dt in range(NDG)]
            self.ln_feat(f"nb{lb}", base, 512, nb_sl, T[f"x2c{lb}_{NDG - 1}"],
                         rms=True, wcol=self.c_mnw, eps=RMS_EPS, src_sl=x2_sl)

            def f_spn(g, lb=lb):
                self.w(g, "POOL", T[f"nb{lb}_done"])
                t1 = self.mkd("DPO", g.dma_start(
                    self.d_nb[lb, :, :], A2[:, NB2:NB2 + 4096]))
                self.w(g, "POOL", t1)
                T[f"nbsp{lb}"] = self.mk("POOL", g.engine_nop())
            self.PO(f_spn)
            T[f"c2x{lb}"] = T[f"nb{lb}_done"]

        NBD, AD, DWD = 0, 16384, 20480
        bcbf16 = self.bc_b.bitcast(F16)
        groups = [(0, 4), (4, 4), (8, 3)]

        prevD = {"sil": None, "am": None, "abuf": [None, None],
                 "wbuf": [T[f"sgm3_{NDG - 1}"], T[f"aom3_{NDG - 1}"]],
                 "dwbuf": [T[f"aom3_{NDG - 1}"], T["corrm3"]],
                 "bank": [self.pslast["B"], self.pslast["B"]],
                 "da": {}}
        first_gm = [True]

        def emit_gw(ih, gi):
            g0, gn = groups[gi]
            gidx = ih * 3 + gi
            buf = (gidx % 2) * 8192
            dwb = DWD + (gidx % 2) * 4096

            def f_gw(g, ih=ih, g0=g0, gn=gn, gidx=gidx, buf=buf, dwb=dwb):
                self.w(g, "POOL", prevD["wbuf"][gidx % 2],
                       prevD["dwbuf"][gidx % 2])
                self.dma_gw("DPO", g, A1[:, buf:buf + gn * 1024],
                            t["Wgate"][:, ih * 1408 + g0 * 128:
                                       ih * 1408 + (g0 + gn) * 128], NDG)
                self.dma_gw("DPO", g,
                            A1[:, buf + 4096:buf + 4096 + gn * 1024],
                            t["Wup"][:, ih * 1408 + g0 * 128:
                                     ih * 1408 + (g0 + gn) * 128], NDG)
                T[f"gw{gidx}"] = self.dma_gw(
                    "DPO", g, A2[:, dwb:dwb + gn * 1024],
                    t["Wdown"][ih * 1408 + g0 * 128:
                               ih * 1408 + (g0 + gn) * 128, :], gn)
            self.PO(f_gw)

        emit_gw(0, 0)

        def f_nbl(g):
            self.w(g, "POOL", T["c2x3"],
                   *[T[f"nbsp{lb}"] for lb in range(4)])
            tk = None
            for lb in range(4):
                tk = self.mkd("DPO", g.dma_start(
                    A2[:, NBD + lb * 4096:NBD + (lb + 1) * 4096],
                    self.d_nb[lb, :, :]))
            self.w(g, "POOL", tk)
            T["nbls"] = self.mk("POOL", g.engine_nop())
        self.PO(f_nbl)
        for ih in range(2):
            for gi, (g0, gn) in enumerate(groups):
                gidx = ih * 3 + gi
                buf = (gidx % 2) * 8192
                dwb = DWD + (gidx % 2) * 4096
                if gidx > 0:
                    emit_gw(ih, gi)

                for lb in range(4):
                    base = lb * 512
                    abuf = (lb % 2) * 2048
                    for ii in range(gn):
                        def f_gm(te, gidx=gidx, lb=lb, ii=ii, gn=gn, buf=buf):
                            self.w(te, "TE", T[f"gw{gidx}"], T["nbls"],
                                   prevD["sil"])
                            if first_gm[0]:
                                self.w(te, "TE", self.pslast["A"])
                                first_gm[0] = False
                            mm = None
                            for dt in range(NDG):
                                mm = te.matmul(
                                    self.PS_A[:, 0:512],
                                    A1[:, buf + (dt * gn + ii) * 128:
                                       buf + (dt * gn + ii) * 128 + 128],
                                    A2[:, NBD + lb * 4096 + dt * 512:
                                       NBD + lb * 4096 + (dt + 1) * 512],
                                    start=(dt == 0), stop=(dt == NDG - 1))
                            T[f"gm{gidx}_{lb}_{ii}"] = self.mk("TE", mm)
                        self.TE(f_gm)

                        def f_sil(sc, gidx=gidx, lb=lb, ii=ii):
                            if gidx == 0 and lb == 0 and ii == 0:
                                self.act_table(sc, 18)
                            self.w(sc, "ACT", T[f"gm{gidx}_{lb}_{ii}"],
                                   prevD["am"])
                            T[f"sil{gidx}_{lb}_{ii}"] = self.mk(
                                "ACT", sc.activation(
                                    bcbf16[:, 1024:1536], self.PS_A[:, 0:512],
                                    AF.Silu))
                            prevD["sil"] = T[f"sil{gidx}_{lb}_{ii}"]
                        self.SC(f_sil)

                        def f_um(te, gidx=gidx, lb=lb, ii=ii, gn=gn, buf=buf):
                            self.w(te, "TE", T[f"sil{gidx}_{lb}_{ii}"],
                                   prevD["am"] or self.pslast["A"])
                            mm = None
                            for dt in range(NDG):
                                mm = te.matmul(
                                    self.PS_A[:, 512:1024],
                                    A1[:, buf + 4096 + (dt * gn + ii) * 128:
                                       buf + 4096 + (dt * gn + ii) * 128 + 128],
                                    A2[:, NBD + lb * 4096 + dt * 512:
                                       NBD + lb * 4096 + (dt + 1) * 512],
                                    start=(dt == 0), stop=(dt == NDG - 1))
                            T[f"um{gidx}_{lb}_{ii}"] = self.mk("TE", mm)
                        self.TE(f_um)

                        def f_am(v, gidx=gidx, lb=lb, ii=ii, abuf=abuf):
                            self.w(v, "DVE", T[f"um{gidx}_{lb}_{ii}"],
                                   T[f"sil{gidx}_{lb}_{ii}"],
                                   prevD["abuf"][(lb % 2)])
                            T[f"am{gidx}_{lb}_{ii}"] = self.mk(
                                "DVE", v.tensor_tensor(
                                    A2[:, AD + abuf + ii * 512:
                                       AD + abuf + (ii + 1) * 512],
                                    bcbf16[:, 1024:1536],
                                    self.PS_A[:, 512:1024], ALU.mult))
                            prevD["am"] = T[f"am{gidx}_{lb}_{ii}"]
                        self.V(f_am)

                    for dt in range(NDG):
                        def f_dm(te, gidx=gidx, lb=lb, dt=dt, gn=gn,
                                 dwb=dwb, abuf=abuf):
                            self.w(te, "TE", T[f"gw{gidx}"],
                                   prevD["bank"][dt % 2])
                            mm = None
                            for ii in range(gn):
                                self.w(te, "TE", T[f"am{gidx}_{lb}_{ii}"])
                                mm = te.matmul(
                                    self.PS_B[:, (dt % 2) * 512:
                                              (dt % 2) * 512 + 512],
                                    A2[:, dwb + ii * 1024 + dt * 128:
                                       dwb + ii * 1024 + dt * 128 + 128],
                                    A2[:, AD + abuf + ii * 512:
                                       AD + abuf + (ii + 1) * 512],
                                    start=(ii == 0), stop=(ii == gn - 1))
                            T[f"dmm{gidx}_{lb}_{dt}"] = self.mk("TE", mm)
                        self.TE(f_dm)

                        def f_da(v, gidx=gidx, lb=lb, dt=dt, base=base):
                            self.w(v, "DVE", T[f"dmm{gidx}_{lb}_{dt}"],
                                   prevD["da"].get((lb, dt)) or
                                   T.get(f"x2e{lb}_{dt}"))
                            tk = self.mk("DVE", v.tensor_tensor(
                                self.x_res[:, dt * TOK + base:
                                           dt * TOK + base + 512],
                                self.x_res[:, dt * TOK + base:
                                           dt * TOK + base + 512],
                                self.PS_B[:, (dt % 2) * 512:
                                          (dt % 2) * 512 + 512], ALU.add))
                            prevD["da"][(lb, dt)] = tk
                            prevD["bank"][dt % 2] = tk
                        self.V(f_da)
                    prevD["abuf"][(lb % 2)] = T[f"dmm{gidx}_{lb}_{NDG - 1}"]
                prevD["wbuf"][gidx % 2] = T[f"um{gidx}_3_{gn - 1}"]
                prevD["dwbuf"][gidx % 2] = T[f"dmm{gidx}_3_{NDG - 1}"]
        T["attnB_done"] = prevD["da"][(3, NDG - 1)]

    def e_store(self):
        T = self.T

        def f_sy(sy):
            dep = T.get("ffnE_done") or T.get("attnB_done") or \
                T.get("ffnB_done") or T.get("attnA_done")
            tk = None
            for lb in range(4):
                self.w(sy, "SP", T.get(f"ffnE_x{lb}") or dep)
                tk = self.mkd("DSY", sy.dma_start(
                    self.t_out[:, lb * 512:(lb + 1) * 512].rearrange(
                        "(g p) w -> p g w", p=128),
                    self.x_res[:, 0:NDG * TOK].rearrange(
                        "p (g w) -> p g w", g=NDG)[:, :,
                                                   lb * 512:(lb + 1) * 512]))
            if self.DBG:
                tk = self.mkd("DSY", sy.dma_start(self.d_ar1[:, :], self.AR1[:, :]))
                tk = self.mkd("DSY", sy.dma_start(self.d_ar2[:, :], self.AR2[:, :]))
                tk = self.mkd("DSY", sy.dma_start(
                    self.d_cols[:, 0:64], self.cols[:, 0:64]))
                tk = self.mkd("DSY", sy.dma_start(
                    self.d_cols[:, 64:128], self.colsC[:, 0:64]))
                tk = self.mkd("DSY", sy.dma_start(
                    self.d_cols[:, 128:192], self.colsD[:, 0:64]))
                tk = self.mkd("DSY", sy.dma_start(
                    self.d_cols[:, 192:200], self.colsT[:, 0:8]))
                tk = self.mkd("DSY", sy.dma_start(
                    self.d_cols[:, 200:208], self.tpeer[:, 0:8]))
                tk = self.mkd("DSY", sy.dma_start(self.d_nyq[:, :], self.nyqrow[:, :]))
                if hasattr(self, "S_f32"):
                    tk = self.mkd("DSY", sy.dma_start(self.d_S[:, :], self.S_f32[:]))
                    tk = self.mkd("DSY", sy.dma_start(
                        self.d_cols[:, 208:240], self.eblh[:, 0:32]))
            self.w(sy, "SP", tk)
        self.SY(f_sy)


_CACHED = {}


def build_program():
    if "nc" not in _CACHED:
        k1 = K()
        k1.build()
        wm = {qn: sorted(v) for qn, v in k1.rec_wm.items()}
        k2 = K()
        k2.dma_wm = wm
        _CACHED["nc"] = k2.build()
    return _CACHED["nc"]


def kernel(**inputs):
    assert inputs["L0_attn_a_D_im"].shape == (D,)
    nonce = int.from_bytes(np.random.bytes(3), "little") + 7
    nc = build_program()
    in_maps = [host_prepare(c, inputs, nonce) for c in range(8)]
    res = run_bass_kernel_spmd(nc, in_maps, core_ids=list(range(8)))
    out = np.empty((BSZ, SEQ, D), np.float32)
    for c in range(8):
        b, s = c // 2, c % 2
        out[b, s * TOK:(s + 1) * TOK, :] = np.asarray(res.results[c]["y_out"]).T
    return out


# version 32
# speedup vs baseline: 3.0866x; 3.0866x over previous
import bisect
import numpy as np
import ml_dtypes

_SHARED_SPACE = "Shared"
import concourse.bass as bass
import concourse.mybir as mybir
from concourse.bass_utils import run_bass_kernel_spmd

F32 = mybir.dt.float32
BF16 = mybir.dt.bfloat16
F16 = mybir.dt.float16
F8E4 = mybir.dt.float8e4
I32 = mybir.dt.int32
AF = mybir.ActivationFunctionType
ALU = mybir.AluOpType
AX = mybir.AxisListType

BSZ, SEQ, D = 4, 4096, 1024
TOK = 2048
NLT = 16
NDG = 8
CH = 256
NCH = 8
LN_EPS = 1e-5
RMS_EPS = 1e-6
HFF = 4096
IMLP = 2816
NHEAD = 4
DK = 128
DV = 256
BF = ml_dtypes.bfloat16

STAGES = ("A", "B", "C", "E")

XA = 0
XBUF = [8192, 12288, 16384]
YH = 20480
KH = 24576


def host_constants():
    c = {}
    t = np.arange(256)
    f = np.arange(257)
    ang = 2.0 * np.pi * np.outer(t, f) / 512.0
    Dre = np.cos(ang)
    Dim = -np.sin(ang)
    DfT = np.zeros((128, 8 * 128), np.float32)
    for tt in range(2):
        for comp in range(2):
            M = Dre if comp == 0 else Dim
            for fh in range(2):
                k = tt * 4 + comp * 2 + fh
                DfT[:, k * 128:(k + 1) * 128] = M[tt * 128:(tt + 1) * 128,
                                                  fh * 128:(fh + 1) * 128]
    c["DfT"] = DfT
    Dn = np.zeros((128, 2), np.float32)
    for tt in range(2):
        Dn[:, tt] = Dre[tt * 128:(tt + 1) * 128, 256]
    c["Dnyq"] = Dn
    Ere = np.zeros((257, 256))
    Eim = np.zeros((257, 256))
    for ff in range(257):
        e = np.zeros(257, complex)
        e[ff] = 1.0
        Ere[ff] = np.fft.irfft(e, 512)[256:512]
        e[ff] = 1j
        Eim[ff] = np.fft.irfft(e, 512)[256:512]
    Einv = np.zeros((128, 4 * 256), np.float32)
    for comp in range(2):
        M = Ere if comp == 0 else Eim
        for fh in range(2):
            k = comp * 2 + fh
            Einv[:, k * 256:(k + 1) * 256] = M[fh * 128:(fh + 1) * 128, :]
    c["Einv"] = Einv
    c["Enyq"] = np.ascontiguousarray(Ere[256:257, :]).astype(np.float32)
    c["sgnf"] = np.where(np.arange(128) % 2 == 0, 1.0, -1.0).astype(
        ml_dtypes.bfloat16).reshape(128, 1)
    tl = np.arange(256)
    UT = (tl[:, None] <= tl[None, :]).astype(np.float32)
    c["utri_a"] = np.ascontiguousarray(UT[0:128, :])
    c["utri_b"] = np.ascontiguousarray(UT[128:256, 128:256])
    s64 = np.arange(64)
    tril64 = (s64[:, None] <= s64[None, :]).astype(np.float32)
    mp = np.zeros((128, 128), np.float32)
    mp[0:64, 0:64] = tril64
    mp[64:128, 64:128] = tril64
    c["mask_pair"] = mp
    T2 = np.zeros((128, 128), np.float32)
    T2[0:64, 0:64] = tril64 / 16.0
    T2[64:128, 64:128] = tril64 / 16.0
    c["trilT16"] = T2
    return c


HOST_CONST = host_constants()


def host_prepare(core, full, nonce):
    b, s = core // 2, core % 2
    m = {k: v for k, v in HOST_CONST.items()}
    x = full["L0_attn_a_x"]
    m["xT"] = np.ascontiguousarray(x[b, s * TOK:(s + 1) * TOK, :].T)
    m["WinT"] = np.ascontiguousarray(full["L0_attn_a_W_in"].T)
    m["b_in_bc"] = np.ascontiguousarray(np.broadcast_to(full["L0_attn_a_b_in"], (128, D)))
    m["gamma_bc"] = np.ascontiguousarray(np.broadcast_to(full["L0_attn_a_gamma"], (128, D)))
    m["beta_bc"] = np.ascontiguousarray(np.broadcast_to(full["L0_attn_a_beta"], (128, D)))
    m["A_re"] = np.ascontiguousarray(full["L0_attn_a_A_re"].reshape(64, 1))
    m["A_im"] = np.ascontiguousarray(full["L0_attn_a_A_im"].reshape(64, 1))
    m["B_re"] = np.ascontiguousarray(full["L0_attn_a_B_re"].reshape(64, 1))
    m["B_im"] = np.ascontiguousarray(full["L0_attn_a_B_im"].reshape(64, 1))
    m["CT_re"] = np.ascontiguousarray(full["L0_attn_a_C_re"].T)
    m["CT_im"] = np.ascontiguousarray(full["L0_attn_a_C_im"].T)
    m["D_bc"] = np.ascontiguousarray(np.broadcast_to(full["L0_attn_a_D_re"], (128, D)))
    m["D_col"] = np.ascontiguousarray(full["L0_attn_a_D_re"].reshape(NDG, 128).T)
    m["w1T"] = np.ascontiguousarray(full["L0_ffn_w1"].T)
    m["b1c"] = np.ascontiguousarray(full["L0_ffn_b1"].reshape(HFF // 128, 128).T)
    m["w2T"] = np.ascontiguousarray(full["L0_ffn_w2"].T)
    m["b2c"] = np.ascontiguousarray(full["L0_ffn_b2"].reshape(NDG, 128).T)
    m["attn_nw"] = np.ascontiguousarray(full["L0_attn_b_attn_norm_w"].reshape(NDG, 128).T)
    m["mlp_nw"] = np.ascontiguousarray(full["L0_attn_b_mlp_norm_w"].reshape(NDG, 128).T)
    m["gnw"] = np.ascontiguousarray(full["L0_attn_b_gnorm_w"].reshape(2, 128).T)
    for ks, kd in [("Wq", "Wq"), ("Wk", "Wk"), ("Wv", "Wv"), ("Wg", "Wg"),
                   ("Wgk1", "Wgk1"), ("Wgk2", "Wgk2"), ("Wo", "Wo"),
                   ("W_gate", "Wgate"), ("W_up", "Wup"), ("W_down", "Wdown")]:
        m[kd] = full["L0_attn_b_" + ks]
    m["bgk2c"] = np.ascontiguousarray(full["L0_attn_b_bgk2"].reshape(4, 128).T)
    m["bgk2_bc"] = np.ascontiguousarray(np.broadcast_to(full["L0_attn_b_bgk2"], (128, 512)))
    m["nonce"] = np.full((1, 16), nonce, np.int32)
    m["pmaskc"] = np.full((128, 1), float(s), np.float32)
    return m


class _SerialEng:

    _OPS = ("tensor_scalar", "tensor_tensor", "memset", "tensor_copy",
            "tensor_reduce", "scalar_tensor_tensor", "tensor_single_scalar",
            "activation", "tensor_select", "iota", "matmul")

    def __init__(self, k, eng, en):
        self._k, self._e, self._en = k, eng, en

    def __getattr__(self, n):
        a = getattr(self._e, n)
        if n in _SerialEng._OPS:
            k, en = self._k, self._en

            def wrap(*args, **kw):
                k.w(self._e, en, (en, k.cnt[en]))
                return a(*args, **kw)
            return wrap
        return a


class _QProxy:

    def __init__(self, k, g, qn):
        self._k, self._g, self._qn = k, g, qn

    def __getattr__(self, n):
        return getattr(self._g, n)

    def dma_start(self, *a, **kw):
        k, qn = self._k, self._qn
        if k.dma_wm is not None:
            wm = k.dma_wm[qn]
            pre = k.cnt[qn]
            i = bisect.bisect_right(wm, pre) - 1
            if i >= 0 and wm[i] > k.enf[qn]:
                self._g.wait_ge(k.sem[qn], wm[i])
                k.enf[qn] = wm[i]
        return self._g.dma_start(*a, **kw)


class K:
    def __init__(self):
        self.nc = bass.Bass("TRN2", target_bir_lowering=False)
        nc = self.nc
        self.sem = {}
        self.cnt = {}
        self.waited = {}
        for n in ["TE", "DVE", "ACT", "POOL", "SP", "DSY", "DPO", "DAC"]:
            self.sem[n] = nc.alloc_semaphore("sem_" + n)
            self.cnt[n] = 0
            self.waited[n] = {}
        self.psem = nc.alloc_semaphore("psem")
        self.psem2 = nc.alloc_semaphore("psem2")
        self.rec_wm = {"DSY": set(), "DPO": set(), "DAC": set()}
        self.dma_wm = None
        self.enf = {"DSY": 0, "DPO": 0, "DAC": 0}
        self.T = {}
        self.scr_free = None
        self.pslast = {"A": None, "B": None, "C": None}
        self._io()
        self._sbuf()
        self._psum()

    def mk(self, en, instr):
        self.cnt[en] += 1
        instr.then_inc(self.sem[en], 1)
        return (en, self.cnt[en])

    def mkd(self, qn, instr):
        self.cnt[qn] += 16
        instr.then_inc(self.sem[qn], 16)
        return (qn, self.cnt[qn])

    def dma_gw(self, qn, g, dst2d, src2d, ng):
        return self.mkd(qn, g.dma_start(
            dst2d.rearrange("p (g w) -> p g w", g=ng),
            src2d.rearrange("(g p) w -> p g w", p=128)))

    def act_table(self, s, set_id):
        s.add_instruction(mybir.InstLoadActFuncSet(
            name=s.bass.get_next_instruction_name(),
            act_func_set_id=set_id, ins=[], outs=[]))

    def w(self, eng, consumer, *toks):
        best = {}
        for t in toks:
            if t is None:
                continue
            p, v = t
            if best.get(p, 0) < v:
                best[p] = v
        for p, v in sorted(best.items()):
            if self.waited[consumer].get(p, 0) >= v:
                continue
            eng.wait_ge(self.sem[p], v)
            self.waited[consumer][p] = v
            if p in self.rec_wm:
                self.rec_wm[p].add(v)

    def _io(self):
        nc = self.nc

        def din(name, shape, dtype=F32):
            return nc.dram_tensor(name, shape, dtype, kind="ExternalInput")
        self.t = t = {}
        t["xT"] = din("xT", [D, TOK])
        t["WinT"] = din("WinT", [D, D])
        t["b_in_bc"] = din("b_in_bc", [128, D])
        t["gamma_bc"] = din("gamma_bc", [128, D])
        t["beta_bc"] = din("beta_bc", [128, D])
        t["A_re"] = din("A_re", [64, 1])
        t["A_im"] = din("A_im", [64, 1])
        t["B_re"] = din("B_re", [64, 1])
        t["B_im"] = din("B_im", [64, 1])
        t["CT_re"] = din("CT_re", [64, D])
        t["CT_im"] = din("CT_im", [64, D])
        t["D_bc"] = din("D_bc", [128, D])
        t["D_col"] = din("D_col", [128, NDG])
        t["DfT"] = din("DfT", [128, 8 * 128])
        t["Dnyq"] = din("Dnyq", [128, 2])
        t["Einv"] = din("Einv", [128, 4 * 256])
        t["Enyq"] = din("Enyq", [1, 256])
        t["sgnf"] = din("sgnf", [128, 1], BF16)
        t["utri_a"] = din("utri_a", [128, 256])
        t["utri_b"] = din("utri_b", [128, 128])
        t["mask_pair"] = din("mask_pair", [128, 128])
        t["trilT16"] = din("trilT16", [128, 128])
        t["w1T"] = din("w1T", [D, HFF])
        t["b1c"] = din("b1c", [128, HFF // 128])
        t["w2T"] = din("w2T", [HFF, D])
        t["b2c"] = din("b2c", [128, NDG])
        t["attn_nw"] = din("attn_nw", [128, NDG])
        t["mlp_nw"] = din("mlp_nw", [128, NDG])
        t["gnw"] = din("gnw", [128, 2])
        t["Wq"] = din("Wq", [D, 512])
        t["Wk"] = din("Wk", [D, 512])
        t["Wv"] = din("Wv", [D, D])
        t["Wg"] = din("Wg", [D, D])
        t["Wgk1"] = din("Wgk1", [D, 16])
        t["Wgk2"] = din("Wgk2", [16, 512])
        t["bgk2c"] = din("bgk2c", [128, 4])
        t["bgk2_bc"] = din("bgk2_bc", [128, 512])
        t["Wo"] = din("Wo", [D, D])
        t["Wgate"] = din("Wgate", [D, IMLP])
        t["Wup"] = din("Wup", [D, IMLP])
        t["Wdown"] = din("Wdown", [IMLP, D])
        t["nonce"] = din("nonce", [1, 16], I32)
        t["pmaskc"] = din("pmaskc", [128, 1])
        self.t_out = nc.dram_tensor("y_out", [D, TOK], F32, kind="ExternalOutput")
        import os
        self.DBG = bool(int(os.environ.get("KDBG", "0")))
        if self.DBG:
            self.d_ar1 = nc.dram_tensor("d_ar1", [128, 16384], BF16, kind="ExternalOutput")
            self.d_ar2 = nc.dram_tensor("d_ar2", [128, 28672], BF16, kind="ExternalOutput")
            self.d_cols = nc.dram_tensor("d_cols", [128, 256], F32, kind="ExternalOutput")
            self.d_S = nc.dram_tensor("d_S", [128, 1024], F32, kind="ExternalOutput")
            self.d_M = nc.dram_tensor("d_M", [128, 2048], F32, kind="ExternalOutput")
            self.d_M2 = nc.dram_tensor("d_M2", [128, 2048], BF16, kind="ExternalOutput")
            self.d_nyq = nc.dram_tensor("d_nyq", [1, 2048], BF16, kind="ExternalOutput")
        self.sh_tail = nc.dram_tensor("sh_tail", [2, 128, 2048], BF16,
                                      kind="Internal", addr_space=_SHARED_SPACE)
        self.sh_T = nc.dram_tensor("sh_T", [2, 128, NDG], F32,
                                   kind="Internal", addr_space=_SHARED_SPACE)
        self.sh_S = nc.dram_tensor("sh_S", [2, 128, NHEAD * DV], F32,
                                   kind="Internal", addr_space=_SHARED_SPACE)
        self.sh_flagA = nc.dram_tensor("sh_flagA", [2, 16], I32,
                                       kind="Internal", addr_space=_SHARED_SPACE)
        self.sh_flagS = nc.dram_tensor("sh_flagS", [2, 16], I32,
                                       kind="Internal", addr_space=_SHARED_SPACE)
        import os as _os
        _dbg = bool(int(_os.environ.get("KDBG", "0")))
        _kind = "ExternalOutput" if _dbg else "Internal"
        self.d_oS = nc.dram_tensor("d_oS", [4, 128, 4096], BF16, kind=_kind)
        self.d_QE = nc.dram_tensor("d_QE", [4, 128, 2048], BF16, kind=_kind)
        self.d_nb = nc.dram_tensor("d_nb", [4, 128, 4096], BF16, kind=_kind)

    def _sbuf(self):
        A = self.nc.alloc_sbuf_tensor
        self.x_res = A("x_res", [128, NDG * TOK], F32)
        self.AR1 = A("AR1", [128, 16384], BF16)
        self.AR2 = A("AR2", [128, 28672], BF16)
        self.AR1f = self.AR1.bitcast(F32)
        self.AR2f = self.AR2.bitcast(F32)
        self.AR1q = self.AR1.bitcast(F8E4)
        self.c_pb = self.AR2[0:64, 8192:8704]
        self.AR2q = self.AR2.bitcast(F8E4)
        self.AR2h = self.AR2.bitcast(F16)
        self.onesc = A("onesc", [128, 1], BF16)
        self.onescf = A("onescf", [128, 1], F32)
        self.onesr = A("onesr", [1, 128], BF16)
        self.RS1 = A("RS1", [1, 1024], F32)
        self.bc_a = A("bc_a", [128, TOK], BF16)
        self.bc_b = A("bc_b", [128, TOK], BF16)
        self.row_bf = self.bc_b[0:1, :]
        self.sq_s = A("sq_s", [128, TOK], F32)
        self.xcs = A("xcs", [128, TOK], BF16)
        self.xad_s = self.xcs
        self.sc_a = self.sq_s[:, 1024:2048]
        self.pw_t = self.sq_s.bitcast(BF16)[:, 0:1024]
        self.cols = A("cols", [128, 64], F32)
        self.colsT = A("colsT", [128, 8], F32)
        self.colsC = A("colsC", [128, 64], F32)
        self.colsD = A("colsD", [128, 64], F32)
        self.colmu = A("colmu", [128, 1], F32)
        self.colv = A("colv", [128, 1], F32)
        self.tpeer = A("tpeer", [128, NDG], F32)
        self.pcol = A("pcol", [128, NDG], F32)
        self.c_bin = A("c_bin", [128, D], BF16)
        self.c_gam = A("c_gam", [128, D], BF16)
        self.c_bet = A("c_bet", [128, D], BF16)
        self.c_Dbc = A("c_Dbc", [128, D], BF16)
        self.c_Dcol = A("c_Dcol", [128, NDG], F32)
        self.c_DfT = A("c_DfT", [128, 8 * 128], BF16)
        self.c_Dnyq = A("c_Dnyq", [128, 2], BF16)
        self.c_Einv = A("c_Einv", [128, 4 * 256], BF16)
        self.c_Enyq = A("c_Enyq", [1, 256], BF16)
        self.c_sgnf = A("c_sgnf", [128, 1], BF16)
        self.c_utri_a = A("c_utri_a", [128, 256], BF16)
        self.c_utri_b = A("c_utri_b", [128, 128], BF16)
        self.c_mask = A("c_mask", [128, 128], BF16)
        self.c_tril16 = A("c_tril16", [128, 128], F32)
        self.c_Are = A("c_Are", [64, 1], F32)
        self.c_Aim = A("c_Aim", [64, 1], F32)
        self.c_Bre = A("c_Bre", [64, 1], F32)
        self.c_Bim = A("c_Bim", [64, 1], F32)
        self.c_CTre = self.AR1[0:64, 0:1024]
        self.c_CTim = self.AR1[0:64, 1024:2048]
        self.pows = self.sq_s[0:64, 0:1024]
        self.apow = self.cols[0:64, 56:64]
        self.c_b1c = A("c_b1c", [128, 32], F32)
        self.c_b2c = A("c_b2c", [128, NDG], F32)
        self.c_bq = A("c_bq", [128, NDG], F32)
        self.c_bq16 = A("c_bq16", [128, NDG], F16)
        self.c_anw = A("c_anw", [128, NDG], F32)
        self.c_mnw = A("c_mnw", [128, NDG], F32)
        self.c_gnw = A("c_gnw", [128, 2], F32)
        self.c_bgk2c = A("c_bgk2c", [128, 4], F32)
        self.c_bgk2bc = A("c_bgk2bc", [128, 512], BF16)
        self.c_pmask = A("c_pmask", [128, 1], F32)
        self.flag_src = A("flag_src", [1, 16], I32)
        self.flag_rd = A("flag_rd", [1, 16], I32)
        self.nyqrow = A("nyqrow", [1, 2048], BF16)
        self.y_nyq = self.nyqrow[0:1, 0:1024]
        self.k_nyq = self.nyqrow[0:1, 1024:2048]
        self.z64 = A("z64", [128, 64], F32)
        import os as _os
        self.dbgM = self.bc_a.bitcast(F32) if bool(int(_os.environ.get(
            "KDBG", "0"))) else None
        self.dbgM2 = self.bc_b.bitcast(F32) if self.dbgM is not None else None
        self.c_eps = A("c_eps", [128, 6], F32)
        self.S_f32 = A("S_f32", [128, NHEAD * DV], F32)
        self.S_bf = A("S_bf", [128, NHEAD * DV], BF16)
        self.Speer = A("Speer", [128, NHEAD * DV], BF16)
        self.pdh = A("pdh", [128, NHEAD], F32)
        self.pdsnap = A("pdsnap", [128, 2 * NHEAD], F32)
        self.eblh = A("eblh", [128, NHEAD * 8], F32)
        self.g1_bf = A("g1_bf", [16, 512], BF16)
        self.wgk1 = A("wgk1", [128, NDG * 16], BF16)
        self.wgk2 = A("wgk2", [16, 512], BF16)

    def _psum(self):
        nc = self.nc
        self.PS_A = nc.alloc_psum_tensor("PS_A", [128, 1024], F32)
        self.PS_B = nc.alloc_psum_tensor("PS_B", [128, 1024], F32)
        self.PS_C = nc.alloc_psum_tensor("PS_C", [128, 2048], F32)

    def build(self):
        with self.nc.Block("main") as blk:
            self.blk = blk
            self.e_load()
            if "A" in STAGES:
                self.e_attn_a()
            if "B" in STAGES:
                self.e_ffn("B")
            if "C" in STAGES:
                self.e_attn_b()
            if "E" in STAGES:
                self.e_ffn("E")
            self.e_store()
        return self.nc

    def V(self, f):
        self.blk.vector(f)

    def SC(self, f):
        self.blk.scalar(f)

    def TE(self, f):
        self.blk.tensor(f)

    def PO(self, f):
        self.blk.gpsimd(lambda g: f(_QProxy(self, g, "DPO")))

    def SY(self, f):
        self.blk.sync(lambda g: f(_QProxy(self, g, "DSY")))

    def e_load(self):
        T, t = self.T, self.t

        def f_tbl(s):
            self.act_table(s, 6)
        self.SC(f_tbl)

        def f_sy(sy):
            for dst, src in [
                (self.c_sgnf, t["sgnf"]), (self.c_Dcol, t["D_col"]),
                (self.c_Are, t["A_re"]), (self.c_Aim, t["A_im"]),
                (self.c_Bre, t["B_re"]), (self.c_Bim, t["B_im"]),
                (self.c_tril16, t["trilT16"]),
                (self.flag_src, t["nonce"]), (self.c_pmask, t["pmaskc"]),
                (self.c_b1c, t["b1c"]), (self.c_b2c, t["b2c"]),
                (self.c_anw, t["attn_nw"]), (self.c_mnw, t["mlp_nw"]),
                (self.c_gnw, t["gnw"]), (self.c_bgk2c, t["bgk2c"]),
            ]:
                tk = self.mkd("DSY", sy.dma_start(dst[:], src[:]))
            T["consts_f32"] = tk
            T["x_ld"] = self.dma_gw("DSY", sy, self.x_res[:, 0:NDG * TOK],
                                    t["xT"][:, :], NDG)
        self.SY(f_sy)

        def f_po(g):
            tk = None
            for dst, src in [
                (self.c_bin, t["b_in_bc"]),
                (self.c_DfT, t["DfT"]), (self.c_Dnyq, t["Dnyq"]),
                (self.c_Einv, t["Einv"]), (self.c_Enyq, t["Enyq"]),
                (self.c_utri_a, t["utri_a"]), (self.c_utri_b, t["utri_b"]),
                (self.c_mask, t["mask_pair"]),
                (self.c_gam, t["gamma_bc"]), (self.c_bet, t["beta_bc"]),
                (self.c_Dbc, t["D_bc"]), (self.c_CTre, t["CT_re"]),
                (self.c_CTim, t["CT_im"]), (self.c_bgk2bc, t["bgk2_bc"]),
            ]:
                tk = self.mkd("DPO", g.dma_start(dst[:], src[:]))
            T["consts_bf"] = tk
            T["WinT_ld"] = self.dma_gw("DPO", g, self.AR2[:, 0:NDG * 1024],
                                       t["WinT"][:, :], NDG)
        self.PO(f_po)

        def f_v(v):
            self.mk("DVE", v.memset(self.onesc[:], 1.0))
            self.mk("DVE", v.memset(self.onescf[:], 1.0))
            self.mk("DVE", v.memset(self.onesr[:], 1.0))
            self.mk("DVE", v.memset(self.c_eps[:, 0:1], LN_EPS))
            self.mk("DVE", v.memset(self.c_eps[:, 1:2], RMS_EPS))
            self.mk("DVE", v.memset(self.c_eps[:, 2:3], 1.0))
            self.mk("DVE", v.memset(self.c_eps[:, 3:4], float(LN_EPS * RMS_EPS)))
            T["ones"] = self.mk("DVE", v.memset(self.z64[:], 0.0))
        self.V(f_v)
        self.nc.const_aps.register(F32, LN_EPS, self.c_eps[:, 0:1])
        self.nc.const_aps.register(F32, RMS_EPS, self.c_eps[:, 1:2])
        self.nc.const_aps.register(F32, 1.0, self.c_eps[:, 2:3])
        self.nc.const_aps.register(F32, float(LN_EPS * RMS_EPS),
                                   self.c_eps[:, 3:4])

    def ln_feat(self, tag, off, n, dst_sl, src_dep, rms=False, wcol=None,
                eps=LN_EPS, src_sl=None):
        T = self.T
        nt = n // 512
        assert 1 <= nt <= 2

        def srcap(g):
            if src_sl is not None:
                return src_sl[g]
            return self.x_res[:, g * TOK + off:g * TOK + off + n]

        for g in range(NDG):
            sqo = (g % 2) * n
            def f_s(s, g=g, sqo=sqo):
                self.w(s, "ACT", src_dep, T.get(tag + f"_mmsq{g - 2}"))
                if g == 0:
                    self.w(s, "ACT", self.scr_free)
                T[tag + f"_sq{g}"] = self.mk("ACT", s.activation(
                    self.bc_a[:, sqo:sqo + n], srcap(g), AF.Square))
            self.SC(f_s)

            def f_t(te, g=g, sqo=sqo):
                self.w(te, "TE", T["ones"], T[tag + f"_sq{g}"])
                if g == 0:
                    self.w(te, "TE", self.pslast["A"], self.pslast["C"])
                if not rms:
                    mm = None
                    for tt in range(nt):
                        dst = (self.PS_A[0:1, tt * 512:tt * 512 + 512] if tt < 2
                               else self.PS_B[0:1, (tt - 2) * 512:
                                              (tt - 2) * 512 + 512])
                        mm = te.matmul(dst, self.onescf[:],
                                       srcap(g)[:, tt * 512:tt * 512 + 512],
                                       start=(g == 0), stop=(g == NDG - 1))
                    T[tag + f"_mmx{g}"] = self.mk("TE", mm)
                mm = None
                for tt in range(nt):
                    mm = te.matmul(self.PS_C[0:1, tt * 512:tt * 512 + 512],
                                   self.onesc[:],
                                   self.bc_a[:, sqo + tt * 512:
                                             sqo + tt * 512 + 512],
                                   start=(g == 0), stop=(g == NDG - 1))
                T[tag + f"_mmsq{g}"] = self.mk("TE", mm)
            self.TE(f_t)
        T[tag + "_mmsq"] = T[tag + f"_mmsq{NDG - 1}"]
        if not rms:
            T[tag + "_mmx"] = T[tag + f"_mmx{NDG - 1}"]

        sq_row = self.sq_s[0:1, 0:n]

        if not rms:
            def f_r2(s):
                self.w(s, "ACT", T[tag + "_mmsq"], T[tag + "_mmx"])
                t1 = self.mk("ACT", s.mul(self.RS1[0:1, 0:n],
                                          self.PS_A[0:1, 0:n], 1.0 / D))
                self.w(s, "ACT", t1)
                T[tag + "_mubf"] = self.mk("ACT", s.copy(
                    self.bc_b[0:1, 0:n], self.RS1[0:1, 0:n]))
                T[tag + "_musq"] = self.mk("ACT", s.activation(
                    sq_row, self.RS1[0:1, 0:n], AF.Square))
            self.SC(f_r2)

            def f_t2m(te):
                self.w(te, "TE", T[tag + "_mubf"])
                mm = None
                for tt in range(nt):
                    mm = te.matmul(self.PS_A[:, tt * 512:tt * 512 + 512],
                                   self.onesr[:],
                                   self.bc_b[0:1, tt * 512:tt * 512 + 512],
                                   start=True, stop=True)
                T[tag + "_bcM"] = self.mk("TE", mm)
            self.TE(f_t2m)

            def f_s3m(s):
                self.w(s, "ACT", T[tag + "_bcM"])
                tk = self.mk("ACT", s.copy(self.bc_b[:, 0:n],
                                           self.PS_A[:, 0:n]))
                T[tag + "_bcMe"] = tk
                self.pslast["A"] = tk
                self.pslast["B"] = tk
            self.SC(f_s3m)

            def f_vv(v):
                self.w(v, "DVE", T[tag + "_musq"])
                T[tag + "_var"] = self.mk("DVE", v.scalar_tensor_tensor(
                    self.RS1[0:1, 0:n], self.PS_C[0:1, 0:n], 1.0 / D,
                    sq_row, ALU.mult, ALU.subtract))
            self.V(f_vv)
        else:
            def f_vv(v):
                self.w(v, "DVE", T[tag + "_mmsq"])
                T[tag + "_var"] = self.mk("DVE", v.tensor_scalar(
                    self.RS1[0:1, 0:n], self.PS_C[0:1, 0:n], 1.0 / D,
                    None, ALU.mult))
            self.V(f_vv)

        def f_s2(s):
            self.w(s, "ACT", T[tag + "_var"])
            t1 = self.mk("ACT", s.activation(sq_row, self.RS1[0:1, 0:n],
                                             AF.Ln, bias=float(eps)))
            self.w(s, "ACT", t1)
            T[tag + "_rstd"] = self.mk("ACT", s.activation(
                self.bc_a[0:1, 0:n], sq_row, AF.Exp, scale=-0.5))
        self.SC(f_s2)

        def f_t2(te):
            self.w(te, "TE", T[tag + "_rstd"], self.pslast["C"])
            mm = None
            for tt in range(nt):
                mm = te.matmul(self.PS_C[:, tt * 512:tt * 512 + 512],
                               self.onesr[:],
                               self.bc_a[0:1, tt * 512:tt * 512 + 512],
                               start=True, stop=True)
            T[tag + "_bcR"] = self.mk("TE", mm)
        self.TE(f_t2)

        def f_s3(s):
            self.w(s, "ACT", T[tag + "_bcR"])
            tk = self.mk("ACT", s.copy(self.bc_a[:, 0:n], self.PS_C[:, 0:n]))
            T[tag + "_bc"] = tk
            self.pslast["C"] = tk
        self.SC(f_s3)

        def f_v4(v):
            self.w(v, "DVE", T[tag + "_bc"])
            tk = None
            for g in range(NDG):
                dst = dst_sl[g]
                three_d = len(dst.shape) == 3

                def r3(ap):
                    return ap.rearrange("p (a b) -> p a b", b=dst.shape[2])                         if three_d else ap
                if rms:
                    tk = self.mk("DVE", v.scalar_tensor_tensor(
                        dst, r3(srcap(g)),
                        wcol[:, g:g + 1] if wcol is not None else 1.0,
                        r3(self.bc_a[:, 0:n]), ALU.mult, ALU.mult))
                else:
                    self.w(v, "DVE", tk)
                    t1 = self.mk("DVE", v.tensor_tensor(
                        self.sq_s[:, 0:n], srcap(g), self.bc_b[:, 0:n],
                        ALU.subtract))
                    self.w(v, "DVE", t1)
                    tk = self.mk("DVE", v.tensor_tensor(
                        dst, r3(self.sq_s[:, 0:n]), r3(self.bc_a[:, 0:n]),
                        ALU.mult))
            T[tag + "_done"] = tk
            self.scr_free = tk
        self.V(f_v4)

    def ln_fused(self, tag, off, n, dst_sl, src_dep, wcol, hid_sl=None,
                 eps_ln=LN_EPS, eps_rms=RMS_EPS, src_sl=None):
        T = self.T
        assert n == 512

        def srcap(g):
            if src_sl is not None:
                return src_sl[g]
            return self.x_res[:, g * TOK + off:g * TOK + off + n]

        for g in range(NDG):
            sqo = (g % 2) * n
            def f_s(s, g=g, sqo=sqo):
                self.w(s, "ACT", src_dep, T.get(tag + f"_mmsq{g - 2}"))
                if g == 0:
                    self.w(s, "ACT", self.scr_free)
                T[tag + f"_sq{g}"] = self.mk("ACT", s.activation(
                    self.bc_a[:, sqo:sqo + n], srcap(g), AF.Square))
            self.SC(f_s)

            def f_t(te, g=g, sqo=sqo):
                self.w(te, "TE", T["ones"], T[tag + f"_sq{g}"])
                if g == 0:
                    self.w(te, "TE", self.pslast["A"], self.pslast["C"])
                mm = te.matmul(self.PS_A[0:1, 0:n], self.onescf[:],
                               srcap(g),
                               start=(g == 0), stop=(g == NDG - 1))
                T[tag + f"_mmx{g}"] = self.mk("TE", mm)
                mm = te.matmul(self.PS_C[0:1, 0:n], self.onesc[:],
                               self.bc_a[:, sqo:sqo + n],
                               start=(g == 0), stop=(g == NDG - 1))
                T[tag + f"_mmsq{g}"] = self.mk("TE", mm)
            self.TE(f_t)

        sq_row = self.sq_s[0:1, 0:n]

        def f_r2(s):
            self.w(s, "ACT", T[tag + f"_mmsq{NDG - 1}"],
                   T[tag + f"_mmx{NDG - 1}"])
            t1 = self.mk("ACT", s.mul(self.RS1[0:1, 0:n],
                                      self.PS_A[0:1, 0:n], 1.0 / D))
            self.w(s, "ACT", t1)
            T[tag + "_mubf"] = self.mk("ACT", s.copy(
                self.bc_b[0:1, 0:n], self.RS1[0:1, 0:n]))
            T[tag + "_musq"] = self.mk("ACT", s.activation(
                sq_row, self.RS1[0:1, 0:n], AF.Square))
        self.SC(f_r2)

        def f_t2m(te):
            self.w(te, "TE", T[tag + "_mubf"])
            mm = te.matmul(self.PS_A[:, 0:n], self.onesr[:],
                           self.bc_b[0:1, 0:n], start=True, stop=True)
            T[tag + "_bcM"] = self.mk("TE", mm)
        self.TE(f_t2m)

        def f_s3m(s):
            self.w(s, "ACT", T[tag + "_bcM"])
            tk = self.mk("ACT", s.copy(self.bc_b[:, 0:n], self.PS_A[:, 0:n]))
            T[tag + "_bcMe"] = tk
            self.pslast["A"] = tk
        self.SC(f_s3m)

        def f_vv(v):
            self.w(v, "DVE", T[tag + "_musq"])
            T[tag + "_var"] = self.mk("DVE", v.scalar_tensor_tensor(
                self.RS1[0:1, 0:n], self.PS_C[0:1, 0:n], 1.0 / D,
                sq_row, ALU.mult, ALU.subtract))
        self.V(f_vv)

        def f_s2(s):
            self.w(s, "ACT", T[tag + "_var"])
            t1 = self.mk("ACT", s.activation(
                sq_row, self.RS1[0:1, 0:n], AF.Ln,
                bias=float(eps_ln * eps_rms), scale=float(1.0 + eps_rms)))
            if hid_sl is not None:
                t1 = self.mk("ACT", s.activation(
                    self.sq_s[0:1, n:2 * n], self.RS1[0:1, 0:n], AF.Ln,
                    bias=float(eps_ln)))
            self.w(s, "ACT", t1)
            tk = self.mk("ACT", s.activation(
                self.bc_a[0:1, 0:n], sq_row, AF.Exp, scale=-0.5))
            if hid_sl is not None:
                tk = self.mk("ACT", s.activation(
                    self.bc_a[0:1, n:2 * n], self.sq_s[0:1, n:2 * n],
                    AF.Exp, scale=-0.5))
            T[tag + "_rstd"] = tk
        self.SC(f_s2)

        nr = n if hid_sl is None else 2 * n

        def f_t2(te):
            self.w(te, "TE", T[tag + "_rstd"], self.pslast["C"])
            mm = te.matmul(self.PS_C[:, 0:n], self.onesr[:],
                           self.bc_a[0:1, 0:n], start=True, stop=True)
            if hid_sl is not None:
                mm = te.matmul(self.PS_C[:, n:2 * n], self.onesr[:],
                               self.bc_a[0:1, n:2 * n], start=True, stop=True)
            T[tag + "_bcR"] = self.mk("TE", mm)
        self.TE(f_t2)

        def f_s3(s):
            self.w(s, "ACT", T[tag + "_bcR"])
            tk = self.mk("ACT", s.copy(self.bc_a[:, 0:nr],
                                       self.PS_C[:, 0:nr]))
            T[tag + "_bc"] = tk
            self.pslast["C"] = tk
        self.SC(f_s3)

        def f_v4(v):
            self.w(v, "DVE", T[tag + "_bc"])
            tk = None
            for g in range(NDG):
                self.w(v, "DVE", tk)
                t1 = self.mk("DVE", v.tensor_tensor(
                    self.sq_s[:, 0:n], srcap(g), self.bc_b[:, 0:n],
                    ALU.subtract))
                self.w(v, "DVE", t1)
                tk = self.mk("DVE", v.scalar_tensor_tensor(
                    dst_sl[g], self.sq_s[:, 0:n],
                    wcol[:, g:g + 1] if wcol is not None else 1.0,
                    self.bc_a[:, 0:n], ALU.mult, ALU.mult))
                T[tag + f"_ap{g}"] = tk
                if hid_sl is not None:
                    tk = self.mk("DVE", v.tensor_tensor(
                        hid_sl[g], self.sq_s[:, 0:n],
                        self.bc_a[:, n:2 * n], ALU.mult))
            T[tag + "_done"] = tk
            self.scr_free = tk
        self.V(f_v4)

    def fwd_dft(self, tag, src_ar, src_off, dst_ar, dst_off, nyq_dst, dep_toks):
        T = self.T
        for fh in range(2):
            def f_t(te, fh=fh):
                self.w(te, "TE", T["consts_bf"], self.pslast["A"],
                       self.pslast["B"], *dep_toks)
                for comp in range(2):
                    ps = self.PS_A if comp == 0 else self.PS_B
                    mm = None
                    for mh in range(2):
                        for tt in range(2):
                            mm = te.matmul(
                                ps[:, mh * 512:mh * 512 + 512],
                                self.c_DfT[:, (tt * 4 + comp * 2 + fh) * 128:
                                           (tt * 4 + comp * 2 + fh + 1) * 128],
                                src_ar[:, src_off + tt * 1024 + mh * 512:
                                       src_off + tt * 1024 + mh * 512 + 512],
                                start=(tt == 0), stop=(tt == 1))
                    nm = "re" if comp == 0 else "im"
                    T[tag + f"_mm_{nm}{fh}"] = self.mk("TE", mm)
            self.TE(f_t)

            def f_s(s, fh=fh):
                tk = None
                for comp, nm in [(0, "re"), (1, "im")]:
                    ps = self.PS_A if comp == 0 else self.PS_B
                    self.w(s, "ACT", T[tag + f"_mm_{nm}{fh}"])
                    tk = self.mk("ACT", s.copy(
                        dst_ar[:, dst_off + (comp * 2 + fh) * 1024:
                               dst_off + (comp * 2 + fh + 1) * 1024],
                        ps[:, 0:1024]))
                T[tag + f"_ev{fh}"] = tk
                self.pslast["A"] = tk
                self.pslast["B"] = tk
            self.SC(f_s)

        if nyq_dst is not None:
            def f_tn(te):
                self.w(te, "TE", T[tag + "_ev1"])
                mm = None
                for mh in range(2):
                    for tt in range(2):
                        mm = te.matmul(self.PS_A[0:1, mh * 512:mh * 512 + 512],
                                       self.c_Dnyq[:, tt:tt + 1],
                                       src_ar[:, src_off + tt * 1024 + mh * 512:
                                              src_off + tt * 1024 + mh * 512 + 512],
                                       start=(tt == 0), stop=(tt == 1))
                T[tag + "_mm_nyq"] = self.mk("TE", mm)
            self.TE(f_tn)

            def f_sn(s):
                self.w(s, "ACT", T[tag + "_mm_nyq"])
                tk = self.mk("ACT", s.copy(nyq_dst[0:1, 0:1024],
                                           self.PS_A[0:1, 0:1024]))
                T[tag + "_done"] = tk
                self.pslast["A"] = tk
            self.SC(f_sn)
        else:
            T[tag + "_done"] = T[tag + "_ev1"]

    def e_attn_a(self):
        T = self.T

        def v_pows(v):
            v = _SerialEng(self, v, "DVE")
            self.w(v, "DVE", T["consts_f32"])
            P = self.pows
            t0 = self.mk("DVE", v.memset(P[:, 0:1024], 0.0))
            self.w(v, "DVE", t0)
            self.mk("DVE", v.memset(P[:, 0:1], 1.0))
            self.mk("DVE", v.tensor_copy(self.apow[:, 0:1], self.c_Are[:]))
            tk = self.mk("DVE", v.tensor_copy(self.apow[:, 1:2], self.c_Aim[:]))
            self.w(v, "DVE", tk)
            for k in range(8):
                L = 1 << k
                t1 = self.mk("DVE", v.tensor_scalar(
                    P[:, 512:512 + L], P[:, 0:L], self.apow[:, 0:1], None, ALU.mult))
                t2 = self.mk("DVE", v.tensor_scalar(
                    P[:, 512 + L:512 + 2 * L], P[:, 256:256 + L],
                    self.apow[:, 1:2], None, ALU.mult))
                self.w(v, "DVE", t2)
                self.mk("DVE", v.tensor_tensor(
                    P[:, L:2 * L], P[:, 512:512 + L],
                    P[:, 512 + L:512 + 2 * L], ALU.subtract))
                t1 = self.mk("DVE", v.tensor_scalar(
                    P[:, 512:512 + L], P[:, 0:L], self.apow[:, 1:2], None, ALU.mult))
                t2 = self.mk("DVE", v.tensor_scalar(
                    P[:, 512 + L:512 + 2 * L], P[:, 256:256 + L],
                    self.apow[:, 0:1], None, ALU.mult))
                self.w(v, "DVE", t2)
                tk = self.mk("DVE", v.tensor_tensor(
                    P[:, 256 + L:256 + 2 * L], P[:, 512:512 + L],
                    P[:, 512 + L:512 + 2 * L], ALU.add))
                if k < 7:
                    self.mk("DVE", v.tensor_tensor(
                        self.apow[:, 2:3], self.apow[:, 0:1], self.apow[:, 0:1],
                        ALU.mult))
                    self.mk("DVE", v.tensor_tensor(
                        self.apow[:, 3:4], self.apow[:, 1:2], self.apow[:, 1:2],
                        ALU.mult))
                    t3 = self.mk("DVE", v.tensor_tensor(
                        self.apow[:, 4:5], self.apow[:, 0:1], self.apow[:, 1:2],
                        ALU.mult))
                    self.w(v, "DVE", t3)
                    self.mk("DVE", v.tensor_tensor(
                        self.apow[:, 0:1], self.apow[:, 2:3], self.apow[:, 3:4],
                        ALU.subtract))
                    tk = self.mk("DVE", v.tensor_scalar(
                        self.apow[:, 1:2], self.apow[:, 4:5], 2.0, None, ALU.mult))
                    self.w(v, "DVE", tk)
            self.w(v, "DVE", tk)
            t1 = self.mk("DVE", v.tensor_scalar(
                P[:, 512:768], P[:, 0:256], self.c_Bre[:], None, ALU.mult))
            t2 = self.mk("DVE", v.tensor_scalar(
                P[:, 768:1024], P[:, 256:512], self.c_Bim[:], None, ALU.mult))
            self.w(v, "DVE", t2)
            t3 = self.mk("DVE", v.tensor_tensor(
                P[:, 512:768], P[:, 512:768], P[:, 768:1024], ALU.subtract))
            self.w(v, "DVE", t3)
            self.mk("DVE", v.tensor_copy(self.c_pb[:, 0:256], P[:, 512:768]))
            t1 = self.mk("DVE", v.tensor_scalar(
                P[:, 512:768], P[:, 0:256], self.c_Bim[:], None, ALU.mult))
            t2 = self.mk("DVE", v.tensor_scalar(
                P[:, 768:1024], P[:, 256:512], self.c_Bre[:], None, ALU.mult))
            self.w(v, "DVE", t2)
            t3 = self.mk("DVE", v.tensor_tensor(
                P[:, 512:768], P[:, 512:768], P[:, 768:1024], ALU.add))
            self.w(v, "DVE", t3)
            T["pb"] = self.mk("DVE", v.tensor_scalar(
                self.c_pb[:, 256:512], P[:, 512:768], -1.0, None, ALU.mult))
        self.V(v_pows)

        def t_kr(te):
            self.w(te, "TE", T["pb"], T["consts_bf"],
                   self.pslast["A"], self.pslast["B"])
            mm = None
            for lag in range(2):
                ps = self.PS_A if lag == 0 else self.PS_B
                for mh in range(2):
                    te.matmul(ps[:, mh * 512:mh * 512 + 512],
                              self.c_pb[:, lag * 128:lag * 128 + 128],
                              self.c_CTre[:, mh * 512:mh * 512 + 512],
                              start=True, stop=False)
                    mm = te.matmul(ps[:, mh * 512:mh * 512 + 512],
                                   self.c_pb[:, 256 + lag * 128:256 + lag * 128 + 128],
                                   self.c_CTim[:, mh * 512:mh * 512 + 512],
                                   start=False, stop=True)
            T["kr_mm"] = self.mk("TE", mm)
        self.TE(t_kr)

        def v_kr(v):
            self.w(v, "DVE", T["kr_mm"])
            t1 = self.mk("DVE", v.tensor_single_scalar(
                self.PS_A[0:1, 0:1024], self.PS_A[0:1, 0:1024], 1.0, ALU.add))
            self.w(v, "DVE", t1)
            self.mk("DVE", v.tensor_copy(self.AR2[:, YH:YH + 1024],
                                         self.PS_A[:, 0:1024]))
            tk = self.mk("DVE", v.tensor_copy(
                self.AR2[:, YH + 1024:YH + 2048], self.PS_B[:, 0:1024]))
            T["kr_bf"] = tk
            self.pslast["A"] = tk
            self.pslast["B"] = tk
        self.V(v_kr)

        self.fwd_dft("kh", self.AR2, YH, self.AR2, KH, self.k_nyq, [T["kr_bf"]])
        self.yh_free = T["kh_mm_nyq"]

        xn_view = self.AR1[:, 0:16384].rearrange("p (lt e) -> p lt e", e=1024)
        xn_sl = [xn_view[:, :, g * 128:(g + 1) * 128] for g in range(NDG)]
        def v_ctbar(v):
            self.w(v, "DVE", T["kr_mm"])
        self.V(v_ctbar)
        for hh in range(2):
            dst_h = [xn_view[:, hh * 8:(hh + 1) * 8, g * 128:(g + 1) * 128]
                     for g in range(NDG)]
            self.ln_feat(f"lnA{hh}", hh * 1024, 1024, dst_h, T["x_ld"])
        T["lnA_done"] = T["lnA1_done"]

        for lt in range(NLT):
            def t_win(te, lt=lt):
                self.w(te, "TE", T["WinT_ld"],
                       T["lnA0_done"] if lt < 8 else T["lnA1_done"],
                       self.pslast["A"])
                mm = None
                for eh in range(2):
                    for dt in range(NDG):
                        mm = te.matmul(
                            self.PS_A[:, eh * 512:eh * 512 + 512],
                            self.AR1[:, lt * 1024 + dt * 128:
                                     lt * 1024 + dt * 128 + 128],
                            self.AR2[:, dt * 1024 + eh * 512:
                                     dt * 1024 + eh * 512 + 512],
                            start=(dt == 0), stop=(dt == NDG - 1))
                T[f"win{lt}"] = self.mk("TE", mm)
            self.TE(t_win)

            def v_iln(v, lt=lt):
                self.w(v, "DVE", T[f"win{lt}"], T.get(f"xa{lt - 1}"))
                tk = self.mk("DVE", v.tensor_tensor(
                    self.sc_a[:, 0:1024], self.PS_A[:, 0:1024],
                    self.c_bin[:, 0:1024], ALU.add))
                self.pslast["A"] = tk
                self.w(v, "DVE", tk)
                t1 = self.mk("DVE", v.tensor_reduce(
                    self.colmu[:, 0:1], self.sc_a[:, 0:1024], AX.X, ALU.add))
                self.w(v, "DVE", t1)
                t1 = self.mk("DVE", v.tensor_scalar(
                    self.colmu[:, 0:1], self.colmu[:, 0:1], 1.0 / D, None,
                    ALU.mult))
                self.w(v, "DVE", t1)
                T[f"ymu{lt}"] = self.mk("DVE", v.tensor_scalar(
                    self.sq_s[:, 0:1024], self.sc_a[:, 0:1024],
                    self.colmu[:, 0:1], None, ALU.subtract))
            self.V(v_iln)

            def s_iln(s, lt=lt):
                self.w(s, "ACT", T[f"ymu{lt}"])
                T[f"ysq{lt}"] = self.mk("ACT", s.activation(
                    self.sc_a[:, 0:1024], self.sq_s[:, 0:1024], AF.Square))
            self.SC(s_iln)

            def v_iln2(v, lt=lt):
                self.w(v, "DVE", T[f"ysq{lt}"])
                T[f"yv{lt}"] = self.mk("DVE", v.tensor_reduce(
                    self.colv[:, 0:1], self.sc_a[:, 0:1024], AX.X, ALU.add))
            self.V(v_iln2)

            def s_iln2(s, lt=lt):
                self.w(s, "ACT", T[f"yv{lt}"])
                t1 = self.mk("ACT", s.activation(self.colv[:, 0:1],
                                                 self.colv[:, 0:1],
                                                 AF.Ln, bias=LN_EPS,
                                                 scale=1.0 / D))
                self.w(s, "ACT", t1)
                T[f"yr{lt}"] = self.mk("ACT", s.activation(
                    self.colv[:, 0:1], self.colv[:, 0:1], AF.Exp, scale=-0.5))
            self.SC(s_iln2)

            def v_iln3(v, lt=lt):
                self.w(v, "DVE", T[f"yr{lt}"])
                t1 = self.mk("DVE", v.scalar_tensor_tensor(
                    self.sc_a[:, 0:1024], self.sq_s[:, 0:1024],
                    self.colv[:, 0:1], self.c_gam[:, 0:1024],
                    ALU.mult, ALU.mult))
                self.w(v, "DVE", t1)
                T[f"xa{lt}"] = self.mk("DVE", v.tensor_tensor(
                    self.AR1[:, XA + lt * 1024:XA + (lt + 1) * 1024],
                    self.sc_a[:, 0:1024], self.c_bet[:, 0:1024], ALU.add))
            self.V(v_iln3)
        T["xa_all"] = T[f"xa{NLT - 1}"]

        for c in range(NCH):
            def v_xadc(v, c=c):
                self.w(v, "DVE", T[f"xa{c * 2 + 1}"], T.get(f"cs_mm{c - 1}"))
                t1 = self.mk("DVE", v.tensor_tensor(
                    self.xad_s[:, 0:1024],
                    self.AR1[:, (c * 2) * 1024:(c * 2 + 1) * 1024],
                    self.c_Dbc[:, 0:1024], ALU.mult))
                T[f"xadc{c}"] = self.mk("DVE", v.tensor_tensor(
                    self.xad_s[:, 1024:2048],
                    self.AR1[:, (c * 2 + 1) * 1024:(c * 2 + 2) * 1024],
                    self.c_Dbc[:, 0:1024], ALU.mult))
            self.V(v_xadc)

            def t_csc(te, c=c):
                self.w(te, "TE", T[f"xadc{c}"], T["ones"])
                if c == 0:
                    self.w(te, "TE", self.pslast["B"])
                mm = None
                for mt in range(NDG):
                    for tt in range(2):
                        mm = te.matmul(
                            self.PS_B[:, c * 8 + mt:c * 8 + mt + 1],
                            self.xad_s[:, tt * 1024 + mt * 128:
                                       tt * 1024 + mt * 128 + 128],
                            self.onesc[:], start=(tt == 0), stop=(tt == 1))
                T[f"cs_mm{c}"] = self.mk("TE", mm)
            self.TE(t_csc)
        T["cs_mm"] = T[f"cs_mm{NCH - 1}"]

        def v_cs(v):
            self.w(v, "DVE", T["cs_mm"])
            tk = self.mk("DVE", v.tensor_copy(
                self.cols[:, 0:64].rearrange("p (m c) -> p m c", c=8),
                self.PS_B[:, 0:64].rearrange("p (c m) -> p m c", m=8)))
            self.pslast["B"] = tk
            self.w(v, "DVE", tk)
            self.mk("DVE", v.tensor_reduce(
                self.colsT[:, 0:8],
                self.cols[:, 0:64].rearrange("p (m c) -> p m c", c=8),
                AX.X, ALU.add))
            for mt in range(NDG):
                self.mk("DVE", v.tensor_tensor_scan(
                    self.colsC[:, mt * 8:mt * 8 + 8],
                    self.cols[:, mt * 8:mt * 8 + 8], self.z64[:, 0:8],
                    0.0, ALU.add, ALU.add))
            self.w(v, "DVE", ("DVE", self.cnt["DVE"]))
            t1 = self.mk("DVE", v.tensor_tensor(
                self.colsC[:, 0:64], self.colsC[:, 0:64], self.cols[:, 0:64],
                ALU.subtract))
            self.w(v, "DVE", t1)
            tk2 = self.mk("DVE", v.tensor_copy(self.colsD[:, 0:64],
                                               self.colsC[:, 0:64]))
            T["dc_local"] = tk2
            T["T_col"] = tk2
        self.V(v_cs)

        def po_sendA(g):
            pid = g.partition_id()
            slot = pid % 2
            self.w(g, "POOL", T["xa_all"], T["T_col"])
            t1 = self.mkd("DPO", g.dma_start(
                self.sh_tail[bass.ts(slot, 1), :, :].rearrange("a p c -> p (a c)"),
                self.AR1[:, XA + 14 * 1024:XA + 16 * 1024]))
            t2 = self.mkd("DPO", g.dma_start(
                self.sh_T[bass.ts(slot, 1), :, :].rearrange("a p c -> p (a c)"),
                self.colsT[:, 0:8]))
            self.w(g, "POOL", t1, t2)
            self.mkd("DPO", g.dma_start(
                self.sh_flagA[bass.ts(slot, 1), :], self.flag_src[:]))
        self.PO(po_sendA)

        bufof = {0: 2}
        for c in range(1, NCH):
            bufof[c] = (c - 1) % 2

        self.fwd_dft("fd0", self.AR1, XA + 0, self.AR2, XBUF[2], None,
                     [T["xa0"], T["xa1"]])
        for c in range(1, NCH):
            deps = [T[f"xa{c * 2}"], T[f"xa{c * 2 + 1}"]]
            if c >= 3:
                deps.append(T[f"pw{c - 1}_done"])
            self.fwd_dft(f"fd{c}", self.AR1, XA + c * 2048, self.AR2,
                         XBUF[bufof[c]], None, deps)
            self.pw_inv_evac(c, XBUF[bufof[c]], XBUF[bufof[c - 1]],
                             f"fd{c}", f"fd{c - 1}",
                             (self.AR1, XA + c * 2048),
                             (self.AR1, XA + (c - 1) * 2048))

        def po_recvA(g):
            pid = g.partition_id()
            peer = (pid + 1) % 2
            self.w(g, "POOL", T[f"win{NLT - 1}"], T["consts_f32"])
            with g.register("cnt_a") as r_cnt:
                g.reg_mov(r_cnt, 0)

                def poll():
                    g.dma_start(self.flag_rd[:],
                                self.sh_flagA[bass.ts(peer, 1), :]).then_inc(
                        self.psem, 16)
                    g.reg_add(r_cnt, r_cnt, 16)
                    g.wait_ge(self.psem, r_cnt)

                poll()
                with g.While(lambda: g.value_load(self.flag_rd[0:1, 0:1])
                             != g.value_load(self.flag_src[0:1, 0:1])):
                    poll()
            t1 = self.mkd("DPO", g.dma_start(
                self.AR2[:, 0:2048],
                self.sh_tail[bass.ts(peer, 1), :, :].rearrange("a p c -> p (a c)")))
            t2 = self.mkd("DPO", g.dma_start(
                self.tpeer[:, 0:8],
                self.sh_T[bass.ts(peer, 1), :, :].rearrange("a p c -> p (a c)")))
            self.w(g, "POOL", t1, t2)
            T["peerA"] = self.mk("POOL", g.engine_nop())
        self.PO(po_recvA)

        def v_peer(v):
            self.w(v, "DVE", T["peerA"])
            T["ptail"] = self.mk("DVE", v.tensor_scalar(
                self.AR2[:, 0:2048], self.AR2[:, 0:2048],
                self.c_pmask[:, 0:1], None, ALU.mult))
            T["pT"] = self.mk("DVE", v.tensor_scalar(
                self.tpeer[:, 0:8], self.tpeer[:, 0:8],
                self.c_pmask[:, 0:1], None, ALU.mult))
        self.V(v_peer)

        self.fwd_dft("fdp", self.AR2, 0, self.AR2, XBUF[0], None,
                     [T["ptail"], T[f"pw{NCH - 1}_done"]])
        self.pw_inv_evac(0, XBUF[2], XBUF[0], "fd0", "fdp",
                         (self.AR1, XA + 0), (self.AR2, 0))

        def v_pdc(v):
            self.w(v, "DVE", T["pT"], T["evac0_done"])
            t1 = self.mk("DVE", v.tensor_copy(self.pcol[:, 0:8],
                                              self.tpeer[:, 0:8]))
            self.w(v, "DVE", t1)
            tk = None
            for mt in range(NDG):
                tk = self.mk("DVE", v.tensor_scalar(
                    self.x_res[:, mt * TOK:(mt + 1) * TOK],
                    self.x_res[:, mt * TOK:(mt + 1) * TOK],
                    self.pcol[:, mt:mt + 1], None, ALU.add))
            T["attnA_done"] = tk
        self.V(v_pdc)

    def pw_inv_evac(self, c, cur_off, prev_off, cur_tag, prev_tag,
                    src_cur, src_prev):
        T = self.T

        def f_tn(te, c=c):
            self.w(te, "TE", T[cur_tag + "_done"], T[prev_tag + "_done"],
                   self.pslast["A"])
            mm = None
            for mh in range(2):
                parts = [(ar, off, tt) for (ar, off) in (src_prev, src_cur)
                         for tt in range(2)]
                for i, (ar, off, tt) in enumerate(parts):
                    mm = te.matmul(
                        self.PS_A[0:1, mh * 512:mh * 512 + 512],
                        self.c_Dnyq[:, tt:tt + 1],
                        ar[:, off + tt * 1024 + mh * 512:
                           off + tt * 1024 + mh * 512 + 512],
                        start=(i == 0), stop=(i == 3))
            T[f"nyqmm{c}"] = self.mk("TE", mm)
        self.TE(f_tn)

        def f_sn(s, c=c):
            self.w(s, "ACT", T[f"nyqmm{c}"])
            tk = self.mk("ACT", s.copy(self.y_nyq, self.PS_A[0:1, 0:1024]))
            T[f"nyqev{c}"] = tk
            self.pslast["A"] = tk
        self.SC(f_sn)

        def f_pw(v, c=c):
            self.w(v, "DVE", T[cur_tag + "_done"], T[prev_tag + "_done"],
                   self.yh_free)
            self.mk("DVE", v.tensor_tensor(
                self.xad_s[:, 0:1024],
                self.AR1[:, (c * 2) * 1024:(c * 2 + 1) * 1024],
                self.c_Dbc[:, 0:1024], ALU.mult))
            self.mk("DVE", v.tensor_tensor(
                self.xad_s[:, 1024:2048],
                self.AR1[:, (c * 2 + 1) * 1024:(c * 2 + 2) * 1024],
                self.c_Dbc[:, 0:1024], ALU.mult))
            sttk = []
            for fh in range(2):
                sttk.append(self.mk("DVE", v.scalar_tensor_tensor(
                    self.bc_a[:, fh * 1024:(fh + 1) * 1024],
                    self.AR2[:, cur_off + (0 * 2 + fh) * 1024:
                             cur_off + (0 * 2 + fh + 1) * 1024],
                    self.c_sgnf[:, 0:1],
                    self.AR2[:, prev_off + (0 * 2 + fh) * 1024:
                             prev_off + (0 * 2 + fh + 1) * 1024],
                    ALU.mult, ALU.add)))
                sttk.append(self.mk("DVE", v.scalar_tensor_tensor(
                    self.bc_b[:, fh * 1024:(fh + 1) * 1024],
                    self.AR2[:, cur_off + (1 * 2 + fh) * 1024:
                             cur_off + (1 * 2 + fh + 1) * 1024],
                    self.c_sgnf[:, 0:1],
                    self.AR2[:, prev_off + (1 * 2 + fh) * 1024:
                             prev_off + (1 * 2 + fh + 1) * 1024],
                    ALU.mult, ALU.add)))
            self.w(v, "DVE", sttk[-1])
            tadd = None
            for fh in range(2):
                xr = self.bc_a[:, fh * 1024:(fh + 1) * 1024]
                xi = self.bc_b[:, fh * 1024:(fh + 1) * 1024]
                kr = self.AR2[:, KH + (0 * 2 + fh) * 1024:KH + (0 * 2 + fh + 1) * 1024]
                ki = self.AR2[:, KH + (1 * 2 + fh) * 1024:KH + (1 * 2 + fh + 1) * 1024]
                yr = self.AR2[:, YH + (0 * 2 + fh) * 1024:YH + (0 * 2 + fh + 1) * 1024]
                yi = self.AR2[:, YH + (1 * 2 + fh) * 1024:YH + (1 * 2 + fh + 1) * 1024]
                t1 = self.mk("DVE", v.tensor_tensor(yr, xr, kr, ALU.mult))
                self.w(v, "DVE", tadd)
                t2 = self.mk("DVE", v.tensor_tensor(self.pw_t[:, 0:1024], xi, ki,
                                                    ALU.mult))
                self.w(v, "DVE", t2)
                ts = self.mk("DVE", v.tensor_tensor(yr, yr, self.pw_t[:, 0:1024],
                                                    ALU.subtract))
                t1 = self.mk("DVE", v.tensor_tensor(yi, xr, ki, ALU.mult))
                self.w(v, "DVE", ts)
                t2 = self.mk("DVE", v.tensor_tensor(self.pw_t[:, 0:1024], xi, kr,
                                                    ALU.mult))
                self.w(v, "DVE", t2)
                tadd = self.mk("DVE", v.tensor_tensor(yi, yi, self.pw_t[:, 0:1024],
                                                      ALU.add))
            self.w(v, "DVE", T[f"nyqev{c}"])
            T[f"pw{c}_done"] = self.mk("DVE", v.tensor_tensor(
                self.y_nyq, self.y_nyq, self.k_nyq, ALU.mult))
        self.V(f_pw)

        def f_inv(te, c=c):
            self.w(te, "TE", T[f"pw{c}_done"], self.pslast["C"])
            mm = None
            for mt in range(NDG):
                first = True
                for comp in range(2):
                    for fh in range(2):
                        mm = te.matmul(
                            self.PS_C[:, mt * 256:mt * 256 + 256],
                            self.AR2[:, YH + (comp * 2 + fh) * 1024 + mt * 128:
                                     YH + (comp * 2 + fh) * 1024 + mt * 128 + 128],
                            self.c_Einv[:, (comp * 2 + fh) * 256:
                                        (comp * 2 + fh + 1) * 256],
                            start=first, stop=False)
                        first = False
                mm = te.matmul(self.PS_C[:, mt * 256:mt * 256 + 256],
                               self.nyqrow[0:1, mt * 128:mt * 128 + 128],
                               self.c_Enyq[0:1, 0:256], start=False, stop=False)
                mm = te.matmul(self.PS_C[:, mt * 256:mt * 256 + 256],
                               self.xad_s[:, mt * 128:mt * 128 + 128],
                               self.c_utri_a[:, 0:256], start=False, stop=False)
                mm = te.matmul(self.PS_C[:, mt * 256 + 128:mt * 256 + 256],
                               self.xad_s[:, 1024 + mt * 128:1024 + mt * 128 + 128],
                               self.c_utri_b[:, 0:128], start=False, stop=True)
            T[f"inv{c}_mm"] = self.mk("TE", mm)
            self.yh_free = T[f"inv{c}_mm"]
        self.TE(f_inv)

        def f_ev(v, c=c):
            self.w(v, "DVE", T[f"inv{c}_mm"], T["dc_local"])
            tk = None
            for mt in range(NDG):
                tk = self.mk("DVE", v.scalar_tensor_tensor(
                    self.x_res[:, mt * TOK + c * 256:mt * TOK + c * 256 + 256],
                    self.PS_C[:, mt * 256:mt * 256 + 256],
                    self.colsD[:, mt * 8 + c:mt * 8 + c + 1],
                    self.x_res[:, mt * TOK + c * 256:mt * TOK + c * 256 + 256],
                    ALU.add, ALU.add))
            T[f"evac{c}_done"] = tk
            self.pslast["C"] = tk
        self.V(f_ev)

    def e_ffn(self, stage):
        T = self.T
        t = self.t
        tag = "ffn" + stage
        if stage == "E":
            def f_tbl(s):
                self.act_table(s, 6)
            self.SC(f_tbl)
        if stage == "B":
            pred = [T["attnA_done"], T["inv0_mm"], T["fdp_done"],
                    T["pw0_done"], T["evac0_done"]]
        else:
            pred = [T["attnB_done"]]

        def load_quarter(g, q):
            jq = q % 4
            buf = (q % 2) * 8192
            T[tag + f"_w1q{q}"] = self.dma_gw(
                "DPO", g, self.AR1[:, buf:buf + 8192],
                t["w1T"][:, jq * 1024:(jq + 1) * 1024], NDG)

        def f_po0(g):
            self.w(g, "POOL", *pred)
            load_quarter(g, 0)
            load_quarter(g, 1)
            T[tag + "_w2"] = self.dma_gw("DPO", g, self.AR2q[:, 0:32768],
                                         t["w2T"][:, :], 32)
        self.PO(f_po0)

        if stage == "B":
            def f_bq_s(s):
                self.w(s, "ACT", T["consts_f32"])
                T["bq16"] = self.mk("ACT", s.copy(self.c_bq16[:], self.c_b2c[:]))
            self.SC(f_bq_s)

            def f_bq_v(v):
                self.w(v, "DVE", T["bq16"])
                T["bqf"] = self.mk("DVE", v.tensor_copy(self.c_bq[:], self.c_bq16[:]))
            self.V(f_bq_v)

        tmp16 = self.sq_s.bitcast(F16)[:, 0:512]
        last = None
        for q in range(16):
            lb, jq = q // 4, q % 4
            if jq == 0:
                xnb_sl = [self.AR2[:, 24576 + dt * 512:24576 + (dt + 1) * 512]
                          for dt in range(NDG)]
                dep = pred[0] if lb == 0 else T[tag + f"_x{lb - 1}"]
                self.ln_feat(tag + f"ln{lb}", lb * 512, 512, xnb_sl, dep)
            buf = (q % 2) * 8192
            for jj in range(8):
                jt = jq * 8 + jj
                def f_t1(te, jt=jt, jj=jj, lb=lb, q=q, buf=buf):
                    self.w(te, "TE", T[tag + f"_w1q{q}"], T[tag + f"ln{lb}_done"],
                           T.get(tag + f"_h1f{lb}_{jt - 2}"))
                    if jt == 0:
                        self.w(te, "TE", self.pslast["A"])
                    mm = None
                    for dt in range(NDG):
                        mm = te.matmul(
                            self.PS_A[:, (jt % 2) * 512:(jt % 2) * 512 + 512],
                            self.AR1[:, buf + dt * 1024 + jj * 128:
                                     buf + dt * 1024 + jj * 128 + 128],
                            self.AR2[:, 24576 + dt * 512:24576 + (dt + 1) * 512],
                            start=(dt == 0), stop=(dt == NDG - 1))
                    T[tag + f"_h1{lb}_{jt}"] = self.mk("TE", mm)
                self.TE(f_t1)

                if jt % 2 == 0:
                    def f_e1(s, jt=jt, lb=lb):
                        self.w(s, "ACT", T[tag + f"_h1{lb}_{jt}"])
                        T[tag + f"_h1f{lb}_{jt}"] = self.mk("ACT", s.activation(
                            self.AR2q[:, 32768 + jt * 512:32768 + (jt + 1) * 512],
                            self.PS_A[:, (jt % 2) * 512:(jt % 2) * 512 + 512],
                            AF.Relu, bias=self.c_b1c[:, jt:jt + 1]))
                    self.SC(f_e1)
                else:
                    def f_e1(v, jt=jt, lb=lb):
                        self.w(v, "DVE", T[tag + f"_h1{lb}_{jt}"])
                        T[tag + f"_h1f{lb}_{jt}"] = self.mk("DVE", v.tensor_scalar(
                            self.AR2q[:, 32768 + jt * 512:32768 + (jt + 1) * 512],
                            self.PS_A[:, (jt % 2) * 512:(jt % 2) * 512 + 512],
                            self.c_b1c[:, jt:jt + 1], 0.0, ALU.add, ALU.max))
                    self.V(f_e1)
            if q + 2 < 16:
                def f_poq(g, q=q, lb=lb, jq=jq):
                    self.w(g, "POOL", T[tag + f"_h1{lb}_{jq * 8 + 7}"])
                    load_quarter(g, q + 2)
                self.PO(f_poq)

            if jq == 3:
                hq_done = [T[tag + f"_h1f{lb}_{jt}"] for jt in range(30, 32)]
                for dt in range(NDG):
                    def f_t2(te, dt=dt, lb=lb):
                        self.w(te, "TE", T[tag + "_w2"], *hq_done)
                        self.w(te, "TE", T.get(tag + f"_o16_{lb}_{dt - 2}"))
                        if dt == 0:
                            self.w(te, "TE", self.pslast["B"])
                        mm = None
                        for j in range(16):
                            mm = te.matmul(
                                self.PS_B[:, (dt % 2) * 512:(dt % 2) * 512 + 512],
                                self.AR2q[:, j * 2048:(j + 1) * 2048].rearrange(
                                    "q (r c) -> q r c", r=2)[:, :, dt * 128:
                                                             (dt + 1) * 128],
                                self.AR2q[:, 32768 + j * 1024:
                                          32768 + (j + 1) * 1024].rearrange(
                                    "q (r c) -> q r c", r=2),
                                start=(j == 0), stop=(j == 15),
                                perf_mode=mybir.MatmulPerfMode.DoubleRow)
                        T[tag + f"_o{lb}_{dt}"] = self.mk("TE", mm)
                    self.TE(f_t2)

                    def f_e2(v, dt=dt, lb=lb):
                        self.w(v, "DVE", T[tag + f"_o{lb}_{dt}"], T.get("bqf"),
                               T.get(tag + f"_xr{lb}_{dt - 1}") if dt else
                               T.get(tag + f"_xr{lb - 1}_{NDG - 1}"))
                        T[tag + f"_o16_{lb}_{dt}"] = self.mk("DVE", v.tensor_scalar(
                            tmp16,
                            self.PS_B[:, (dt % 2) * 512:(dt % 2) * 512 + 512],
                            self.c_bq[:, dt:dt + 1], None, ALU.add))
                    self.V(f_e2)

                    def f_e3(v, dt=dt, lb=lb):
                        self.w(v, "DVE", T[tag + f"_o16_{lb}_{dt}"])
                        T[tag + f"_xr{lb}_{dt}"] = self.mk("DVE", v.tensor_tensor(
                            self.x_res[:, dt * TOK + lb * 512:
                                       dt * TOK + lb * 512 + 512],
                            self.x_res[:, dt * TOK + lb * 512:
                                       dt * TOK + lb * 512 + 512],
                            tmp16, ALU.add))
                    self.V(f_e3)
                T[tag + f"_x{lb}"] = T[tag + f"_xr{lb}_{NDG - 1}"]
                last = T[tag + f"_x{lb}"]
        T[tag + "_done"] = last

    def e_attn_b(self):
        T = self.T
        t = self.t
        A2, A2f, A1 = self.AR2, self.AR2f, self.AR1
        HB, QT, KT, KTM, VTM = 0, 4096, 6144, 8192, 10240
        GLF, B32, OB, QEB, EM = 14336, 16384, 20480, 24576, 26624
        pred = T["ffnB_done"]

        def f_w1(g):
            self.w(g, "POOL", pred)
            self.dma_gw("DPO", g, A1[:, 0:4096], t["Wq"][:, :], NDG)
            self.dma_gw("DPO", g, A1[:, 4096:8192], t["Wk"][:, :], NDG)
            self.dma_gw("DPO", g, A1[:, 8192:16384], t["Wv"][:, :], NDG)
            self.dma_gw("DPO", g, self.wgk1[:, 0:128], t["Wgk1"][:, :], NDG)
            tk = self.mkd("DPO", g.dma_start(self.wgk2[:], t["Wgk2"][:]))
            T["wC1"] = tk
        self.PO(f_w1)

        def f_init(v):
            self.w(v, "DVE", T["consts_f32"])
            self.mk("DVE", v.memset(self.S_f32[:], 0.0))
            self.mk("DVE", v.memset(self.S_bf[:], 0.0))
            T["Sinit"] = self.mk("DVE", v.memset(self.pdh[:], 1.0))
        self.V(f_init)

        for lb in range(4):
            base = lb * 512
            xb_sl = [A2[:, HB + dt * 512:HB + (dt + 1) * 512] for dt in range(NDG)]
            dep = pred if lb == 0 else T[f"c1x{lb - 1}"]
            self.ln_fused(f"rmC{lb}", base, 512, xb_sl, dep, self.c_anw)
            htok = T[f"rmC{lb}_done"]

            def proj_fm(name, wt_off, n_ot, dst_off, ps, wstride, psname):
                for ot in range(n_ot):
                    def f_t(te, ot=ot, wt_off=wt_off, ps=ps, name=name,
                            wstride=wstride, psname=psname):
                        self.w(te, "TE", T["wC1"],
                               T.get(f"{name}{lb}_e{ot - 2}"))
                        if ot < 2:
                            self.w(te, "TE", self.pslast[psname])
                        mm = None
                        for dt in range(NDG):
                            self.w(te, "TE", T[f"rmC{lb}_ap{dt}"])
                            mm = te.matmul(
                                ps[:, (ot % 2) * 512:(ot % 2) * 512 + 512],
                                A1[:, wt_off + dt * wstride + ot * 128:
                                   wt_off + dt * wstride + ot * 128 + 128],
                                A2[:, HB + dt * 512:HB + (dt + 1) * 512],
                                start=(dt == 0), stop=(dt == NDG - 1))
                        T[f"{name}{lb}_m{ot}"] = self.mk("TE", mm)
                    self.TE(f_t)

                    def f_e(v, ot=ot, dst_off=dst_off, ps=ps, name=name):
                        self.w(v, "DVE", T[f"{name}{lb}_m{ot}"])
                        T[f"{name}{lb}_e{ot}"] = self.mk("DVE", v.tensor_copy(
                            A2[:, dst_off + ot * 512:dst_off + (ot + 1) * 512],
                            ps[:, (ot % 2) * 512:(ot % 2) * 512 + 512]))
                    self.V(f_e)
            proj_fm("q", 0, 4, QT, self.PS_A, 512, "A")
            self.pslast["A"] = T[f"q{lb}_e3"]
            proj_fm("k", 4096, 4, KT, self.PS_A, 512, "A")
            self.pslast["A"] = T[f"k{lb}_e3"]

            for lt in range(4):
                def f_t(te, lt=lt):
                    self.w(te, "TE", T["wC1"], htok, T.get(f"kt{lb}_e{lt - 2}"))
                    if lt < 2:
                        self.w(te, "TE", self.pslast["A"])
                    mm = None
                    for dt in range(NDG):
                        mm = te.matmul(
                            self.PS_A[:, (lt % 2) * 512:(lt % 2) * 512 + 512],
                            A2[:, HB + dt * 512 + lt * 128:
                               HB + dt * 512 + lt * 128 + 128],
                            A1[:, 4096 + dt * 512:4096 + (dt + 1) * 512],
                            start=(dt == 0), stop=(dt == NDG - 1))
                    T[f"kt{lb}_m{lt}"] = self.mk("TE", mm)
                self.TE(f_t)

                def f_e(v, lt=lt):
                    self.w(v, "DVE", T[f"kt{lb}_m{lt}"])
                    T[f"kt{lb}_e{lt}"] = self.mk("DVE", v.tensor_copy(
                        A2[:, KTM + lt * 512:KTM + (lt + 1) * 512],
                        self.PS_A[:, (lt % 2) * 512:(lt % 2) * 512 + 512]))
                self.V(f_e)
            self.pslast["A"] = T[f"kt{lb}_e3"]
            for lt in range(4):
                for eh in range(2):
                    i = lt * 2 + eh

                    def f_t(te, lt=lt, eh=eh, i=i):
                        self.w(te, "TE", T["wC1"], htok,
                               T.get(f"vt{lb}_e{i - 2}"))
                        if i < 2:
                            self.w(te, "TE", self.pslast["B"])
                        mm = None
                        for dt in range(NDG):
                            mm = te.matmul(
                                self.PS_B[:, (i % 2) * 512:(i % 2) * 512 + 512],
                                A2[:, HB + dt * 512 + lt * 128:
                                   HB + dt * 512 + lt * 128 + 128],
                                A1[:, 8192 + dt * 1024 + eh * 512:
                                   8192 + dt * 1024 + (eh + 1) * 512],
                                start=(dt == 0), stop=(dt == NDG - 1))
                        T[f"vt{lb}_m{i}"] = self.mk("TE", mm)
                    self.TE(f_t)

                    def f_e(v, lt=lt, eh=eh, i=i):
                        self.w(v, "DVE", T[f"vt{lb}_m{i}"])
                        T[f"vt{lb}_e{i}"] = self.mk("DVE", v.tensor_copy(
                            A2[:, VTM + lt * 1024 + eh * 512:
                               VTM + lt * 1024 + (eh + 1) * 512],
                            self.PS_B[:, (i % 2) * 512:(i % 2) * 512 + 512]))
                    self.V(f_e)

            self.pslast["B"] = T[f"vt{lb}_e7"]
            def f_g1(te, lb=lb):
                self.w(te, "TE", T["wC1"], htok, self.pslast["B"])
                mm = None
                for dt in range(NDG):
                    mm = te.matmul(self.PS_B[0:16, 0:512],
                                   self.wgk1[:, dt * 16:(dt + 1) * 16],
                                   A2[:, HB + dt * 512:HB + (dt + 1) * 512],
                                   start=(dt == 0), stop=(dt == NDG - 1))
                T[f"g1m{lb}"] = self.mk("TE", mm)
            self.TE(f_g1)

            def f_g1e(v, lb=lb):
                self.w(v, "DVE", T[f"g1m{lb}"])
                T[f"g1e{lb}"] = self.mk("DVE", v.tensor_copy(
                    self.g1_bf[:], self.PS_B[0:16, 0:512]))
                self.pslast["B"] = T[f"g1e{lb}"]
            self.V(f_g1e)

            for ot in range(4):
                def f_t(te, ot=ot, lb=lb):
                    self.w(te, "TE", T[f"g1e{lb}"], T.get(f"zfd{lb}_{ot - 2}"))
                    if ot < 2:
                        self.w(te, "TE", self.pslast["A"])
                    mm = te.matmul(self.PS_A[:, (ot % 2) * 512:(ot % 2) * 512 + 512],
                                   self.wgk2[:, ot * 128:(ot + 1) * 128],
                                   self.g1_bf[:], start=True, stop=True)
                    T[f"zfm{lb}_{ot}"] = self.mk("TE", mm)
                self.TE(f_t)

                def f_ls(v, ot=ot, lb=lb):
                    self.w(v, "DVE", T[f"zfm{lb}_{ot}"],
                           T.get(f"glf{lb}_{ot - 1}"))
                    s = self.sq_s
                    t1 = self.mk("DVE", v.tensor_scalar(
                        s[:, 0:512],
                        self.PS_A[:, (ot % 2) * 512:(ot % 2) * 512 + 512],
                        self.c_bgk2c[:, ot:ot + 1], None, ALU.add))
                    T[f"zfd{lb}_{ot}"] = t1
                    self.w(v, "DVE", t1)
                    T[f"zfmin{lb}_{ot}"] = self.mk("DVE", v.tensor_scalar(
                        s[:, 1536:2048], s[:, 0:512], 0.0, None, ALU.min))
                self.V(f_ls)

                def f_ls2(sc, ot=ot, lb=lb):
                    self.w(sc, "ACT", T[f"zfd{lb}_{ot}"])
                    t0 = self.mk("ACT", sc.activation(
                        self.sq_s[:, 512:1024], self.sq_s[:, 0:512], AF.Abs))
                    self.w(sc, "ACT", t0)
                    t1 = self.mk("ACT", sc.activation(
                        self.sq_s[:, 1024:1536], self.sq_s[:, 512:1024],
                        AF.Exp, scale=-1.0))
                    self.w(sc, "ACT", t1)
                    T[f"zfe{lb}_{ot}"] = self.mk("ACT", sc.activation(
                        self.sq_s[:, 512:1024], self.sq_s[:, 1024:1536],
                        AF.Ln, bias=1.0))
                self.SC(f_ls2)

                def f_ls3(v, ot=ot, lb=lb):
                    self.w(v, "DVE", T[f"zfe{lb}_{ot}"],
                           T[f"zfmin{lb}_{ot}"])
                    T[f"glf{lb}_{ot}"] = self.mk("DVE", v.tensor_tensor(
                        A2[:, GLF + ot * 512:GLF + (ot + 1) * 512],
                        self.sq_s[:, 1536:2048], self.sq_s[:, 512:1024],
                        ALU.subtract))
                self.V(f_ls3)

            for lt in range(4):
                def f_t(te, lt=lt, lb=lb):
                    self.w(te, "TE", T[f"g1e{lb}"], T.get(f"ztd{lb}_{lt - 2}"))
                    if lt < 2:
                        self.w(te, "TE", self.pslast["B"])
                    mm = te.matmul(self.PS_B[:, (lt % 2) * 512:(lt % 2) * 512 + 512],
                                   self.g1_bf[0:16, lt * 128:(lt + 1) * 128],
                                   self.wgk2[:], start=True, stop=True)
                    T[f"ztm{lb}_{lt}"] = self.mk("TE", mm)
                self.TE(f_t)

                def f_ls(v, lt=lt, lb=lb):
                    self.w(v, "DVE", T[f"ztm{lb}_{lt}"],
                           T.get(f"glt{lb}_{lt - 1}"))
                    s = self.sq_s
                    gt = A2f[:, 8192 + lt * 512:8192 + (lt + 1) * 512]
                    t1 = self.mk("DVE", v.tensor_tensor(
                        gt, self.PS_B[:, (lt % 2) * 512:(lt % 2) * 512 + 512],
                        self.c_bgk2bc[:, 0:512], ALU.add))
                    T[f"ztd{lb}_{lt}"] = t1
                    self.w(v, "DVE", t1)
                    T[f"ztmin{lb}_{lt}"] = self.mk("DVE", v.tensor_scalar(
                        s[:, 1024:1536], gt, 0.0, None, ALU.min))
                self.V(f_ls)

                def f_ls2(sc, lt=lt, lb=lb):
                    self.w(sc, "ACT", T[f"ztd{lb}_{lt}"])
                    gt = A2f[:, 8192 + lt * 512:8192 + (lt + 1) * 512]
                    t0 = self.mk("ACT", sc.activation(
                        self.sq_s[:, 0:512], gt, AF.Abs))
                    self.w(sc, "ACT", t0)
                    t1 = self.mk("ACT", sc.activation(
                        self.sq_s[:, 512:1024], self.sq_s[:, 0:512],
                        AF.Exp, scale=-1.0))
                    self.w(sc, "ACT", t1)
                    T[f"zte{lb}_{lt}"] = self.mk("ACT", sc.activation(
                        self.sq_s[:, 0:512], self.sq_s[:, 512:1024],
                        AF.Ln, bias=1.0))
                self.SC(f_ls2)

                def f_ls3(v, lt=lt, lb=lb):
                    self.w(v, "DVE", T[f"zte{lb}_{lt}"],
                           T[f"ztmin{lb}_{lt}"])
                    gt = A2f[:, 8192 + lt * 512:8192 + (lt + 1) * 512]
                    T[f"glt{lb}_{lt}"] = self.mk("DVE", v.tensor_tensor(
                        gt, self.sq_s[:, 1024:1536], self.sq_s[:, 0:512],
                        ALU.subtract))
                self.V(f_ls3)

            self.pslast["A"] = T[f"zfd{lb}_3"]
            self.pslast["B"] = T[f"ztd{lb}_3"]
            def f_bl(v, lb=lb):
                self.w(v, "DVE", *[T[f"glf{lb}_{h}"] for h in range(4)])
                tk = None
                for h in range(4):
                    tk = self.mk("DVE", v.tensor_reduce(
                        self.cols[:, h * 8:h * 8 + 8],
                        A2[:, GLF + h * 512:GLF + (h + 1) * 512].rearrange(
                            "p (c w) -> p c w", w=64), AX.X, ALU.add))
                T[f"blr{lb}"] = tk
            self.V(f_bl)

            def f_ble(sc, lb=lb):
                self.w(sc, "ACT", T[f"blr{lb}"])
                T[f"ebl{lb}"] = self.mk("ACT", sc.activation(
                    self.eblh[:, 0:32], self.cols[:, 0:32], AF.Exp,
                    scale=1.0 / 16.0))
            self.SC(f_ble)

            def f_btm(te, lb=lb):
                self.w(te, "TE", T["consts_f32"], self.pslast["A"],
                       self.pslast["B"],
                       *[T[f"glt{lb}_{lt}"] for lt in range(4)])
                for lt in range(4):
                    ps = self.PS_A if lt < 2 else self.PS_B
                    mm = te.matmul(ps[:, (lt % 2) * 512:(lt % 2) * 512 + 512],
                                   self.c_tril16[:],
                                   A2f[:, 8192 + lt * 512:8192 + (lt + 1) * 512],
                                   start=True, stop=True)
                    T[f"btm{lb}_{lt}"] = self.mk("TE", mm)
            self.TE(f_btm)

            def f_emt(sc, lb=lb):
                self.w(sc, "ACT", T[f"btm{lb}_3"])
                for lt in range(4):
                    ps = self.PS_A if lt < 2 else self.PS_B
                    T[f"emt{lb}_{lt}"] = self.mk("ACT", sc.activation(
                        A2[:, B32 + 2048 + lt * 512:B32 + 2048 + (lt + 1) * 512],
                        ps[:, (lt % 2) * 512:(lt % 2) * 512 + 512],
                        AF.Exp, scale=-1.0))
            self.SC(f_emt)

            self.pslast["A"] = T[f"emt{lb}_1"]
            self.pslast["B"] = T[f"emt{lb}_3"]
            def f_scan(v, lb=lb):
                self.w(v, "DVE", *[T[f"glf{lb}_{h}"] for h in range(4)])
                self.w(v, "DVE", T[f"glt{lb}_3"], T[f"glf{lb}_3"])
                tk = None
                for h in range(4):
                    for cc in range(8):
                        o = h * 512 + cc * 64
                        tk = self.mk("DVE", v.tensor_tensor_scan(
                            self.sq_s[:, o:o + 64],
                            A2[:, GLF + o:GLF + o + 64],
                            self.z64[:, 0:64], 0.0, ALU.add, ALU.add))
                T[f"bscan{lb}"] = tk
            self.V(f_scan)

            def f_exp(sc, lb=lb):
                self.w(sc, "ACT", T[f"bscan{lb}"], T[f"ebl{lb}"])
                t1 = self.mk("ACT", sc.activation(
                    A2[:, GLF:GLF + 2048], self.sq_s[:, 0:2048],
                    AF.Exp, scale=1.0 / 16.0))
                T[f"eb{lb}"] = t1
                T[f"em{lb}"] = self.mk("ACT", sc.activation(
                    A2[:, EM:EM + 2048], self.sq_s[:, 0:2048],
                    AF.Exp, scale=-1.0 / 16.0))
            self.SC(f_exp)

            def f_qe(v, lb=lb):
                self.w(v, "DVE", T[f"eb{lb}"], T[f"em{lb}"],
                       *[T[f"emt{lb}_{lt}"] for lt in range(4)])
                t1 = self.mk("DVE", v.scalar_tensor_tensor(
                    A2[:, QT:QT + 2048], A2[:, QT:QT + 2048],
                    float(DK) ** -0.5, A2[:, GLF:GLF + 2048],
                    ALU.mult, ALU.mult))
                t2 = self.mk("DVE", v.tensor_tensor(
                    A2[:, KT:KT + 2048], A2[:, KT:KT + 2048],
                    A2[:, EM:EM + 2048], ALU.mult))
                t3 = self.mk("DVE", v.tensor_tensor(
                    A2[:, KTM:KTM + 2048], A2[:, KTM:KTM + 2048],
                    A2[:, B32 + 2048:B32 + 4096], ALU.mult))
                self.w(v, "DVE", t1, t2, t3)
                T[f"qeke{lb}"] = t3
            self.V(f_qe)

            def emit_front(p, lb=lb):
                cc0, cc1 = 2 * p, 2 * p + 1
                def f_qep(v, p=p, lb=lb, cc0=cc0, cc1=cc1):
                    self.w(v, "DVE", T[f"qeke{lb}"], T["Sinit"],
                           T.get(f"spill{lb - 1}"),
                           T.get(f"qpc{lb}_{p - 1}") if p else
                           T.get(f"qpc{lb - 1}_3"))
                    eblr = self.eblh[:, 0:32].rearrange(
                        "p (h c) -> p h c", c=8)
                    t0 = self.mk("DVE", v.tensor_copy(
                        self.pdsnap[:, 0:4], self.pdh[:, 0:4]))
                    self.w(v, "DVE", t0)
                    t1 = self.mk("DVE", v.tensor_tensor(
                        self.pdh[:, 0:4], self.pdh[:, 0:4],
                        eblr[:, :, cc0], ALU.mult))
                    self.w(v, "DVE", t1)
                    t2 = self.mk("DVE", v.tensor_copy(
                        self.pdsnap[:, 4:8], self.pdh[:, 0:4]))
                    self.w(v, "DVE", t2)
                    t3 = self.mk("DVE", v.tensor_tensor(
                        self.pdh[:, 0:4], self.pdh[:, 0:4],
                        eblr[:, :, cc1], ALU.mult))
                    T[f"qeps{lb}_{p}"] = t2
                    T[f"qep{lb}_{p}"] = t3
                self.V(f_qep)
                def f_qpc(sc, p=p, lb=lb, cc0=cc0, cc1=cc1):
                    self.w(sc, "ACT", T[f"qeps{lb}_{p}"], T[f"qeke{lb}"],
                           T.get(f"spill{lb - 1}"))
                    tk = None
                    for h in range(4):
                        for k, cc in enumerate((cc0, cc1)):
                            tk = self.mk("ACT", sc.mul(
                                A2[:, QEB + h * 512 + cc * 64:
                                   QEB + h * 512 + cc * 64 + 64],
                                A2[:, QT + h * 512 + cc * 64:
                                   QT + h * 512 + cc * 64 + 64],
                                self.pdsnap[:, k * 4 + h:k * 4 + h + 1]))
                    T[f"qpc{lb}_{p}"] = tk
                self.SC(f_qpc)
                def f_at2(te, p=p, lb=lb):
                    self.w(te, "TE", T[f"qeke{lb}"], self.pslast["A"])
                    mm = None
                    for h in range(4):
                        mm = te.matmul(
                            self.PS_A[:, h * 128:(h + 1) * 128],
                            A2[:, KT + h * 512 + p * 128:
                               KT + h * 512 + p * 128 + 128],
                            A2[:, QT + h * 512 + p * 128:
                               QT + h * 512 + p * 128 + 128],
                            start=True, stop=True)
                    T[f"at2{lb}_{p}"] = self.mk("TE", mm)
                self.TE(f_at2)
                def f_msk(v, p=p, lb=lb):
                    self.w(v, "DVE", T[f"at2{lb}_{p}"])
                    tk = None
                    for h in range(4):
                        tk = self.mk("DVE", v.tensor_tensor(
                            self.xcs[:, h * 128:(h + 1) * 128],
                            self.PS_A[:, h * 128:(h + 1) * 128],
                            self.c_mask[:], ALU.mult))
                    T[f"at2m{lb}_{p}"] = tk
                    self.pslast["A"] = tk
                self.V(f_msk)

            def emit_mid(p, lb=lb):
                cc0, cc1 = 2 * p, 2 * p + 1
                def f_me(te, p=p, lb=lb):
                    self.w(te, "TE", T[f"qeke{lb}"], self.pslast["B"])
                    mm = None
                    for h in range(4):
                        mm = te.matmul(
                            self.PS_B[:, h * 256:(h + 1) * 256],
                            A2[0:64, KTM + p * 512 + h * 128:
                               KTM + p * 512 + h * 128 + 128],
                            A2[0:64, VTM + p * 1024 + h * 256:
                               VTM + p * 1024 + (h + 1) * 256],
                            start=True, stop=True)
                    T[f"me{lb}_{p}"] = self.mk("TE", mm)
                self.TE(f_me)
                def f_og1(te, p=p, lb=lb):
                    self.w(te, "TE", T[f"at2m{lb}_{p}"], self.pslast["C"],
                           self.pslast["A"],
                           T.get(f"soc{lb}_{p - 1}") if p else
                           T.get(f"soc{lb - 1}_3"))
                    mm = None
                    for h in range(4):
                        for eh in range(2):
                            i = h * 2 + eh
                            col = i * 128
                            te.matmul(self.PS_C[:, col:col + 128],
                                      A2[:, VTM + p * 1024 + h * 256 + eh * 128:
                                         VTM + p * 1024 + h * 256 + eh * 128 + 128],
                                      self.xcs[:, h * 128:(h + 1) * 128],
                                      start=(i % 4 == 0), stop=False)
                            mm = te.matmul(
                                self.PS_C[:, col:col + 64],
                                self.S_bf[:, h * 256 + eh * 128:
                                          h * 256 + eh * 128 + 128],
                                A2[:, QT + h * 512 + p * 128:
                                   QT + h * 512 + p * 128 + 64],
                                start=False, stop=False)
                    T[f"og1{lb}_{p}"] = self.mk("TE", mm)
                self.TE(f_og1)
                def f_se(v, p=p, lb=lb, cc0=cc0):
                    self.w(v, "DVE", T[f"me{lb}_{p}"], T[f"og1{lb}_{p}"],
                           T.get(f"soc{lb}_{p - 1}") if p else
                           T.get(f"soc{lb - 1}_3"))
                    if self.dbgM is not None and lb == 3 and p == 2:
                        self.mk("DVE", v.tensor_copy(self.dbgM[:, 0:1024],
                                                     self.PS_B[:, 0:1024]))
                        tdbg = self.mk("DVE", v.tensor_copy(
                            self.dbgM2[:, 0:1024], self.S_f32[:]))
                        self.w(v, "DVE", tdbg)
                        T["dbgM_tok"] = tdbg
                    t1 = self.mk("DVE", v.tensor_tensor(
                        self.sq_s[:, 0:1024], self.S_f32[:, 0:1024],
                        self.PS_B[:, 0:1024], ALU.add))
                    self.w(v, "DVE", t1)
                    t2 = None
                    for h in range(4):
                        t2 = self.mk("DVE", v.tensor_scalar(
                            self.S_f32[:, h * 256:(h + 1) * 256],
                            self.sq_s[:, h * 256:(h + 1) * 256],
                            self.eblh[:, h * 8 + cc0:h * 8 + cc0 + 1],
                            None, ALU.mult))
                        T[f"seu{lb}_{p}_{h}"] = t2
                    T[f"se{lb}_{p}"] = t2
                    self.pslast["B"] = t2
                self.V(f_se)
                def f_sec(sc, p=p, lb=lb):
                    self.w(sc, "ACT", T[f"og1{lb}_{p}"])
                    tk = None
                    for h in range(4):
                        self.w(sc, "ACT", T[f"seu{lb}_{p}_{h}"])
                        tk = self.mk("ACT", sc.copy(
                            self.S_bf[:, h * 256:(h + 1) * 256],
                            self.S_f32[:, h * 256:(h + 1) * 256]))
                    T[f"sec{lb}_{p}"] = tk
                self.SC(f_sec)
                def f_mo(te, p=p, lb=lb):
                    self.w(te, "TE", T[f"se{lb}_{p}"])
                    mm = None
                    for h in range(4):
                        mm = te.matmul(
                            self.PS_B[:, h * 256:(h + 1) * 256],
                            A2[64:128, KTM + p * 512 + h * 128:
                               KTM + p * 512 + h * 128 + 128],
                            A2[64:128, VTM + p * 1024 + h * 256:
                               VTM + p * 1024 + (h + 1) * 256],
                            start=True, stop=True)
                    T[f"mo{lb}_{p}"] = self.mk("TE", mm)
                self.TE(f_mo)

            def emit_tail(p, lb=lb):
                cc0, cc1 = 2 * p, 2 * p + 1
                def f_og2(te, p=p, lb=lb):
                    self.w(te, "TE", T[f"se{lb}_{p}"], T[f"sec{lb}_{p}"])
                    mm = None
                    for h in range(4):
                        for eh in range(2):
                            i = h * 2 + eh
                            col = i * 128
                            mm = te.matmul(
                                self.PS_C[:, col + 64:col + 128],
                                self.S_bf[:, h * 256 + eh * 128:
                                          h * 256 + eh * 128 + 128],
                                A2[:, QT + h * 512 + p * 128 + 64:
                                   QT + h * 512 + p * 128 + 128],
                                start=False, stop=(i % 4 == 3))
                    T[f"og2{lb}_{p}"] = self.mk("TE", mm)
                self.TE(f_og2)
                def f_so(v, p=p, lb=lb, cc1=cc1):
                    self.w(v, "DVE", T[f"mo{lb}_{p}"], T[f"og2{lb}_{p}"],
                           T[f"sec{lb}_{p}"])
                    t1 = self.mk("DVE", v.tensor_tensor(
                        self.sq_s[:, 0:1024], self.S_f32[:, 0:1024],
                        self.PS_B[:, 0:1024], ALU.add))
                    self.w(v, "DVE", t1)
                    t2 = None
                    for h in range(4):
                        t2 = self.mk("DVE", v.tensor_scalar(
                            self.S_f32[:, h * 256:(h + 1) * 256],
                            self.sq_s[:, h * 256:(h + 1) * 256],
                            self.eblh[:, h * 8 + cc1:h * 8 + cc1 + 1],
                            None, ALU.mult))
                        T[f"sou{lb}_{p}_{h}"] = t2
                    T[f"so{lb}_{p}"] = t2
                    self.pslast["B"] = t2
                self.V(f_so)
                def f_soc(sc, p=p, lb=lb):
                    self.w(sc, "ACT", T[f"og2{lb}_{p}"])
                    tk = None
                    for h in range(4):
                        self.w(sc, "ACT", T[f"sou{lb}_{p}_{h}"])
                        tk = self.mk("ACT", sc.copy(
                            self.S_bf[:, h * 256:(h + 1) * 256],
                            self.S_f32[:, h * 256:(h + 1) * 256]))
                    T[f"soc{lb}_{p}"] = tk
                self.SC(f_soc)
                def f_oe(v, p=p, lb=lb):
                    self.w(v, "DVE", T[f"og2{lb}_{p}"])
                    tk = self.mk("DVE", v.tensor_copy(
                        A2[:, OB:OB + 4096].rearrange(
                            "q (i w) -> q i w", w=512)[:, :, p * 128:
                                                       (p + 1) * 128],
                        self.PS_C[:, 0:1024].rearrange(
                            "q (i w) -> q i w", w=128)))
                    T[f"oe{lb}_{p}"] = tk
                    self.pslast["C"] = tk
                self.V(f_oe)

            emit_front(0)
            for p in range(4):
                emit_mid(p)
                if p < 3:
                    emit_front(p + 1)
                emit_tail(p)
            def f_sp(g, lb=lb):
                self.w(g, "POOL", T[f"oe{lb}_3"], T[f"qpc{lb}_3"])
                t1 = self.mkd("DPO", g.dma_start(
                    self.d_oS[lb, :, :], A2[:, OB:OB + 4096]))
                t2 = self.mkd("DPO", g.dma_start(
                    self.d_QE[lb, :, :], A2[:, QEB:QEB + 2048]))
                self.w(g, "POOL", t1, t2)
                T[f"spill{lb}"] = self.mk("POOL", g.engine_nop())
            self.PO(f_sp)
            if self.dbgM is not None and lb == 3:
                def f_dbg(g):
                    self.w(g, "POOL", T["dbgM_tok"])
                    tk = self.mkd("DPO", g.dma_start(self.d_M[:, 0:1024],
                                                     self.dbgM[:, 0:1024]))
                    tk = self.mkd("DPO", g.dma_start(self.d_M[:, 1024:2048],
                                                     self.dbgM2[:, 0:1024]))
                    tk = self.mkd("DPO", g.dma_start(
                        self.d_M2[:, 0:512], A2[:, 8192 + 1024:8192 + 1536]))
                    tk = self.mkd("DPO", g.dma_start(
                        self.d_M2[:, 512:1536], A2[:, 10240 + 2048:10240 + 3072]))
                    tk = self.mkd("DPO", g.dma_start(
                        self.d_M2[:, 1536:2048], A2[:, 4096 + 1024:4096 + 1536]))
                    self.w(g, "POOL", tk)
                self.PO(f_dbg)
            T[f"c1x{lb}"] = T[f"oe{lb}_3"]

        def f_send(g):
            pid = g.partition_id()
            slot = pid % 2
            self.w(g, "POOL", T["so3_3"])
            t1 = self.mkd("DPO", g.dma_start(
                self.sh_S[bass.ts(slot, 1), :, :].rearrange("a p c -> p (a c)"),
                self.S_f32[:]))
            self.w(g, "POOL", t1)
            self.mkd("DPO", g.dma_start(
                self.sh_flagS[bass.ts(slot, 1), :], self.flag_src[:]))
        self.PO(f_send)



        XB2, H2, SG2, X2B, NB2 = 0, 4096, 8192, 12288, 16384

        def f_w2(g):
            self.w(g, "POOL", T["c1x3"], T["so3_3"])
            self.dma_gw("DPO", g, A1[:, 0:8192], t["Wg"][:, :], NDG)
            T["wC2"] = self.dma_gw("DPO", g, A1[:, 8192:16384],
                                   t["Wo"][:, :], NDG)
        self.PO(f_w2)

        def f_recv(g):
            pid = g.partition_id()
            peer = (pid + 1) % 2
            with g.register("cnt_s") as r_cnt:
                g.reg_mov(r_cnt, 0)

                def poll():
                    g.dma_start(self.flag_rd[:],
                                self.sh_flagS[bass.ts(peer, 1), :]).then_inc(
                        self.psem2, 16)
                    g.reg_add(r_cnt, r_cnt, 16)
                    g.wait_ge(self.psem2, r_cnt)

                poll()
                with g.While(lambda: g.value_load(self.flag_rd[0:1, 0:1])
                             != g.value_load(self.flag_src[0:1, 0:1])):
                    poll()
            t1 = self.mkd("DPO", g.dma_start(
                self.Speer[:],
                self.sh_S[bass.ts(peer, 1), :, :].rearrange("a p c -> p (a c)")))
            self.w(g, "POOL", t1)
            T["Sprecv"] = self.mk("POOL", g.engine_nop())
        self.PO(f_recv)

        def f_sp2(v):
            self.w(v, "DVE", T["Sprecv"])
            T["Speer"] = self.mk("DVE", v.tensor_scalar(
                self.Speer[:], self.Speer[:], self.c_pmask[:, 0:1],
                None, ALU.mult))
        self.V(f_sp2)

        for lb in range(4):
            base = lb * 512

            def f_rl(g, lb=lb):
                self.w(g, "POOL", T[f"spill{lb}"],
                       T.get(f"c2x{lb - 1}"), T["so3_3"])
                t1 = self.mkd("DPO", g.dma_start(
                    A2[:, OB:OB + 4096], self.d_oS[lb, :, :]))
                t2 = self.mkd("DPO", g.dma_start(
                    A2[:, QEB:QEB + 2048], self.d_QE[lb, :, :]))
                self.w(g, "POOL", t1, t2)
                T[f"rl{lb}"] = self.mk("POOL", g.engine_nop())
            self.PO(f_rl)

            xb_sl = [A2[:, XB2 + dt * 512:XB2 + (dt + 1) * 512]
                     for dt in range(NDG)]
            h_sl = [A2[:, H2 + dt * 512:H2 + (dt + 1) * 512] for dt in range(NDG)]
            dep2 = T["wC2"] if lb == 0 else T[f"c2x{lb - 1}"]
            self.ln_fused(f"rm2C{lb}", base, 512, h_sl, dep2, self.c_anw,
                          hid_sl=xb_sl)

            for ot in range(NDG):
                def f_t(te, ot=ot, lb=lb):
                    self.w(te, "TE", T["wC2"], T.get(f"sg{lb}_{ot - 2}"))
                    if ot < 2:
                        self.w(te, "TE", self.pslast["B"])
                    mm = None
                    for dt in range(NDG):
                        self.w(te, "TE", T[f"rm2C{lb}_ap{dt}"])
                        mm = te.matmul(
                            self.PS_B[:, (ot % 2) * 512:(ot % 2) * 512 + 512],
                            A1[:, dt * 1024 + ot * 128:dt * 1024 + ot * 128 + 128],
                            A2[:, H2 + dt * 512:H2 + (dt + 1) * 512],
                            start=(dt == 0), stop=(dt == NDG - 1))
                    T[f"sgm{lb}_{ot}"] = self.mk("TE", mm)
                self.TE(f_t)

                def f_e(sc, ot=ot, lb=lb):
                    if ot == 0:
                        self.act_table(sc, 18)
                    self.w(sc, "ACT", T[f"sgm{lb}_{ot}"])
                    T[f"sg{lb}_{ot}"] = self.mk("ACT", sc.activation(
                        A2[:, SG2 + ot * 512:SG2 + (ot + 1) * 512],
                        self.PS_B[:, (ot % 2) * 512:(ot % 2) * 512 + 512],
                        AF.Silu))
                self.SC(f_e)
            self.pslast["B"] = T[f"sg{lb}_{NDG - 1}"]
            def cbank(i):
                if i < 2:
                    return self.PS_A[:, i * 512:(i + 1) * 512]
                if i < 4:
                    return self.PS_B[:, (i - 2) * 512:(i - 1) * 512]
                return self.PS_C[:, (i - 4) * 512:(i - 3) * 512]

            def f_corr(te, lb=lb):
                self.w(te, "TE", T["Speer"], T[f"rl{lb}"], self.pslast["A"],
                       self.pslast["B"], self.pslast["C"])
                mm = None
                for h in range(4):
                    for eh in range(2):
                        i = h * 2 + eh
                        mm = te.matmul(
                            cbank(i),
                            self.Speer[:, h * 256 + eh * 128:
                                       h * 256 + eh * 128 + 128],
                            A2[:, QEB + h * 512:QEB + (h + 1) * 512],
                            start=True, stop=True)
                T[f"corrm{lb}"] = self.mk("TE", mm)
            self.TE(f_corr)

            def f_corra(v, lb=lb):
                self.w(v, "DVE", T[f"corrm{lb}"])
                tk = None
                for i in range(8):
                    tk = self.mk("DVE", v.tensor_tensor(
                        A2[:, OB + i * 512:OB + (i + 1) * 512],
                        A2[:, OB + i * 512:OB + (i + 1) * 512],
                        cbank(i), ALU.add))
                T[f"corr{lb}"] = tk
            self.V(f_corra)
            ocorr = T[f"corr{lb}"]
            self.pslast["A"] = ocorr
            self.pslast["B"] = ocorr
            self.pslast["C"] = ocorr


            def f_sq(sc, lb=lb):
                self.act_table(sc, 6)
                self.w(sc, "ACT", ocorr, self.scr_free)
                T[f"gsqA{lb}"] = self.mk("ACT", sc.activation(
                    self.xcs[:, 0:2048], A2[:, OB:OB + 2048], AF.Square))
                T[f"gsqB{lb}"] = self.mk("ACT", sc.activation(
                    self.bc_b[:, 0:2048], A2[:, OB + 2048:OB + 4096],
                    AF.Square))
            self.SC(f_sq)

            def f_st(te, lb=lb):
                self.w(te, "TE", T[f"gsqA{lb}"], T[f"gsqB{lb}"], T["ones"],
                       self.pslast["C"])
                mm = None
                for h in range(4):
                    src = self.xcs if h < 2 else self.bc_b
                    for et in range(2):
                        off = (h % 2) * 1024 + et * 512
                        mm = te.matmul(self.PS_C[0:1, h * 512:h * 512 + 512],
                                       self.onesc[:], src[:, off:off + 512],
                                       start=(et == 0), stop=(et == 1))
                T[f"gst{lb}"] = self.mk("TE", mm)
            self.TE(f_st)

            def f_r(sc, lb=lb):
                self.w(sc, "ACT", T[f"gst{lb}"])
                t1 = self.mk("ACT", sc.activation(
                    self.sq_s[0:1, 0:2048], self.PS_C[0:1, 0:2048],
                    AF.Ln, bias=RMS_EPS, scale=1.0 / DV))
                self.w(sc, "ACT", t1)
                T[f"grr{lb}"] = self.mk("ACT", sc.activation(
                    self.bc_a[0:1, 0:2048], self.sq_s[0:1, 0:2048],
                    AF.Exp, scale=-0.5))
            self.SC(f_r)

            def f_bc(te, lb=lb):
                self.w(te, "TE", T[f"grr{lb}"])
                mm = None
                for h in range(4):
                    mm = te.matmul(self.PS_C[:, h * 512:(h + 1) * 512],
                                   self.onesr[:],
                                   self.bc_a[0:1, h * 512:(h + 1) * 512],
                                   start=True, stop=True)
                T[f"gbc{lb}"] = self.mk("TE", mm)
            self.TE(f_bc)

            def f_bce(sc, lb=lb):
                self.w(sc, "ACT", T[f"gbc{lb}"])
                T[f"gbce{lb}"] = self.mk("ACT", sc.copy(
                    self.bc_a[:, 0:2048], self.PS_C[:, 0:2048]))
            self.SC(f_bce)

            def f_app(v, lb=lb):
                self.w(v, "DVE", T[f"gbce{lb}"],
                       *[T[f"sg{lb}_{o}"] for o in range(NDG)])
                tks = {}
                for h in range(4):
                    for eh in range(2):
                        i = h * 2 + eh
                        self.w(v, "DVE", tks.get(h))
                        t1 = self.mk("DVE", v.scalar_tensor_tensor(
                            self.xcs[:, h * 512:(h + 1) * 512],
                            A2[:, OB + i * 512:OB + (i + 1) * 512],
                            self.c_gnw[:, eh:eh + 1],
                            self.bc_a[:, h * 512:(h + 1) * 512],
                            ALU.mult, ALU.mult))
                        self.w(v, "DVE", t1)
                        tks[h] = self.mk("DVE", v.tensor_tensor(
                            A2[:, OB + i * 512:OB + (i + 1) * 512],
                            self.xcs[:, h * 512:(h + 1) * 512],
                            A2[:, SG2 + i * 512:SG2 + (i + 1) * 512],
                            ALU.mult))
                T[f"gat{lb}"] = tks[3]
            self.V(f_app)
            ogat = T[f"gat{lb}"]
            self.pslast["C"] = ogat
            self.scr_free = ogat

            for ot in range(NDG):
                def f_t(te, ot=ot, lb=lb):
                    self.w(te, "TE", T["wC2"], ogat, T.get(f"x2e{lb}_{ot - 2}"))
                    if ot < 2:
                        self.w(te, "TE", self.pslast["B"])
                    mm = None
                    for dvt in range(NDG):
                        mm = te.matmul(
                            self.PS_B[:, (ot % 2) * 512:(ot % 2) * 512 + 512],
                            A1[:, 8192 + dvt * 1024 + ot * 128:
                               8192 + dvt * 1024 + ot * 128 + 128],
                            A2[:, OB + dvt * 512:OB + (dvt + 1) * 512],
                            start=(dvt == 0), stop=(dvt == NDG - 1))
                    T[f"aom{lb}_{ot}"] = self.mk("TE", mm)
                self.TE(f_t)

                def f_x2(v, ot=ot, lb=lb, base=base):
                    self.w(v, "DVE", T[f"aom{lb}_{ot}"],
                           T.get(f"x2e{lb}_{ot - 1}"),
                           T.get(f"x2c{lb}_{ot - 1}"))
                    t1 = self.mk("DVE", v.tensor_tensor(
                        self.sq_s[:, 0:512],
                        self.PS_B[:, (ot % 2) * 512:(ot % 2) * 512 + 512],
                        A2[:, XB2 + ot * 512:XB2 + (ot + 1) * 512], ALU.add))
                    T[f"x2d{lb}_{ot}"] = t1
                    self.w(v, "DVE", t1)
                    T[f"x2e{lb}_{ot}"] = self.mk("DVE", v.tensor_tensor(
                        self.x_res[:, ot * TOK + base:ot * TOK + base + 512],
                        self.x_res[:, ot * TOK + base:ot * TOK + base + 512],
                        self.sq_s[:, 0:512], ALU.add))
                self.V(f_x2)

                def f_x2c(sc, ot=ot, lb=lb):
                    self.w(sc, "ACT", T[f"x2d{lb}_{ot}"])
                    T[f"x2c{lb}_{ot}"] = self.mk("ACT", sc.copy(
                        A2[:, X2B + ot * 512:X2B + (ot + 1) * 512],
                        self.sq_s[:, 0:512]))
                self.SC(f_x2c)

            self.pslast["B"] = T[f"x2e{lb}_{NDG - 1}"]
            x2_sl = [A2[:, X2B + dt * 512:X2B + (dt + 1) * 512]
                     for dt in range(NDG)]
            nb_sl = [A2[:, NB2 + dt * 512:NB2 + (dt + 1) * 512]
                     for dt in range(NDG)]
            self.ln_feat(f"nb{lb}", base, 512, nb_sl, T[f"x2c{lb}_{NDG - 1}"],
                         rms=True, wcol=self.c_mnw, eps=RMS_EPS, src_sl=x2_sl)

            def f_spn(g, lb=lb):
                self.w(g, "POOL", T[f"nb{lb}_done"])
                t1 = self.mkd("DPO", g.dma_start(
                    self.d_nb[lb, :, :], A2[:, NB2:NB2 + 4096]))
                self.w(g, "POOL", t1)
                T[f"nbsp{lb}"] = self.mk("POOL", g.engine_nop())
            self.PO(f_spn)
            T[f"c2x{lb}"] = T[f"nb{lb}_done"]

        NBD, AD, DWD = 0, 16384, 20480
        bcbf16 = self.bc_b.bitcast(F16)
        groups = [(0, 4), (4, 4), (8, 3)]

        prevD = {"sil": None, "am": None, "abuf": [None, None],
                 "wbuf": [T[f"sgm3_{NDG - 1}"], T[f"aom3_{NDG - 1}"]],
                 "dwbuf": [T[f"aom3_{NDG - 1}"], T["corrm3"]],
                 "bank": [self.pslast["B"], self.pslast["B"]],
                 "da": {}}
        first_gm = [True]

        def emit_gw(ih, gi):
            g0, gn = groups[gi]
            gidx = ih * 3 + gi
            buf = (gidx % 2) * 8192
            dwb = DWD + (gidx % 2) * 4096

            def f_gw(g, ih=ih, g0=g0, gn=gn, gidx=gidx, buf=buf, dwb=dwb):
                self.w(g, "POOL", prevD["wbuf"][gidx % 2],
                       prevD["dwbuf"][gidx % 2])
                self.dma_gw("DPO", g, A1[:, buf:buf + gn * 1024],
                            t["Wgate"][:, ih * 1408 + g0 * 128:
                                       ih * 1408 + (g0 + gn) * 128], NDG)
                self.dma_gw("DPO", g,
                            A1[:, buf + 4096:buf + 4096 + gn * 1024],
                            t["Wup"][:, ih * 1408 + g0 * 128:
                                     ih * 1408 + (g0 + gn) * 128], NDG)
                T[f"gw{gidx}"] = self.dma_gw(
                    "DPO", g, A2[:, dwb:dwb + gn * 1024],
                    t["Wdown"][ih * 1408 + g0 * 128:
                               ih * 1408 + (g0 + gn) * 128, :], gn)
            self.PO(f_gw)

        emit_gw(0, 0)

        def f_nbl(g):
            self.w(g, "POOL", T["c2x3"],
                   *[T[f"nbsp{lb}"] for lb in range(4)])
            tk = None
            for lb in range(4):
                tk = self.mkd("DPO", g.dma_start(
                    A2[:, NBD + lb * 4096:NBD + (lb + 1) * 4096],
                    self.d_nb[lb, :, :]))
            self.w(g, "POOL", tk)
            T["nbls"] = self.mk("POOL", g.engine_nop())
        self.PO(f_nbl)
        for ih in range(2):
            for gi, (g0, gn) in enumerate(groups):
                gidx = ih * 3 + gi
                buf = (gidx % 2) * 8192
                dwb = DWD + (gidx % 2) * 4096
                if gidx > 0:
                    emit_gw(ih, gi)

                for lb in range(4):
                    base = lb * 512
                    abuf = (lb % 2) * 2048
                    for ii in range(gn):
                        def f_gm(te, gidx=gidx, lb=lb, ii=ii, gn=gn, buf=buf):
                            self.w(te, "TE", T[f"gw{gidx}"], T["nbls"],
                                   prevD["sil"])
                            if first_gm[0]:
                                self.w(te, "TE", self.pslast["A"])
                                first_gm[0] = False
                            mm = None
                            for dt in range(NDG):
                                mm = te.matmul(
                                    self.PS_A[:, 0:512],
                                    A1[:, buf + (dt * gn + ii) * 128:
                                       buf + (dt * gn + ii) * 128 + 128],
                                    A2[:, NBD + lb * 4096 + dt * 512:
                                       NBD + lb * 4096 + (dt + 1) * 512],
                                    start=(dt == 0), stop=(dt == NDG - 1))
                            T[f"gm{gidx}_{lb}_{ii}"] = self.mk("TE", mm)
                        self.TE(f_gm)

                        def f_sil(sc, gidx=gidx, lb=lb, ii=ii):
                            if gidx == 0 and lb == 0 and ii == 0:
                                self.act_table(sc, 18)
                            self.w(sc, "ACT", T[f"gm{gidx}_{lb}_{ii}"],
                                   prevD["am"])
                            T[f"sil{gidx}_{lb}_{ii}"] = self.mk(
                                "ACT", sc.activation(
                                    bcbf16[:, 1024:1536], self.PS_A[:, 0:512],
                                    AF.Silu))
                            prevD["sil"] = T[f"sil{gidx}_{lb}_{ii}"]
                        self.SC(f_sil)

                        def f_um(te, gidx=gidx, lb=lb, ii=ii, gn=gn, buf=buf):
                            self.w(te, "TE", T[f"sil{gidx}_{lb}_{ii}"],
                                   prevD["am"] or self.pslast["A"])
                            mm = None
                            for dt in range(NDG):
                                mm = te.matmul(
                                    self.PS_A[:, 512:1024],
                                    A1[:, buf + 4096 + (dt * gn + ii) * 128:
                                       buf + 4096 + (dt * gn + ii) * 128 + 128],
                                    A2[:, NBD + lb * 4096 + dt * 512:
                                       NBD + lb * 4096 + (dt + 1) * 512],
                                    start=(dt == 0), stop=(dt == NDG - 1))
                            T[f"um{gidx}_{lb}_{ii}"] = self.mk("TE", mm)
                        self.TE(f_um)

                        def f_am(v, gidx=gidx, lb=lb, ii=ii, abuf=abuf):
                            self.w(v, "DVE", T[f"um{gidx}_{lb}_{ii}"],
                                   T[f"sil{gidx}_{lb}_{ii}"],
                                   prevD["abuf"][(lb % 2)])
                            T[f"am{gidx}_{lb}_{ii}"] = self.mk(
                                "DVE", v.tensor_tensor(
                                    A2[:, AD + abuf + ii * 512:
                                       AD + abuf + (ii + 1) * 512],
                                    bcbf16[:, 1024:1536],
                                    self.PS_A[:, 512:1024], ALU.mult))
                            prevD["am"] = T[f"am{gidx}_{lb}_{ii}"]
                        self.V(f_am)

                    for dt in range(NDG):
                        def f_dm(te, gidx=gidx, lb=lb, dt=dt, gn=gn,
                                 dwb=dwb, abuf=abuf):
                            self.w(te, "TE", T[f"gw{gidx}"],
                                   T[f"am{gidx}_{lb}_{gn - 1}"],
                                   prevD["bank"][dt % 2])
                            mm = None
                            for ii in range(gn):
                                mm = te.matmul(
                                    self.PS_B[:, (dt % 2) * 512:
                                              (dt % 2) * 512 + 512],
                                    A2[:, dwb + ii * 1024 + dt * 128:
                                       dwb + ii * 1024 + dt * 128 + 128],
                                    A2[:, AD + abuf + ii * 512:
                                       AD + abuf + (ii + 1) * 512],
                                    start=(ii == 0), stop=(ii == gn - 1))
                            T[f"dmm{gidx}_{lb}_{dt}"] = self.mk("TE", mm)
                        self.TE(f_dm)

                        def f_da(v, gidx=gidx, lb=lb, dt=dt, base=base):
                            self.w(v, "DVE", T[f"dmm{gidx}_{lb}_{dt}"],
                                   prevD["da"].get((lb, dt)) or
                                   T.get(f"x2e{lb}_{dt}"))
                            tk = self.mk("DVE", v.tensor_tensor(
                                self.x_res[:, dt * TOK + base:
                                           dt * TOK + base + 512],
                                self.x_res[:, dt * TOK + base:
                                           dt * TOK + base + 512],
                                self.PS_B[:, (dt % 2) * 512:
                                          (dt % 2) * 512 + 512], ALU.add))
                            prevD["da"][(lb, dt)] = tk
                            prevD["bank"][dt % 2] = tk
                        self.V(f_da)
                    prevD["abuf"][(lb % 2)] = T[f"dmm{gidx}_{lb}_{NDG - 1}"]
                prevD["wbuf"][gidx % 2] = T[f"um{gidx}_3_{gn - 1}"]
                prevD["dwbuf"][gidx % 2] = T[f"dmm{gidx}_3_{NDG - 1}"]
        T["attnB_done"] = prevD["da"][(3, NDG - 1)]

    def e_store(self):
        T = self.T

        def f_sy(sy):
            dep = T.get("ffnE_done") or T.get("attnB_done") or \
                T.get("ffnB_done") or T.get("attnA_done")
            tk = None
            for lb in range(4):
                self.w(sy, "SP", T.get(f"ffnE_x{lb}") or dep)
                tk = self.mkd("DSY", sy.dma_start(
                    self.t_out[:, lb * 512:(lb + 1) * 512].rearrange(
                        "(g p) w -> p g w", p=128),
                    self.x_res[:, 0:NDG * TOK].rearrange(
                        "p (g w) -> p g w", g=NDG)[:, :,
                                                   lb * 512:(lb + 1) * 512]))
            if self.DBG:
                tk = self.mkd("DSY", sy.dma_start(self.d_ar1[:, :], self.AR1[:, :]))
                tk = self.mkd("DSY", sy.dma_start(self.d_ar2[:, :], self.AR2[:, :]))
                tk = self.mkd("DSY", sy.dma_start(
                    self.d_cols[:, 0:64], self.cols[:, 0:64]))
                tk = self.mkd("DSY", sy.dma_start(
                    self.d_cols[:, 64:128], self.colsC[:, 0:64]))
                tk = self.mkd("DSY", sy.dma_start(
                    self.d_cols[:, 128:192], self.colsD[:, 0:64]))
                tk = self.mkd("DSY", sy.dma_start(
                    self.d_cols[:, 192:200], self.colsT[:, 0:8]))
                tk = self.mkd("DSY", sy.dma_start(
                    self.d_cols[:, 200:208], self.tpeer[:, 0:8]))
                tk = self.mkd("DSY", sy.dma_start(self.d_nyq[:, :], self.nyqrow[:, :]))
                if hasattr(self, "S_f32"):
                    tk = self.mkd("DSY", sy.dma_start(self.d_S[:, :], self.S_f32[:]))
                    tk = self.mkd("DSY", sy.dma_start(
                        self.d_cols[:, 208:240], self.eblh[:, 0:32]))
            self.w(sy, "SP", tk)
        self.SY(f_sy)


_CACHED = {}


def build_program():
    if "nc" not in _CACHED:
        k1 = K()
        k1.build()
        wm = {qn: sorted(v) for qn, v in k1.rec_wm.items()}
        k2 = K()
        k2.dma_wm = wm
        _CACHED["nc"] = k2.build()
    return _CACHED["nc"]


def kernel(**inputs):
    assert inputs["L0_attn_a_D_im"].shape == (D,)
    nonce = int.from_bytes(np.random.bytes(3), "little") + 7
    nc = build_program()
    in_maps = [host_prepare(c, inputs, nonce) for c in range(8)]
    res = run_bass_kernel_spmd(nc, in_maps, core_ids=list(range(8)))
    out = np.empty((BSZ, SEQ, D), np.float32)
    for c in range(8):
        b, s = c // 2, c % 2
        out[b, s * TOK:(s + 1) * TOK, :] = np.asarray(res.results[c]["y_out"]).T
    return out
